# version 23
# baseline (speedup 1.0000x reference)
import numpy as np
import ml_dtypes
from contextlib import ExitStack

import concourse.bass as bass
import concourse.mybir as mybir
from concourse.bass_utils import run_bass_kernel_spmd

F32 = mybir.dt.float32
BF16 = mybir.dt.bfloat16
AF = mybir.ActivationFunctionType
ALU = mybir.AluOpType
AX = mybir.AxisListType.X

NCORES = 8
S = 2048
D = 1024
NCH = D // 128
NTT = S // 512
NTC = S // 128
NH = 16
HD = 64
MAGIC = 12582912.0


def cs(c, t0=0, n=S):
    return slice(c * S + t0, c * S + t0 + n)


class Sem:
    def __init__(self, handle, name):
        self.h = handle
        self.name = name
        self.n = 0


class Prog:
    ENGINES = ("tensor", "vector", "scalar", "gpsimd", "sync")
    STRICT = ("vector", "scalar", "gpsimd")

    def __init__(self, nc, es):
        self.nc = nc
        self.es = es
        self.q = {e: [] for e in self.ENGINES}
        self.esem = {}

    def sem(self, name):
        return Sem(self.es.enter_context(self.nc.semaphore(name)), name)

    def op(self, eng, fn, inc=None):
        incs = []
        if inc is not None:
            incs = inc if isinstance(inc, list) else [inc]
        if eng in self.STRICT:
            if not incs:
                es_ = self.esem.get(eng)
                if es_ is None:
                    es_ = self.esem[eng] = self.sem("self_" + eng)
                incs = [(es_, 1)]
            last = self.esem.get(("last", eng))
            if last is not None:
                self.q[eng].append(("wait", last[0], last[1]))
        for sem, d in incs:
            sem.n += d
        if eng in self.STRICT:
            self.esem[("last", eng)] = (incs[0][0], incs[0][0].n)
        self.q[eng].append(("op", fn, incs))

    def wait(self, eng, sem, val=None):
        v = sem.n if val is None else val
        if v > 0:
            self.q[eng].append(("wait", sem, v))

    def finalize(self, block):
        def make_body(items):
            def body(e):
                for kind, a, b in items:
                    if kind == "op":
                        ins = a(e)
                        for sem, d in b:
                            ins = ins.then_inc(sem.h, d)
                    else:
                        e.wait_ge(a.h, b)
            return body
        for eng in self.ENGINES:
            if self.q[eng]:
                getattr(block, eng)(make_body(self.q[eng]))


class Banks:
    def __init__(self, p, pb):
        self.p = p
        self.pb = pb
        self.tag = [None] * len(pb)

    def acquire(self, b):
        if self.tag[b] is not None:
            sem, cnt = self.tag[b]
            self.p.wait("tensor", sem, cnt)
            self.tag[b] = None
        return self.pb[b]

    def release(self, b, sem, cnt=None):
        self.tag[b] = (sem, sem.n if cnt is None else cnt)


def _bias_cols(b):
    b = np.asarray(b, np.float32)
    n = b.shape[0] // 128
    return np.ascontiguousarray(b.reshape(n, 128).T)


def host_prepare(inputs):
    g = {}
    proj = np.asarray(inputs["L0_attn_projection_1nck"], np.float32)[0]
    g["projT"] = np.ascontiguousarray(proj)
    halves = np.asarray(inputs["L0_attn_halves"], np.float32)
    diag = np.asarray(inputs["L0_attn_diagonals_nk"], np.float32)
    H, K = diag.shape
    iu, ju = np.triu_indices(K, k=1)
    metric = np.zeros((H, K, K), np.float32)
    metric[:, iu, ju] = halves
    metric[:, ju, iu] = halves
    dd = np.arange(K)
    metric[:, dd, dd] = diag
    g["metric"] = np.ascontiguousarray(metric)
    g["mixer_wT"] = np.ascontiguousarray(np.asarray(inputs["L0_attn_mixer_w"], np.float32).T)
    g["mixer_b"] = _bias_cols(inputs["L0_attn_mixer_b"])
    g["gate_wT"] = np.ascontiguousarray(np.asarray(inputs["L0_ffn_gate_w"], np.float32).T)
    g["gate_b"] = np.ascontiguousarray(np.asarray(inputs["L0_ffn_gate_b"], np.float32).reshape(1, 8))
    ew = np.asarray(inputs["L0_ffn_expert_w"], np.float32)
    g["expert_wT"] = np.ascontiguousarray(ew.transpose(0, 2, 1))
    g["expert_b"] = np.ascontiguousarray(np.asarray(inputs["L0_ffn_expert_b"], np.float32))
    for k in ("wq", "wqc", "wqr", "wkv", "wkr", "wkc", "wv", "wo"):
        g[k] = np.ascontiguousarray(np.asarray(inputs["L1_attn_" + k], np.float32))
    for k, n in (("bq", 2), ("bqc", 7), ("bqr", 1), ("bkv", 2), ("bkr", 1), ("bkc", 7),
                 ("bv", 8), ("bo", 8)):
        b = np.asarray(inputs["L1_attn_" + k], np.float32)
        g[k] = np.ascontiguousarray(b.reshape(n, 128).T)
    g["w1T"] = np.ascontiguousarray(np.asarray(inputs["L1_ffn_w1"], np.float32).T)
    g["w2T"] = np.ascontiguousarray(np.asarray(inputs["L1_ffn_w2"], np.float32).T)
    cw = np.asarray(inputs["L1_ffn_conv_w"], np.float32).reshape(4096, 3)
    g["convw"] = np.ascontiguousarray(cw.reshape(32, 128, 3).transpose(1, 0, 2).reshape(128, 96))
    g["convb"] = _bias_cols(inputs["L1_ffn_conv_b"])
    return g


def build_l0(debug=(), phases=("attn", "gates", "mixer", "moe")):
    nc = bass.Bass("TRN2", target_bir_lowering=False)
    xT = nc.dram_tensor("xT", [D, S], F32, kind="ExternalInput").ap()
    projT = nc.dram_tensor("projT", [NH, D, HD], F32, kind="ExternalInput").ap()
    metric = nc.dram_tensor("metric", [NH, HD, HD], F32, kind="ExternalInput").ap()
    mixer_wT = nc.dram_tensor("mixer_wT", [D, D], F32, kind="ExternalInput").ap()
    mixer_b = nc.dram_tensor("mixer_b", [128, NCH], F32, kind="ExternalInput").ap()
    gate_wT = nc.dram_tensor("gate_wT", [D, 8], F32, kind="ExternalInput").ap()
    gate_b = nc.dram_tensor("gate_b", [1, 8], F32, kind="ExternalInput").ap()
    expert_wT = nc.dram_tensor("expert_wT", [8, D, D], F32, kind="ExternalInput").ap()
    expert_b = nc.dram_tensor("expert_b", [8, D], F32, kind="ExternalInput").ap()
    x1T = nc.dram_tensor("x1T", [D, S], F32, kind="ExternalOutput").ap()
    dbg = {}
    for name in debug:
        shp = {"xn": [D, S], "pfm": [D, S], "gacc": [64, 1024], "mg": [128, 512],
               "gates": [8, S]}[name]
        dbg[name] = nc.dram_tensor("dbg_" + name, shp, F32, kind="ExternalOutput").ap()

    with ExitStack() as es:
        sb = lambda name, shape, dt=F32, **kw: es.enter_context(nc.sbuf_tensor(name, shape, dt, **kw))
        x = sb("x_res", [128, NCH * S], F32)
        xn = sb("xn", [128, NCH * S], BF16)
        pfm = sb("pfm", [128, NCH * S], BF16)
        wbuf = sb("wbuf", [128, 16384], BF16)
        ptokb = sb("ptokb", [128, 2 * 1024], BF16)
        gacc = sb("gacc", [64, 1024], F32)
        gbf = sb("gbf", [64, 1024], BF16)
        metric_sb = sb("metric_sb", [64, 1024], BF16)
        mg = sb("mg", [128, 512], BF16)
        ones_bf = sb("ones_bf", [128, 128], BF16)
        ones_f = sb("ones_f", [128, 128], F32)
        ident_bf = sb("ident_bf", [128, 128], BF16)
        x2s = sb("x2s", [128, 512], BF16)
        scr = sb("scr", [128, 512], F32)
        rowm = sb("rowm", [1, 512], F32)
        rowe = sb("rowe", [1, 512], F32)
        rowt = sb("rowt", [1, 512], F32)
        ab = sb("ab", [128, 1024], F32)
        mixb_sb = sb("mixb_sb", [128, NCH], F32)
        gatew_bf = sb("gatew_bf", [128, 64], BF16)
        gb_bf = sb("gb_bf", [1, 8], BF16)
        gtok = sb("gtok", [128, 8], F32)
        gtok2 = sb("gtok2", [128, 8], F32)
        gtok_bf = sb("gtok_bf", [128, 8], BF16)
        col1 = sb("col1", [128, 1], F32)
        col2 = sb("col2", [128, 1], F32)
        col3 = sb("col3", [128, 1], F32)
        g_sb = sb("g_sb", [8, S], BF16)
        g_row = sb("g_row", [1, S], BF16)
        gbc = sb("gbc", [128, 2 * 512], BF16)
        xne = sb("xne", [128, NCH * 512], BF16)
        eb_bf = sb("eb_bf", [8, D], BF16)
        pb = [es.enter_context(nc.psum_tensor(f"pb{i}", [128, 512], F32)) for i in range(8)]

        p = Prog(nc, es)
        sW = p.sem("sW")
        sX = p.sem("sX")
        sPE = p.sem("sPE")
        sDVE = p.sem("sDVE")
        sACT = p.sem("sACT")
        sPOOL = p.sem("sPOOL")
        sWE = [p.sem("sWE0"), p.sem("sWE1")]
        banks = Banks(p, pb)
        block = es.enter_context(nc.Block())

        p.op("gpsimd", lambda e: e.memset(ones_f[:], 1.0))
        p.op("gpsimd", lambda e: e.tensor_copy(out=ones_bf[:], in_=ones_f[:]))
        p.op("gpsimd", lambda e: e.tensor_copy(out=ident_bf[:], in_=ones_f[:]))
        p.op("gpsimd", lambda e: e.affine_select(ident_bf[:], ident_bf[:], pattern=[[-1, 128]],
                                                 compare_op=ALU.is_equal, fill=0.0, base=0,
                                                 channel_multiplier=1))
        p.op("gpsimd", lambda e: e.memset(gacc[:], 0.0), inc=(sPOOL, 1))

        p.op("sync", lambda e: e.dma_start(
            out=x[:].rearrange("p (c t) -> p c t", c=NCH),
            in_=xT.rearrange("(c p) t -> p c t", p=128)), inc=(sX, 16))
        p.op("sync", lambda e: e.dma_start(out=mixb_sb[:], in_=mixer_b), inc=(sX, 16))
        x_loaded = sX.n

        for c in range(NCH):
            p.op("gpsimd", lambda e, c=c: e.dma_start(
                out=wbuf[:, c * 1024:(c + 1) * 1024].rearrange("p (h k) -> p h k", h=NH),
                in_=projT[:, c * 128:(c + 1) * 128, :].rearrange("h p k -> p h k")), inc=(sW, 16))
        p.op("gpsimd", lambda e: e.dma_start(
            out=metric_sb[:].rearrange("m (h l) -> m h l", h=NH),
            in_=metric.rearrange("h m l -> m h l")), inc=(sW, 16))
        p.op("gpsimd", lambda e: e.dma_start(
            out=gatew_bf[:].rearrange("p (c e) -> p c e", c=NCH),
            in_=gate_wT.rearrange("(c p) e -> p c e", p=128)), inc=(sW, 16))
        p.op("gpsimd", lambda e: e.dma_start(out=gb_bf[:], in_=gate_b), inc=(sW, 16))
        p.op("gpsimd", lambda e: e.dma_start(out=eb_bf[:], in_=expert_b), inc=(sW, 16))
        p.op("gpsimd", lambda e: e.dma_start(
            out=wbuf[:, 8192:16384].rearrange("p (c o) -> p c o", c=NCH),
            in_=mixer_wT.rearrange("(c p) o -> p c o", p=128)), inc=(sW, 16))
        weights_loaded = sW.n

        def proj_tile(c, h):
            return wbuf[:, (c * NH + h) * HD:(c * NH + h + 1) * HD]

        def mix_tile(c, oc):
            return wbuf[:, 8192 + c * D + oc * 128: 8192 + c * D + oc * 128 + 128]

        p.wait("vector", sX, x_loaded)
        p.wait("scalar", sX, x_loaded)
        p.wait("tensor", sPOOL, 1)
        for tt in range(NTT):
            t0 = tt * 512
            for c in range(NCH):
                p.op("vector", lambda e, c=c, t0=t0: e.tensor_copy(
                    out=xn[:, cs(c, t0, 512)], in_=x[:, cs(c, t0, 512)]),
                    inc=(sDVE, 1) if c == NCH - 1 else None)
            p.wait("tensor", sDVE)
            b0 = banks.acquire(0)
            b1 = banks.acquire(1)
            for c in range(NCH):
                p.wait("scalar", sPE)
                p.op("scalar", lambda e, c=c, t0=t0: e.activation(
                    out=x2s[:], in_=x[:, cs(c, t0, 512)], func=AF.Square), inc=(sACT, 1))
                p.op("tensor", lambda e, c=c, t0=t0, b0=b0: e.matmul(
                    b0[0:1, :], ones_bf[:, 0:1], xn[:, cs(c, t0, 512)],
                    start=(c == 0), stop=(c == NCH - 1)))
                p.wait("tensor", sACT)
                p.op("tensor", lambda e, c=c, b1=b1: e.matmul(
                    b1[0:1, :], ones_bf[:, 0:1], x2s[:],
                    start=(c == 0), stop=(c == NCH - 1)), inc=(sPE, 1))
            p.wait("vector", sPE)
            p.op("vector", lambda e, b0=b0: e.tensor_scalar(out=rowm[:], in0=b0[0:1, :],
                                                            scalar1=1.0 / D, scalar2=None, op0=ALU.mult))
            p.op("vector", lambda e, b1=b1: e.tensor_scalar(out=rowe[:], in0=b1[0:1, :],
                                                            scalar1=1.0 / D, scalar2=None, op0=ALU.mult),
                 inc=(sDVE, 1))
            banks.release(0, sDVE)
            banks.release(1, sDVE)
            p.op("vector", lambda e: e.tensor_mul(out=rowt[:], in0=rowm[:], in1=rowm[:]))
            p.op("vector", lambda e: e.tensor_sub(out=rowe[:], in0=rowe[:], in1=rowt[:]))
            p.op("vector", lambda e: e.tensor_scalar(out=rowe[:], in0=rowe[:], scalar1=1e-5,
                                                     scalar2=None, op0=ALU.add), inc=(sDVE, 1))
            p.wait("scalar", sDVE)
            p.op("scalar", lambda e: e.activation(out=rowt[:], in_=rowe[:], func=AF.Sqrt),
                 inc=(sACT, 1))
            p.wait("vector", sACT)
            p.op("vector", lambda e: e.reciprocal(out=rowt[:], in_=rowt[:]))
            p.op("vector", lambda e: e.tensor_mul(out=rowm[:], in0=rowm[:], in1=rowt[:]))
            p.op("vector", lambda e: e.tensor_scalar(out=rowm[:], in0=rowm[:], scalar1=-1.0,
                                                     scalar2=None, op0=ALU.mult), inc=(sDVE, 1))
            p.wait("tensor", sDVE)
            b2 = banks.acquire(2)
            b3 = banks.acquire(3)
            p.op("tensor", lambda e, b2=b2: e.matmul(b2[:, :], ones_f[0:1, :], rowt[:],
                                                     start=True, stop=True))
            p.op("tensor", lambda e, b3=b3: e.matmul(b3[:, :], ones_f[0:1, :], rowm[:],
                                                     start=True, stop=True), inc=(sPE, 1))
            p.wait("scalar", sPE)
            p.op("scalar", lambda e, b2=b2: e.copy(out=ab[:, 0:512], in_=b2[:, :]))
            p.op("scalar", lambda e, b3=b3: e.copy(out=ab[:, 512:1024], in_=b3[:, :]), inc=(sACT, 1))
            banks.release(2, sACT)
            banks.release(3, sACT)
            p.wait("vector", sACT)
            for c in range(NCH):
                p.op("vector", lambda e, c=c, t0=t0: e.tensor_mul(
                    out=scr[:], in0=x[:, cs(c, t0, 512)], in1=ab[:, 0:512]))
                p.op("vector", lambda e, c=c, t0=t0: e.tensor_add(
                    out=xn[:, cs(c, t0, 512)], in0=scr[:], in1=ab[:, 512:1024]),
                    inc=(sDVE, 1) if c == NCH - 1 else None)
        xn_done = sDVE.n

        ptok_mm_done = None
        nudged_done = None
        if "attn" in phases:
            p.wait("tensor", sW, weights_loaded)
            p.wait("tensor", sDVE, xn_done)
            for j in range(NCH):
                for tt in range(NTT):
                    b = 4 + ((j * NTT + tt) % 2)
                    pbx = banks.acquire(b)
                    for half in range(2):
                        h = 2 * j + half
                        for c in range(NCH):
                            p.op("tensor", lambda e, pbx=pbx, half=half, h=h, c=c, tt=tt: e.matmul(
                                pbx[half * 64:(half + 1) * 64, :], proj_tile(c, h),
                                xn[:, cs(c, tt * 512, 512)],
                                start=(c == 0), stop=(c == NCH - 1)),
                                inc=(sPE, 1) if (half == 1 and c == NCH - 1) else None)
                    p.wait("scalar", sPE)
                    p.op("scalar", lambda e, pbx=pbx, j=j, tt=tt: e.copy(
                        out=pfm[:, cs(j, tt * 512, 512)], in_=pbx[:, :]), inc=(sACT, 1))
                    banks.release(b, sACT)

            g_mm_done = {}
            for tc in range(NTC):
                pt = tc % 2
                pb6 = banks.acquire(6)
                pb7 = banks.acquire(7)
                for c in range(NCH):
                    p.op("tensor", lambda e, pb6=pb6, c=c, tc=tc: e.matmul(
                        pb6[:, :], xn[:, cs(c, tc * 128, 128)], wbuf[:, c * 1024: c * 1024 + 512],
                        start=(c == 0), stop=(c == NCH - 1)))
                    p.op("tensor", lambda e, pb7=pb7, c=c, tc=tc: e.matmul(
                        pb7[:, :], xn[:, cs(c, tc * 128, 128)], wbuf[:, c * 1024 + 512: c * 1024 + 1024],
                        start=(c == 0), stop=(c == NCH - 1)),
                        inc=(sPE, 1) if c == NCH - 1 else None)
                p.wait("scalar", sPE)
                if tc >= 2:
                    p.wait("scalar", sPE, g_mm_done[tc - 2])
                p.op("scalar", lambda e, pb6=pb6, pt=pt: e.copy(
                    out=ptokb[:, pt * 1024: pt * 1024 + 512], in_=pb6[:, :]))
                p.op("scalar", lambda e, pb7=pb7, pt=pt: e.copy(
                    out=ptokb[:, pt * 1024 + 512: pt * 1024 + 1024], in_=pb7[:, :]), inc=(sACT, 1))
                banks.release(6, sACT)
                banks.release(7, sACT)
                p.wait("tensor", sACT)
                b0 = banks.acquire(0)
                b1 = banks.acquire(1)
                for h in range(NH):
                    pbx = b0 if h < 8 else b1
                    p.op("tensor", lambda e, pbx=pbx, h=h, pt=pt: e.matmul(
                        pbx[0:64, (h % 8) * 64:(h % 8) * 64 + 64],
                        ptokb[:, pt * 1024 + h * 64: pt * 1024 + h * 64 + 64],
                        ptokb[:, pt * 1024 + h * 64: pt * 1024 + h * 64 + 64],
                        start=True, stop=True), inc=(sPE, 1) if h == NH - 1 else None)
                g_mm_done[tc] = sPE.n
                p.wait("vector", sPE)
                p.wait("vector", sPOOL, 1)
                p.op("vector", lambda e, b0=b0: e.tensor_add(out=gacc[0:64, 0:512],
                                                             in0=gacc[0:64, 0:512], in1=b0[0:64, :]))
                p.op("vector", lambda e, b1=b1: e.tensor_add(out=gacc[0:64, 512:1024],
                                                             in0=gacc[0:64, 512:1024], in1=b1[0:64, :]),
                     inc=(sDVE, 1))
                banks.release(0, sDVE)
                banks.release(1, sDVE)
            ptok_mm_done = sPE.n
            p.op("vector", lambda e: e.tensor_copy(out=gbf[:], in_=gacc[:]), inc=(sDVE, 1))
            g_done = sDVE.n

            p.wait("tensor", sW, weights_loaded)
            p.wait("tensor", sDVE, g_done)
            b2 = banks.acquire(2)
            b3 = banks.acquire(3)
            for h in range(NH):
                pbx = b2 if h < 8 else b3
                p.op("tensor", lambda e, pbx=pbx, h=h: e.matmul(
                    pbx[0:64, (h % 8) * 64:(h % 8) * 64 + 64],
                    metric_sb[:, h * 64: h * 64 + 64], gbf[:, h * 64: h * 64 + 64],
                    start=True, stop=True), inc=(sPE, 1) if h == NH - 1 else None)
            p.wait("scalar", sPE)
            for h in range(NH):
                pbx = b2 if h < 8 else b3
                p.op("scalar", lambda e, pbx=pbx, h=h: e.copy(
                    out=mg[(h % 2) * 64:(h % 2) * 64 + 64, (h // 2) * 64:(h // 2) * 64 + 64],
                    in_=pbx[0:64, (h % 8) * 64:(h % 8) * 64 + 64]),
                    inc=(sACT, 1) if h == NH - 1 else None)
            banks.release(2, sACT)
            banks.release(3, sACT)
            mg_done = sACT.n

            p.wait("tensor", sACT, mg_done)
            for h in range(NH):
                base = (h % 2) * 64
                j = h // 2
                for tt in range(NTT):
                    b = 2 + ((h * NTT + tt) % 2)
                    pbx = banks.acquire(b)
                    p.op("tensor", lambda e, pbx=pbx, j=j, tt=tt, base=base: e.matmul(
                        pbx[0:64, :], mg[base:base + 64, j * 64: j * 64 + 64],
                        pfm[base:base + 64, cs(j, tt * 512, 512)],
                        start=True, stop=True), inc=(sPE, 1))
                    p.wait("scalar", sPE)
                    p.op("scalar", lambda e, pbx=pbx, j=j, tt=tt, base=base: e.copy(
                        out=pfm[base:base + 64, cs(j, tt * 512, 512)], in_=pbx[0:64, :]),
                        inc=(sACT, 1))
                    banks.release(b, sACT)
            nudged_done = sACT.n

        gates_done = None
        if "gates" in phases:
            p.wait("tensor", sW, weights_loaded)
            p.wait("tensor", sDVE, xn_done)
            for tc in range(NTC):
                b = 4 + (tc % 2)
                pbx = banks.acquire(b)
                for c in range(NCH):
                    p.op("tensor", lambda e, pbx=pbx, c=c, tc=tc: e.matmul(
                        pbx[:, 0:8], xn[:, cs(c, tc * 128, 128)], gatew_bf[:, c * 8: c * 8 + 8],
                        start=(c == 0), stop=False))
                p.op("tensor", lambda e, pbx=pbx: e.matmul(
                    pbx[:, 0:8], ones_bf[0:1, :], gb_bf[:], start=False, stop=True), inc=(sPE, 1))
                p.wait("vector", sPE)
                p.op("vector", lambda e, pbx=pbx: e.reduce_max(out=col1[:], in_=pbx[:, 0:8], axis=AX,
                                                               negate=True), inc=(sDVE, 1))
                p.wait("scalar", sDVE)
                p.op("scalar", lambda e, pbx=pbx: e.activation(out=gtok[:], in_=pbx[:, 0:8],
                                                               func=AF.Exp, bias=col1[:],
                                                               accum_out=col2[:]), inc=(sACT, 1))
                banks.release(b, sACT)
                p.wait("vector", sACT)
                p.op("vector", lambda e: e.reciprocal(out=col2[:], in_=col2[:]), inc=(sDVE, 1))
                p.wait("vector", sDVE)
                p.op("vector", lambda e: e.tensor_scalar(out=gtok[:], in0=gtok[:], scalar1=col2[:],
                                                         scalar2=None, op0=ALU.mult), inc=(sDVE, 1))
                p.wait("vector", sDVE)
                p.op("vector", lambda e: e.reduce_max(out=col1[:], in_=gtok[:], axis=AX),
                     inc=(sDVE, 1))
                p.wait("vector", sDVE)
                p.op("vector", lambda e: e.tensor_scalar(out=gtok2[:], in0=gtok[:], scalar1=col1[:],
                                                         scalar2=None, op0=ALU.is_equal), inc=(sDVE, 1))
                p.wait("vector", sDVE)
                p.op("vector", lambda e: e.scalar_tensor_tensor(out=gtok2[:], in0=gtok2[:], scalar=-2.0,
                                                                in1=gtok[:], op0=ALU.mult, op1=ALU.add),
                     inc=(sDVE, 1))
                p.wait("vector", sDVE)
                p.op("vector", lambda e: e.reduce_max(out=col3[:], in_=gtok2[:], axis=AX),
                     inc=(sDVE, 1))
                p.wait("vector", sDVE)
                p.op("vector", lambda e: e.tensor_scalar(out=gtok2[:], in0=gtok[:], scalar1=col3[:],
                                                         scalar2=None, op0=ALU.is_ge), inc=(sDVE, 1))
                p.wait("vector", sDVE)
                p.op("vector", lambda e: e.tensor_mul(out=gtok[:], in0=gtok[:], in1=gtok2[:]),
                     inc=(sDVE, 1))
                p.wait("vector", sDVE)
                p.op("vector", lambda e: e.tensor_copy(out=gtok_bf[:], in_=gtok[:]), inc=(sDVE, 1))
                p.wait("tensor", sDVE)
                bT = 6 + (tc % 2)
                pbt = banks.acquire(bT)
                p.op("tensor", lambda e, pbt=pbt: e.matmul(pbt[0:8, 0:128], gtok_bf[:], ident_bf[:],
                                                           start=True, stop=True), inc=(sPE, 1))
                p.wait("scalar", sPE)
                p.op("scalar", lambda e, pbt=pbt, tc=tc: e.copy(out=g_sb[:, tc * 128:(tc + 1) * 128],
                                                                in_=pbt[0:8, 0:128]), inc=(sACT, 1))
                banks.release(bT, sACT)
            gates_done = sACT.n

        if "mixer" in phases:
            p.wait("tensor", sW, weights_loaded)
            p.wait("tensor", sACT, nudged_done)
            p.wait("vector", sX, x_loaded)
            for oc in range(NCH):
                for tt in range(NTT):
                    b = (oc * NTT + tt) % 4
                    pbx = banks.acquire(b)
                    for c in range(NCH):
                        p.op("tensor", lambda e, pbx=pbx, c=c, oc=oc, tt=tt: e.matmul(
                            pbx[:, :], mix_tile(c, oc), pfm[:, cs(c, tt * 512, 512)],
                            start=(c == 0), stop=(c == NCH - 1)),
                            inc=(sPE, 1) if c == NCH - 1 else None)
                    p.wait("vector", sPE)
                    p.op("vector", lambda e, pbx=pbx, oc=oc, tt=tt: e.scalar_tensor_tensor(
                        out=x[:, cs(oc, tt * 512, 512)], in0=pbx[:, :], scalar=mixb_sb[:, oc:oc + 1],
                        in1=x[:, cs(oc, tt * 512, 512)], op0=ALU.add, op1=ALU.add), inc=(sDVE, 1))
                    banks.release(b, sDVE)
        pe_after_mixer = sPE.n

        if "moe" in phases:
            expert_bufs = [wbuf[:, 0:8192], wbuf[:, 8192:16384]]

            def ebuf_tile(ebuf, c, oc):
                return ebuf[:, c * D + oc * 128: c * D + oc * 128 + 128]

            w_loaded = {}
            p.wait("gpsimd", sPE, ptok_mm_done if ptok_mm_done is not None else 0)
            p.op("gpsimd", lambda e: e.dma_start(
                out=expert_bufs[0].rearrange("p (c o) -> p c o", c=NCH),
                in_=expert_wT[0].rearrange("(c p) o -> p c o", p=128)), inc=(sWE[0], 16))
            w_loaded[0] = (0, sWE[0].n)
            p.wait("gpsimd", sPE, pe_after_mixer)
            p.op("gpsimd", lambda e: e.dma_start(
                out=expert_bufs[1].rearrange("p (c o) -> p c o", c=NCH),
                in_=expert_wT[1].rearrange("(c p) o -> p c o", p=128)), inc=(sWE[1], 16))
            w_loaded[1] = (1, sWE[1].n)
            if gates_done:
                p.wait("sync", sACT, gates_done)
            grow_loaded = {}
            p.op("sync", lambda e: e.dma_start(out=g_row[:], in_=g_sb[0:1, :]), inc=(sX, 16))
            grow_loaded[0] = sX.n

            p.wait("vector", sDVE)
            last_xne_reader = {0: 0, 1: 0}
            last_gbc_reader = {0: 0, 1: 0}
            last_grow_reader = {0: pe_after_mixer}
            for e_i in range(8):
                ebuf = expert_bufs[e_i % 2]
                p.wait("tensor", sWE[w_loaded[e_i][0]], w_loaded[e_i][1])
                p.wait("tensor", sX, grow_loaded[e_i])
                for tt in range(NTT):
                    par = tt % 2
                    bg = par
                    pbg = banks.acquire(bg)
                    p.op("tensor", lambda e, pbg=pbg, e_i=e_i, tt=tt: e.matmul(
                        pbg[:, :], ones_bf[0:1, :], g_row[0:1, tt * 512:tt * 512 + 512],
                        start=True, stop=True), inc=(sPE, 1))
                    last_grow_reader[0] = sPE.n
                    p.wait("scalar", sPE)
                    p.wait("scalar", sDVE, last_gbc_reader[par])
                    p.op("scalar", lambda e, pbg=pbg, par=par: e.copy(
                        out=gbc[:, par * 512:par * 512 + 512], in_=pbg[:, :]), inc=(sACT, 1))
                    banks.release(bg, sACT)
                    p.wait("vector", sACT)
                    p.wait("vector", sPE, last_xne_reader[0])
                    for c in range(NCH):
                        p.op("vector", lambda e, c=c, tt=tt, par=par: e.tensor_mul(
                            out=xne[:, c * 512: c * 512 + 512],
                            in0=xn[:, cs(c, tt * 512, 512)],
                            in1=gbc[:, par * 512:par * 512 + 512]),
                            inc=(sDVE, 1) if c == NCH - 1 else None)
                    last_gbc_reader[par] = sDVE.n
                    p.wait("tensor", sDVE)
                    for oc in range(NCH):
                        pbx = banks.acquire(oc)
                        for c in range(NCH):
                            p.op("tensor", lambda e, pbx=pbx, ebuf=ebuf, c=c, oc=oc, par=par, e_i=e_i: e.matmul(
                                pbx[:, :], ebuf_tile(ebuf, c, oc),
                                xne[:, c * 512: c * 512 + 512],
                                start=(c == 0), stop=(c == NCH - 1 and e_i != 0)),
                                inc=(sPE, 1) if (c == NCH - 1 and e_i != 0) else None)
                        if e_i == 0:
                            p.op("tensor", lambda e, pbx=pbx, oc=oc, tt=tt: e.matmul(
                                pbx[:, :], eb_bf[:, oc * 128:oc * 128 + 128],
                                g_sb[:, tt * 512:tt * 512 + 512],
                                start=False, stop=True), inc=(sPE, 1))
                        p.wait("vector", sPE)
                        p.op("vector", lambda e, pbx=pbx, oc=oc, tt=tt: e.tensor_add(
                            out=x[:, cs(oc, tt * 512, 512)], in0=x[:, cs(oc, tt * 512, 512)],
                            in1=pbx[:, :]), inc=(sDVE, 1))
                        banks.release(oc, sDVE)
                    last_xne_reader[0] = sPE.n
                if e_i + 2 < 8:
                    p.wait("gpsimd", sPE, sPE.n)
                    p.op("gpsimd", lambda e, e_i=e_i: e.dma_start(
                        out=expert_bufs[e_i % 2].rearrange("p (c o) -> p c o", c=NCH),
                        in_=expert_wT[e_i + 2].rearrange("(c p) o -> p c o", p=128)),
                        inc=(sWE[e_i % 2], 16))
                    w_loaded[e_i + 2] = (e_i % 2, sWE[e_i % 2].n)
                if e_i + 1 < 8:
                    p.wait("sync", sPE, last_grow_reader[0])
                    p.op("sync", lambda e, e_i=e_i: e.dma_start(
                        out=g_row[:], in_=g_sb[e_i + 1:e_i + 2, :]), inc=(sX, 16))
                    grow_loaded[e_i + 1] = sX.n

        p.wait("sync", sDVE)
        p.wait("sync", sACT)
        p.op("sync", lambda e: e.dma_start(
            out=x1T.rearrange("(c p) t -> p c t", p=128),
            in_=x[:].rearrange("p (c t) -> p c t", c=NCH)), inc=(sX, 16))
        if dbg:
            p.wait("gpsimd", sDVE)
            p.wait("gpsimd", sACT)
        if "xn" in dbg:
            p.op("gpsimd", lambda e: e.dma_start(
                out=dbg["xn"].rearrange("(c p) t -> p c t", p=128),
                in_=xn[:].rearrange("p (c t) -> p c t", c=NCH)), inc=(sW, 16))
        if "pfm" in dbg:
            p.op("gpsimd", lambda e: e.dma_start(
                out=dbg["pfm"].rearrange("(c p) t -> p c t", p=128),
                in_=pfm[:].rearrange("p (c t) -> p c t", c=NCH)), inc=(sW, 16))
        if "gacc" in dbg:
            p.op("gpsimd", lambda e: e.dma_start(out=dbg["gacc"], in_=gacc[:]), inc=(sW, 16))
        if "mg" in dbg:
            p.op("gpsimd", lambda e: e.dma_start(out=dbg["mg"], in_=mg[:]), inc=(sW, 16))
        if "gates" in dbg:
            p.op("gpsimd", lambda e: e.dma_start(out=dbg["gates"], in_=g_sb[:]), inc=(sW, 16))
        p.wait("sync", sW)
        p.wait("sync", sX)

        p.finalize(block)
    return nc


def l0_in_map(g, xb):
    return {
        "xT": np.ascontiguousarray(xb.T),
        "projT": g["projT"], "metric": g["metric"],
        "mixer_wT": g["mixer_wT"], "mixer_b": g["mixer_b"],
        "gate_wT": g["gate_wT"], "gate_b": g["gate_b"],
        "expert_wT": g["expert_wT"], "expert_b": g["expert_b"],
    }


import math


def build_l1(debug=(), phases=("attn", "ffn")):
    nc = bass.Bass("TRN2", target_bir_lowering=False)
    x1T = nc.dram_tensor("x1T", [D, S], F32, kind="ExternalInput").ap()
    W = {}
    for name, shape in [("wq", [D, 256]), ("bq", [128, 2]), ("wqc", [256, 896]), ("bqc", [128, 7]),
                        ("wqr", [D, 128]), ("bqr", [128, 1]), ("wkv", [D, 256]), ("bkv", [128, 2]),
                        ("wkr", [D, 128]), ("bkr", [128, 1]), ("wkc", [256, 896]), ("bkc", [128, 7]),
                        ("wv", [256, D]), ("bv", [128, 8]), ("wo", [D, D]), ("bo", [128, 8]),
                        ("w1T", [D, 4096]), ("convw", [128, 96]), ("convb", [128, 32]),
                        ("w2T", [4096, D])]:
        W[name] = nc.dram_tensor(name, shape, F32, kind="ExternalInput").ap()
    outT = nc.dram_tensor("outT", [D, S], F32, kind="ExternalOutput").ap()
    dbg = {}
    for name in debug:
        shp = {"xn": [D, S], "q": [D, S], "k": [D, S], "attnfm": [D, S], "xq": [D, S],
               "s0": [128, 32 * 1024], "mrow": [1, S], "woacc": [128, 4096],
               "bprime": [128, 8], "h0": [128, 1025], "w1q": [128, 2048],
               "w2q": [128, 4096]}[name]
        dbg[name] = nc.dram_tensor("dbg_" + name, shp, F32, kind="ExternalOutput").ap()

    with ExitStack() as es:
        sb = lambda name, shape, dt=F32, **kw: es.enter_context(nc.sbuf_tensor(name, shape, dt, **kw))
        xn = sb("xn", [128, NCH * S], BF16)
        arena = sb("arena", [128, 49664], BF16)
        Q = arena[:, 0:16384]
        Kb = arena[:, 16384:32768]
        Vb = arena[:, 32768:49408]
        xq = arena[:, 0:16384]
        s_sb = arena[:, 16384:49152]
        arena2 = sb("arena2", [128, 8192], BF16)
        cq = arena2[:, 0:4096]
        ckv = arena2[:, 4096:8192]
        wo_sb = arena2[:, 0:8192]
        w1q = arena2[:, 0:2048]
        w2q = arena2[:, 2048:6144]
        hbuf = arena2[:, 6144:7169]
        arena3 = sb("arena3", [128, 8192], F32)
        cosT = arena3[:, 0:2048]
        sinT = arena3[:, 2048:4096]
        awbuf = arena3[:, 4096:8192].bitcast(BF16)
        pwq = awbuf[:, 0:2048]
        pwkv = awbuf[:, 2048:4096]
        pwqc = awbuf[:, 4096:5888]
        pwkc = awbuf[:, 5888:7680]
        qrS = arena3[:, 4096:6144]
        rotS = arena3[:, 6144:8192]
        attn_out = arena3[:].bitcast(BF16)
        inv_bc = arena3[:, 0:2048]
        sc_bc = arena3[:, 2048:4096]
        sc2_bc = arena3[:, 2048:3072]
        inv2_bc = arena3[:, 3072:4096]
        mfold = arena3[:, 4096:6144]
        ubuf = arena3[:, 4096:5120]
        ystg = arena3[:, 6144:8192]
        xstg = sb("xstg", [128, 4096], F32)
        wqr_sb = sb("wqr_sb", [128, 1024], BF16)
        wkr_sb = sb("wkr_sb", [128, 1024], BF16)
        wv_sb = sb("wv_sb", [128, 2048], BF16)
        bias_sb = {}
        for name, n in (("bq", 2), ("bqc", 7), ("bqr", 1), ("bkv", 2), ("bkr", 1), ("bkc", 7),
                        ("bv", 8), ("bo", 8), ("convw", 96), ("convb", 32)):
            bias_sb[name] = sb("b_" + name, [128, n], F32)
        bv_bf = sb("bv_bf", [128, 8], BF16)
        bprime = sb("bprime", [128, 8], F32)
        ones_bf = sb("ones_bf", [128, 128], BF16)
        ones_f = sb("ones_f", [128, 128], F32)
        ident_bf = sb("ident_bf", [128, 128], BF16)
        tri_bf = sb("tri_bf", [128, 128], BF16)
        hpi = sb("hpi", [64, 1], F32)
        invf = sb("invf", [64, 1], F32)
        x2s = arena[:, 1024:1536]
        scr = arena[:, 0:1024].bitcast(F32)
        rows = arena[:, 3584:6656].bitcast(F32)
        rowm = rows[0:1, 0:512]
        rowe = rows[0:1, 512:1024]
        rowt = rows[0:1, 1024:1536]
        ab = arena[:, 1536:3584].bitcast(F32)
        ptb = sb("ptb", [128, 3 * 512], BF16)
        rrow = xstg[0:1, 3584:4096]
        rstg = xstg[0:64, 2560:3072]
        scrA = xstg[0:64, 2048:2304].bitcast(BF16)
        qmaxc = sb("qmaxc", [128, 9], F32)
        qkmax = sb("qkmax", [128, 2], BF16)
        negb = sb("negb", [128, 1], F32)
        brow = sb("brow", [1, 4], F32)
        wcols = sb("wcols", [128, 33], F32)
        wtot_bf = sb("wtot_bf", [128, 1], BF16)
        winv = sb("winv", [1, 2], F32)
        wscal = sb("wscal", [1, 2], F32)
        swcol = sb("swcol", [128, 2], F32)
        mcol = sb("mcol", [128, 1], F32)
        mcol_bf = sb("mcol_bf", [128, 1], BF16)
        pb = [es.enter_context(nc.psum_tensor(f"pb{i}", [128, 512], F32)) for i in range(8)]

        p = Prog(nc, es)
        sW = p.sem("sW")
        sX = p.sem("sX")
        sPE = p.sem("sPE")
        sDVE = p.sem("sDVE")
        sACT = p.sem("sACT")
        sPOOL = p.sem("sPOOL")
        banks = Banks(p, pb)
        block = es.enter_context(nc.Block())

        p.op("gpsimd", lambda e: e.memset(ones_f[:], 1.0))
        p.op("gpsimd", lambda e: e.tensor_copy(out=ones_bf[:], in_=ones_f[:]))
        p.op("gpsimd", lambda e: e.tensor_copy(out=ident_bf[:], in_=ones_f[:]))
        p.op("gpsimd", lambda e: e.affine_select(ident_bf[:], ident_bf[:], pattern=[[-1, 128]],
                                                 compare_op=ALU.is_equal, fill=0.0, base=0,
                                                 channel_multiplier=1))
        p.op("gpsimd", lambda e: e.tensor_copy(out=tri_bf[:], in_=ones_f[:]))
        p.op("gpsimd", lambda e: e.affine_select(tri_bf[:], tri_bf[:], pattern=[[1, 128]],
                                                 compare_op=ALU.is_ge, fill=0.0, base=0,
                                                 channel_multiplier=-1))
        p.op("gpsimd", lambda e: e.memset(hpi[:], math.pi / 2))
        p.op("gpsimd", lambda e: e.iota(invf[:], pattern=[[0, 1]], base=0, channel_multiplier=1,
                                        allow_small_or_imprecise_dtypes=True))
        p.op("gpsimd", lambda e: e.iota(xstg[0:64, 0:2048], pattern=[[1, 2048]], base=0,
                                        channel_multiplier=0,
                                        allow_small_or_imprecise_dtypes=True), inc=(sPOOL, 1))
        p.wait("scalar", sPOOL, 1)
        p.op("scalar", lambda e: e.activation(out=invf[:], in_=invf[:], func=AF.Exp,
                                              scale=-math.log(10000.0) / 64.0), inc=(sACT, 1))
        p.wait("vector", sACT)
        tmpA = xstg[0:64, 0:2048]
        tmpB = xstg[0:64, 2048:4096]
        p.op("vector", lambda e: e.tensor_scalar(out=tmpA, in0=tmpA, scalar1=invf[0:64, 0:1],
                                                 scalar2=None, op0=ALU.mult), inc=(sDVE, 1))
        p.wait("vector", sDVE)
        p.op("vector", lambda e: e.tensor_scalar(out=tmpB, in0=tmpA, scalar1=1.0 / (2 * math.pi),
                                                 scalar2=MAGIC, op0=ALU.mult, op1=ALU.add))
        p.op("vector", lambda e: e.tensor_scalar(out=tmpB, in0=tmpB, scalar1=MAGIC,
                                                 scalar2=None, op0=ALU.subtract))
        p.op("vector", lambda e: e.scalar_tensor_tensor(out=tmpA, in0=tmpB, scalar=-2 * math.pi,
                                                        in1=tmpA, op0=ALU.mult, op1=ALU.add),
             inc=(sDVE, 1))
        p.wait("scalar", sDVE)
        p.op("scalar", lambda e: e.activation(out=tmpB, in_=tmpA, func=AF.Sin, scale=0.5))
        p.op("scalar", lambda e: e.activation(out=sinT[0:64, :], in_=tmpA, func=AF.Sin,
                                              scale=-0.5, bias=hpi[:, 0:1]))
        p.op("scalar", lambda e: e.activation(out=cosT[0:64, :], in_=tmpB, func=AF.Square),
             inc=(sACT, 1))
        p.wait("vector", sACT)
        p.op("vector", lambda e: e.scalar_tensor_tensor(out=sinT[0:64, :], in0=tmpB, scalar=2.0,
                                                        in1=sinT[0:64, :], op0=ALU.mult,
                                                        op1=ALU.mult))
        p.op("vector", lambda e: e.tensor_scalar(out=cosT[0:64, :], in0=cosT[0:64, :],
                                                 scalar1=-2.0, scalar2=1.0,
                                                 op0=ALU.mult, op1=ALU.add), inc=(sDVE, 1))
        p.wait("sync", sDVE)
        p.op("sync", lambda e: e.dma_start(out=cosT[64:128, :], in_=cosT[0:64, :]), inc=(sX, 16))
        p.op("sync", lambda e: e.dma_start(out=sinT[64:128, :], in_=sinT[0:64, :]), inc=(sX, 16))
        p.wait("vector", sX)
        p.op("vector", lambda e: e.tensor_scalar(out=sinT[0:64, :], in0=sinT[0:64, :],
                                                 scalar1=-1.0, scalar2=None, op0=ALU.mult),
             inc=(sDVE, 1))
        rope_done = sDVE.n

        p.op("gpsimd", lambda e: e.dma_start(out=outT, in_=x1T), inc=(sW, 16))
        out_init = sW.n
        for name, dst, c in [("wq", pwq, NCH), ("wkv", pwkv, NCH), ("wqc", pwqc, 2),
                             ("wkc", pwkc, 2), ("wqr", wqr_sb, NCH), ("wkr", wkr_sb, NCH),
                             ("wv", wv_sb, 2)]:
            p.op("gpsimd", lambda e, name=name, dst=dst, c=c: e.dma_start(
                out=dst.rearrange("p (c o) -> p c o", c=c),
                in_=W[name].rearrange("(c p) o -> p c o", p=128)), inc=(sW, 16))
        attw_loaded = sW.n
        sB = p.sem("sB")
        for name in ("bq", "bqc", "bqr", "bkv", "bkr", "bkc", "bv", "bo", "convw", "convb"):
            p.op("sync", lambda e, name=name: e.dma_start(out=bias_sb[name][:], in_=W[name]),
                 inc=(sB, 16))
        bias_loaded = sB.n

        xstg_free = (sDVE, rope_done)
        for tt in range(NTT):
            sem0, cnt0 = xstg_free
            p.wait("sync", sem0, cnt0)
            p.op("sync", lambda e, tt=tt: e.dma_start(
                out=xstg[:].rearrange("p (c t) -> p c t", c=NCH),
                in_=x1T.rearrange("(c p) t -> p c t", p=128)[:, :, tt * 512:(tt + 1) * 512]),
                inc=(sX, 16))
            xld = sX.n

            def xs(c):
                return xstg[:, c * 512: c * 512 + 512]

            p.wait("vector", sX, xld)
            p.wait("scalar", sX, xld)
            for c in range(NCH):
                p.op("vector", lambda e, c=c, tt=tt: e.tensor_copy(
                    out=xn[:, cs(c, tt * 512, 512)], in_=xs(c)),
                    inc=(sDVE, 1) if c == NCH - 1 else None)
            p.wait("tensor", sDVE)
            b0 = banks.acquire(0)
            b1 = banks.acquire(1)
            for c in range(NCH):
                p.wait("scalar", sPE)
                p.op("scalar", lambda e, c=c: e.activation(
                    out=x2s[:], in_=xs(c), func=AF.Square), inc=(sACT, 1))
                p.op("tensor", lambda e, c=c, tt=tt, b0=b0: e.matmul(
                    b0[0:1, :], ones_bf[:, 0:1], xn[:, cs(c, tt * 512, 512)],
                    start=(c == 0), stop=(c == NCH - 1)))
                p.wait("tensor", sACT)
                p.op("tensor", lambda e, c=c, b1=b1: e.matmul(
                    b1[0:1, :], ones_bf[:, 0:1], x2s[:],
                    start=(c == 0), stop=(c == NCH - 1)), inc=(sPE, 1))
            p.wait("vector", sPE)
            p.op("vector", lambda e, b0=b0: e.tensor_scalar(out=rowm, in0=b0[0:1, :],
                                                            scalar1=1.0 / D, scalar2=None, op0=ALU.mult))
            p.op("vector", lambda e, b1=b1: e.tensor_scalar(out=rowe, in0=b1[0:1, :],
                                                            scalar1=1.0 / D, scalar2=None, op0=ALU.mult),
                 inc=(sDVE, 1))
            banks.release(0, sDVE)
            banks.release(1, sDVE)
            p.op("vector", lambda e: e.tensor_mul(out=rowt, in0=rowm, in1=rowm))
            p.op("vector", lambda e: e.tensor_sub(out=rowe, in0=rowe, in1=rowt))
            p.op("vector", lambda e: e.tensor_scalar(out=rowe, in0=rowe, scalar1=1e-5,
                                                     scalar2=None, op0=ALU.add), inc=(sDVE, 1))
            p.wait("scalar", sDVE)
            p.op("scalar", lambda e: e.activation(out=rowt, in_=rowe, func=AF.Sqrt),
                 inc=(sACT, 1))
            p.wait("vector", sACT)
            p.op("vector", lambda e: e.reciprocal(out=rowt, in_=rowt))
            p.op("vector", lambda e: e.tensor_mul(out=rowm, in0=rowm, in1=rowt))
            p.op("vector", lambda e: e.tensor_scalar(out=rowm, in0=rowm, scalar1=-1.0,
                                                     scalar2=None, op0=ALU.mult), inc=(sDVE, 1))
            p.wait("tensor", sDVE)
            b2 = banks.acquire(2)
            b3 = banks.acquire(3)
            p.op("tensor", lambda e, b2=b2: e.matmul(b2[:, :], ones_f[0:1, :], rowt,
                                                     start=True, stop=True))
            p.op("tensor", lambda e, b3=b3: e.matmul(b3[:, :], ones_f[0:1, :], rowm,
                                                     start=True, stop=True), inc=(sPE, 1))
            p.wait("scalar", sPE)
            p.op("scalar", lambda e, b2=b2: e.copy(out=ab[:, 0:512], in_=b2[:, :]))
            p.op("scalar", lambda e, b3=b3: e.copy(out=ab[:, 512:1024], in_=b3[:, :]), inc=(sACT, 1))
            banks.release(2, sACT)
            banks.release(3, sACT)
            p.wait("vector", sACT)
            for c in range(NCH):
                p.op("vector", lambda e, c=c, tt=tt: e.tensor_mul(
                    out=scr[:], in0=xs(c), in1=ab[:, 0:512]))
                p.op("vector", lambda e, c=c, tt=tt: e.tensor_add(
                    out=xn[:, cs(c, tt * 512, 512)], in0=scr[:], in1=ab[:, 512:1024]),
                    inc=(sDVE, 1) if c == NCH - 1 else None)
            xstg_free = (sDVE, sDVE.n)
        ln_done = sDVE.n

        attn_accum_done = 0
        pe_attn_end = 0
        if "attn" in phases:
            p.wait("tensor", sW, attw_loaded)
            p.wait("tensor", sDVE, ln_done)
            p.wait("scalar", sB, bias_loaded)

            def proj_block(dst_fn, w_tile_fn, nin_ch, bias_name, noc, func, src_fn):
                for oc in range(noc):
                    for tt in range(NTT):
                        b = (oc * NTT + tt) % 4
                        pbx = banks.acquire(b)
                        for c in range(nin_ch):
                            p.op("tensor", lambda e, pbx=pbx, c=c, oc=oc, tt=tt: e.matmul(
                                pbx[:, :], w_tile_fn(c, oc), src_fn(c, tt),
                                start=(c == 0), stop=(c == nin_ch - 1)),
                                inc=(sPE, 1) if c == nin_ch - 1 else None)
                        p.wait("scalar", sPE)
                        p.op("scalar", lambda e, pbx=pbx, oc=oc, tt=tt: e.activation(
                            out=dst_fn(oc, tt), in_=pbx[:, :], func=func,
                            bias=bias_sb[bias_name][:, oc:oc + 1]), inc=(sACT, 1))
                        banks.release(b, sACT)

            def xn_src(c, tt):
                return xn[:, cs(c, tt * 512, 512)]

            def sl(buf, oc, tt):
                return buf[:, oc * 2048 + tt * 512: oc * 2048 + tt * 512 + 512]

            proj_block(lambda oc, tt: sl(cq, oc, tt),
                       lambda c, oc: pwq[:, c * 256 + oc * 128: c * 256 + oc * 128 + 128],
                       NCH, "bq", 2, AF.Gelu, xn_src)
            proj_block(lambda oc, tt: sl(ckv, oc, tt),
                       lambda c, oc: pwkv[:, c * 256 + oc * 128: c * 256 + oc * 128 + 128],
                       NCH, "bkv", 2, AF.Gelu, xn_src)
            cqkv_done = sACT.n

            p.wait("tensor", sACT, cqkv_done)
            proj_block(lambda oc, tt: sl(Q, oc, tt),
                       lambda c, oc: pwqc[:, c * 896 + oc * 128: c * 896 + oc * 128 + 128],
                       2, "bqc", 7, AF.Gelu, lambda c, tt: sl(cq, c, tt))
            proj_block(lambda oc, tt: sl(Kb, oc, tt),
                       lambda c, oc: pwkc[:, c * 896 + oc * 128: c * 896 + oc * 128 + 128],
                       2, "bkc", 7, AF.Gelu, lambda c, tt: sl(ckv, c, tt))

            Vv = Vb.rearrange("p (t h o) -> p t h o", t=NTC, h=NH)
            p.op("gpsimd", lambda e: e.memset(Vv[:, :, :, 64:65], 1.0), inc=(sPOOL, 1))
            for tc in range(NTC):
                for n2 in range(2):
                    b = 4 + (tc * 2 + n2) % 2
                    pbx = banks.acquire(b)
                    for c in range(2):
                        p.op("tensor", lambda e, pbx=pbx, c=c, tc=tc, n2=n2: e.matmul(
                            pbx[:, :], ckv[:, c * 2048 + tc * 128: c * 2048 + tc * 128 + 128],
                            wv_sb[:, c * 1024 + n2 * 512: c * 1024 + n2 * 512 + 512],
                            start=(c == 0), stop=(c == 1)), inc=(sPE, 1) if c == 1 else None)
                    p.wait("scalar", sPE)
                    p.op("scalar", lambda e, pbx=pbx, tc=tc, n2=n2: e.activation(
                        out=Vv[:, tc, n2 * 8:(n2 + 1) * 8, 0:64],
                        in_=pbx[:, :].rearrange("p (h o) -> p h o", h=8), func=AF.Copy),
                        inc=(sACT, 1))
                    banks.release(b, sACT)
            v_done = sACT.n
            qkv_mm_done = sPE.n

            p.wait("vector", sDVE, rope_done)
            for which, w_sb, bias_name, dstc in (("q", wqr_sb, "bqr", Q), ("k", wkr_sb, "bkr", Kb)):
                for tt in range(NTT):
                    b = (tt % 2)
                    pbx = banks.acquire(b)
                    for c in range(NCH):
                        p.op("tensor", lambda e, pbx=pbx, c=c, tt=tt, w_sb=w_sb: e.matmul(
                            pbx[:, :], w_sb[:, c * 128: c * 128 + 128], xn_src(c, tt),
                            start=(c == 0), stop=(c == NCH - 1)),
                            inc=(sPE, 1) if c == NCH - 1 else None)
                    if which == "q" and tt == 0:
                        p.wait("scalar", sPE, qkv_mm_done)
                    p.wait("scalar", sPE)
                    p.op("scalar", lambda e, pbx=pbx, tt=tt, bias_name=bias_name: e.activation(
                        out=qrS[:, tt * 512:(tt + 1) * 512], in_=pbx[:, :], func=AF.Identity,
                        bias=bias_sb[bias_name][:, 0:1]), inc=(sACT, 1))
                    banks.release(b, sACT)
                p.wait("sync", sACT)
                p.op("sync", lambda e: e.dma_start(out=rotS[0:64, :], in_=qrS[64:128, :]),
                     inc=(sX, 16))
                p.op("sync", lambda e: e.dma_start(out=rotS[64:128, :], in_=qrS[0:64, :]),
                     inc=(sX, 16))
                p.wait("vector", sX)
                p.wait("vector", sACT)
                p.op("vector", lambda e: e.tensor_mul(out=qrS[:], in0=qrS[:], in1=cosT[:]))
                p.op("vector", lambda e: e.tensor_mul(out=rotS[:], in0=rotS[:], in1=sinT[:]))
                p.op("vector", lambda e: e.tensor_add(out=qrS[:], in0=qrS[:], in1=rotS[:]),
                     inc=(sDVE, 1))
                p.wait("scalar", sDVE)
                for tt in range(NTT):
                    p.op("scalar", lambda e, tt=tt, dstc=dstc: e.activation(
                        out=dstc[:, 7 * 2048 + tt * 512: 7 * 2048 + tt * 512 + 512],
                        in_=qrS[:, tt * 512:(tt + 1) * 512], func=AF.Gelu),
                        inc=(sACT, 1) if tt == NTT - 1 else None)
            qk_done = sACT.n

            p.wait("vector", sACT, qk_done)
            for i, buf in enumerate((Q, Kb)):
                for c in range(NCH):
                    p.op("vector", lambda e, c=c, buf=buf: e.tensor_reduce(
                        out=qmaxc[:, c:c + 1], in_=buf[:, c * 2048:(c + 1) * 2048], axis=AX,
                        op=ALU.max, apply_absolute_value=True),
                        inc=(sDVE, 1) if c == NCH - 1 else None)
                p.wait("vector", sDVE)
                p.op("vector", lambda e: e.tensor_reduce(out=qmaxc[:, 8:9], in_=qmaxc[:, 0:8],
                                                         axis=AX, op=ALU.max), inc=(sDVE, 1))
                p.wait("vector", sDVE)
                p.op("vector", lambda e, i=i: e.tensor_copy(out=qkmax[:, i:i + 1],
                                                            in_=qmaxc[:, 8:9]), inc=(sDVE, 1))
                p.wait("vector", sDVE)
            p.wait("tensor", sDVE)
            b0 = banks.acquire(0)
            p.op("tensor", lambda e, b0=b0: e.matmul(b0[0:1, 0:128], qkmax[:, 0:1], ident_bf[:],
                                                     start=True, stop=True))
            p.op("tensor", lambda e, b0=b0: e.matmul(b0[0:1, 128:256], qkmax[:, 1:2], ident_bf[:],
                                                     start=True, stop=True), inc=(sPE, 1))
            p.wait("vector", sPE)
            p.op("vector", lambda e, b0=b0: e.reduce_max(out=brow[:, 0:1], in_=b0[0:1, 0:128],
                                                         axis=AX))
            p.op("vector", lambda e, b0=b0: e.reduce_max(out=brow[:, 1:2], in_=b0[0:1, 128:256],
                                                         axis=AX), inc=(sDVE, 1))
            banks.release(0, sDVE)
            p.wait("vector", sDVE)
            p.op("vector", lambda e: e.tensor_mul(out=brow[:, 2:3], in0=brow[:, 0:1],
                                                  in1=brow[:, 1:2]), inc=(sDVE, 1))
            p.wait("vector", sDVE)
            p.op("vector", lambda e: e.tensor_scalar(out=brow[:, 2:3], in0=brow[:, 2:3],
                                                     scalar1=-8.0, scalar2=None, op0=ALU.mult),
                 inc=(sDVE, 1))
            p.wait("tensor", sDVE)
            b0 = banks.acquire(0)
            p.op("tensor", lambda e, b0=b0: e.matmul(b0[:, 0:1], ones_f[0:1, :], brow[:, 2:3],
                                                     start=True, stop=True), inc=(sPE, 1))
            p.wait("scalar", sPE)
            p.op("scalar", lambda e, b0=b0: e.copy(out=negb[:], in_=b0[:, 0:1]), inc=(sACT, 1))
            banks.release(0, sACT)
            bound_done = sACT.n

            p.wait("tensor", sACT, bound_done)
            p.wait("tensor", sACT, v_done)
            p.wait("tensor", sPOOL, 2)
            ptb_reader = [0, 0, 0]
            scrA_reader = 0
            for h in range(NH):
                base = (h % 2) * 64
                jc = h // 2
                for qg in range(NTT):
                    nkt = 4 * qg + 4
                    bO = 4 + (h * NTT + qg) % 2
                    pbO = banks.acquire(bO)
                    for kt in range(nkt):
                        bS = (kt % 2)
                        pbS = banks.acquire(bS)
                        p.op("tensor", lambda e, pbS=pbS, base=base, jc=jc, kt=kt, qg=qg: e.matmul(
                            pbS[:, :],
                            Kb[base:base + 64, jc * 2048 + kt * 128: jc * 2048 + kt * 128 + 128],
                            Q[base:base + 64, jc * 2048 + qg * 512: jc * 2048 + qg * 512 + 512],
                            start=True, stop=True), inc=(sPE, 1))
                        slot = kt % 3
                        p.wait("scalar", sPE)
                        p.wait("scalar", sPE, ptb_reader[slot])
                        p.op("scalar", lambda e, pbS=pbS, slot=slot: e.activation(
                            out=ptb[:, slot * 512:(slot + 1) * 512], in_=pbS[:, :], func=AF.Exp,
                            scale=0.125, bias=negb[:, 0:1]), inc=(sACT, 1))
                        banks.release(bS, sACT)
                        ktrel = kt - 4 * qg
                        if ktrel >= 0:
                            p.wait("vector", sACT)
                            if ktrel > 0:
                                p.op("vector", lambda e, slot=slot, ktrel=ktrel: e.memset(
                                    ptb[:, slot * 512: slot * 512 + ktrel * 128], 0.0))
                            p.op("vector", lambda e, slot=slot, ktrel=ktrel: e.tensor_mul(
                                out=ptb[:, slot * 512 + ktrel * 128: slot * 512 + (ktrel + 1) * 128],
                                in0=ptb[:, slot * 512 + ktrel * 128: slot * 512 + (ktrel + 1) * 128],
                                in1=tri_bf[:]), inc=(sDVE, 1))
                            p.wait("tensor", sDVE)
                        else:
                            p.wait("tensor", sACT)
                        p.op("tensor", lambda e, pbO=pbO, h=h, kt=kt, slot=slot, nkt=nkt: e.matmul(
                            pbO[0:65, :], Vv[:, kt, h, :], ptb[:, slot * 512:(slot + 1) * 512],
                            start=(kt == 0), stop=(kt == nkt - 1)), inc=(sPE, 1))
                        ptb_reader[slot] = sPE.n
                    p.wait("vector", sPE)
                    p.op("vector", lambda e, pbO=pbO: e.reciprocal(out=rrow[:], in_=pbO[64:65, :]),
                         inc=(sDVE, 1))
                    p.wait("tensor", sDVE)
                    bR = 6 + (h * NTT + qg) % 2
                    pbR = banks.acquire(bR)
                    p.op("tensor", lambda e, pbR=pbR: e.matmul(pbR[0:64, :], ones_f[0:1, 0:64],
                                                               rrow[:], start=True, stop=True),
                         inc=(sPE, 1))
                    p.wait("scalar", sPE)
                    p.op("scalar", lambda e, pbR=pbR: e.copy(out=rstg[:], in_=pbR[0:64, :]),
                         inc=(sACT, 1))
                    banks.release(bR, sACT)
                    p.wait("vector", sACT)
                    p.wait("vector", sACT, scrA_reader)
                    p.op("vector", lambda e, pbO=pbO: e.tensor_mul(out=scrA[:], in0=pbO[0:64, :],
                                                                   in1=rstg[:]), inc=(sDVE, 1))
                    banks.release(bO, sDVE)
                    p.wait("scalar", sDVE)
                    p.op("scalar", lambda e, base=base, jc=jc, qg=qg: e.activation(
                        out=attn_out[base:base + 64,
                                     jc * 2048 + qg * 512: jc * 2048 + qg * 512 + 512],
                        in_=scrA[:], func=AF.Copy), inc=(sACT, 1))
                    scrA_reader = sACT.n
            scores_done = sACT.n
            pe_attn_end = sPE.n

            p.wait("gpsimd", sPE, pe_attn_end)
            p.op("gpsimd", lambda e: e.dma_start(
                out=wo_sb.rearrange("p (c o) -> p c o", c=NCH),
                in_=W["wo"].rearrange("(c p) o -> p c o", p=128)), inc=(sW, 16))
            wo_loaded = sW.n
            p.wait("vector", sB, bias_loaded)
            p.op("vector", lambda e: e.tensor_copy(out=bv_bf[:], in_=bias_sb["bv"][:]),
                 inc=(sDVE, 1))
            p.wait("tensor", sW, wo_loaded)
            p.wait("tensor", sDVE)
            for oc in range(NCH):
                b = oc % 2
                pbx = banks.acquire(b)
                for c in range(NCH):
                    p.op("tensor", lambda e, pbx=pbx, c=c, oc=oc: e.matmul(
                        pbx[:, 0:1], wo_sb[:, c * 1024 + oc * 128: c * 1024 + oc * 128 + 128],
                        bv_bf[:, c:c + 1], start=(c == 0), stop=(c == NCH - 1)),
                        inc=(sPE, 1) if c == NCH - 1 else None)
                p.wait("scalar", sPE)
                p.op("scalar", lambda e, pbx=pbx, oc=oc: e.activation(
                    out=bprime[:, oc:oc + 1], in_=pbx[:, 0:1], func=AF.Identity,
                    bias=bias_sb["bo"][:, oc:oc + 1]), inc=(sACT, 1))
                banks.release(b, sACT)
            bprime_done = sACT.n

            p.wait("tensor", sACT, scores_done)
            p.wait("tensor", sACT, bprime_done)
            xstg_f = xstg[:]
            astg_reader = {0: 0, 1: 0}
            for oc in range(NCH):
                par = oc % 2
                for tt in range(NTT):
                    b = 2 + (oc * NTT + tt) % 4
                    pbx = banks.acquire(b)
                    for c in range(NCH):
                        p.op("tensor", lambda e, pbx=pbx, c=c, oc=oc, tt=tt: e.matmul(
                            pbx[:, :], wo_sb[:, c * 1024 + oc * 128: c * 1024 + oc * 128 + 128],
                            attn_out[:, cs(c, tt * 512, 512)],
                            start=(c == 0), stop=(c == NCH - 1)),
                            inc=(sPE, 1) if c == NCH - 1 else None)
                    p.wait("scalar", sPE)
                    p.wait("scalar", sW, astg_reader[par] and sW.n)
                    p.op("scalar", lambda e, pbx=pbx, oc=oc, tt=tt, par=par: e.activation(
                        out=xstg_f[:, par * 2048 + tt * 512: par * 2048 + tt * 512 + 512],
                        in_=pbx[:, :], func=AF.Identity, bias=bprime[:, oc:oc + 1]), inc=(sACT, 1))
                    banks.release(b, sACT)
                p.wait("gpsimd", sACT)
                p.wait("gpsimd", sW)
                p.op("gpsimd", lambda e, oc=oc, par=par: e.dma_start(
                    out=outT.rearrange("(c p) t -> p c t", p=128)[:, oc, :],
                    in_=xstg_f[:, par * 2048: par * 2048 + 2048],
                    accum_op=ALU.add), inc=(sW, 16))
                astg_reader[par] = sW.n
            attn_accum_done = sW.n

        if "ffn" in phases:
            emit_ffn(p, banks, locals())

        p.wait("gpsimd", sDVE)
        p.wait("gpsimd", sACT)
        if "xn" in dbg:
            p.op("gpsimd", lambda e: e.dma_start(
                out=dbg["xn"].rearrange("(c p) t -> p c t", p=128),
                in_=xn[:].rearrange("p (c t) -> p c t", c=NCH)), inc=(sW, 16))
        if "q" in dbg:
            p.op("gpsimd", lambda e: e.dma_start(
                out=dbg["q"].rearrange("(c p) t -> p c t", p=128),
                in_=Q.rearrange("p (c t) -> p c t", c=NCH)), inc=(sW, 16))
        if "k" in dbg:
            p.op("gpsimd", lambda e: e.dma_start(
                out=dbg["k"].rearrange("(c p) t -> p c t", p=128),
                in_=Kb.rearrange("p (c t) -> p c t", c=NCH)), inc=(sW, 16))
        if "attnfm" in dbg:
            p.op("gpsimd", lambda e: e.dma_start(
                out=dbg["attnfm"].rearrange("(c p) t -> p c t", p=128),
                in_=attn_out[:].rearrange("p (c t) -> p c t", c=NCH)), inc=(sW, 16))
        if "xq" in dbg:
            p.op("gpsimd", lambda e: e.dma_start(
                out=dbg["xq"].rearrange("(c p) t -> p c t", p=128),
                in_=xq.rearrange("p (c t) -> p c t", c=NCH)), inc=(sW, 16))
        if "s0" in dbg:
            p.op("gpsimd", lambda e: e.dma_start(out=dbg["s0"], in_=s_sb[:]), inc=(sW, 16))
        if "h0" in dbg:
            p.op("gpsimd", lambda e: e.dma_start(out=dbg["h0"], in_=hbuf[:]), inc=(sW, 16))
        if "w1q" in dbg:
            p.op("gpsimd", lambda e: e.dma_start(out=dbg["w1q"], in_=w1q), inc=(sW, 16))
        if "w2q" in dbg:
            p.op("gpsimd", lambda e: e.dma_start(out=dbg["w2q"], in_=w2q), inc=(sW, 16))
        if "mrow" in dbg:
            p.op("gpsimd", lambda e: e.dma_start(out=dbg["mrow"], in_=inv_bc[0:1, :]), inc=(sW, 16))
        if "woacc" in dbg:
            p.op("gpsimd", lambda e: e.dma_start(out=dbg["woacc"], in_=xstg[:]), inc=(sW, 16))
        if "bprime" in dbg:
            p.op("gpsimd", lambda e: e.dma_start(out=dbg["bprime"], in_=bprime[:]), inc=(sW, 16))
        p.wait("sync", sW)
        p.wait("sync", sX)

        p.finalize(block)
    return nc


def gen_wmean(p, banks, L, done):
    (xstg, wcols, wtot_bf, winv, wscal, swcol, W, ones_bf, ones_f) = (
        L["xstg"], L["wcols"], L["wtot_bf"], L["winv"], L["wscal"], L["swcol"], L["W"],
        L["ones_bf"], L["ones_f"])
    sW, sX, sPE, sDVE, sACT = L["sW"], L["sX"], L["sPE"], L["sDVE"], L["sACT"]
    p.wait("sync", sPE, L["qkv_mm_done"])
    sWM = [p.sem("sWM0"), p.sem("sWM1")]
    stg_free = {0: (sDVE, 0), 1: (sDVE, 0)}
    for wi, wname in enumerate(("w1T", "w2T")):
        for i in range(32):
            par = i % 2
            sem0, cnt0 = stg_free[par]
            p.wait("sync", sem0, cnt0)
            if wi == 0:
                srcap = W[wname][(i // 4) * 128:(i // 4 + 1) * 128,
                                 (i % 4) * 1024:(i % 4) * 1024 + 1024]
            else:
                srcap = W[wname][i * 128:(i + 1) * 128, :]
            for h0 in (0, 64):
                p.op("sync", lambda e, srcap=srcap, par=par, h0=h0: e.dma_start(
                    out=xstg[h0:h0 + 64, par * 1024: par * 1024 + 1024],
                    in_=srcap[h0:h0 + 64, :]), inc=(sWM[par], 16))
            p.wait("vector", sWM[par])
            p.op("vector", lambda e, i=i, par=par: e.tensor_reduce(
                out=wcols[:, i:i + 1], in_=xstg[:, par * 1024: par * 1024 + 1024],
                axis=AX, op=ALU.add, apply_absolute_value=True), inc=(sDVE, 1))
            stg_free[par] = (sDVE, sDVE.n)
            yield
        p.wait("vector", sDVE)
        p.op("vector", lambda e: e.tensor_reduce(
            out=wcols[:, 32:33], in_=wcols[:, 0:32], axis=AX, op=ALU.add), inc=(sDVE, 1))
        p.wait("vector", sDVE)
        p.op("vector", lambda e: e.tensor_copy(out=wtot_bf[:], in_=wcols[:, 32:33]),
             inc=(sDVE, 1))
        p.wait("tensor", sDVE)
        b0 = banks.acquire(0)
        p.op("tensor", lambda e, b0=b0: e.matmul(b0[0:1, 0:1], wtot_bf[:], ones_bf[:, 0:1],
                                                 start=True, stop=True), inc=(sPE, 1))
        p.wait("vector", sPE)
        p.op("vector", lambda e, b0=b0, wi=wi: e.tensor_scalar(
            out=winv[:, wi:wi + 1], in0=b0[0:1, 0:1], scalar1=1.0 / (4096.0 * 1024.0),
            scalar2=1e-5, op0=ALU.mult, op1=ALU.max), inc=(sDVE, 1))
        banks.release(0, sDVE)
        p.wait("vector", sDVE)
        p.op("vector", lambda e, wi=wi: e.reciprocal(out=wscal[:, wi:wi + 1],
                                                     in_=winv[:, wi:wi + 1]), inc=(sDVE, 1))
        p.wait("tensor", sDVE)
        b0 = banks.acquire(0)
        p.op("tensor", lambda e, b0=b0, wi=wi: e.matmul(
            b0[:, 0:1], ones_f[0:1, :], wscal[:, wi:wi + 1], start=True, stop=True), inc=(sPE, 1))
        p.wait("scalar", sPE)
        p.op("scalar", lambda e, b0=b0, wi=wi: e.copy(out=swcol[:, wi:wi + 1], in_=b0[:, 0:1]),
             inc=(sACT, 1))
        banks.release(0, sACT)
    done["act"] = sACT.n
    yield


def emit_ffn(p, banks, L):
    (xn, xq, s_sb, w1q, w2q, hbuf, inv_bc, sc_bc, sc2_bc, inv2_bc, mfold, ubuf, ystg,
     xstg, bias_sb, ones_bf, ones_f, ident_bf, wcols, wtot_bf, winv, wscal, swcol,
     mcol, mcol_bf, W, outT, sW, sX, sPE, sDVE, sACT, sPOOL, scr) = (
        L["xn"], L["xq"], L["s_sb"], L["w1q"], L["w2q"], L["hbuf"], L["inv_bc"], L["sc_bc"],
        L["sc2_bc"], L["inv2_bc"], L["mfold"], L["ubuf"], L["ystg"], L["xstg"], L["bias_sb"],
        L["ones_bf"], L["ones_f"], L["ident_bf"], L["wcols"], L["wtot_bf"], L["winv"],
        L["wscal"], L["swcol"], L["mcol"], L["mcol_bf"], L["W"], L["outT"], L["sW"], L["sX"],
        L["sPE"], L["sDVE"], L["sACT"], L["sPOOL"], L["scr"])
    out_init = L["out_init"]
    attn_accum_done = L["attn_accum_done"]
    pe_now = sPE.n
    mbf = L["arena2"][:, 2048:4096]

    wmean_done = L["wmean_done"]

    p.wait("vector", sPE, pe_now)
    p.wait("scalar", sPE, pe_now)
    p.wait("scalar", sDVE)
    p.wait("vector", sACT)
    absf = L["sc_bc"]
    p.op("scalar", lambda e: e.activation(out=mfold[:], in_=xn[:, cs(0)], func=AF.Abs),
         inc=(sACT, 1))
    for c in range(1, NCH):
        p.op("scalar", lambda e, c=c: e.activation(out=absf[:], in_=xn[:, cs(c)], func=AF.Abs),
             inc=(sACT, 1))
        p.wait("vector", sACT)
        p.op("vector", lambda e: e.tensor_max(out=mfold[:], in0=mfold[:], in1=absf[:]),
             inc=(sDVE, 1))
        p.wait("scalar", sDVE)
    p.op("vector", lambda e: e.tensor_copy(out=mbf[:], in_=mfold[:]), inc=(sDVE, 1))
    mrow = ystg[0:1, :]
    invrow = mfold[0:1, :]
    p.wait("tensor", sDVE)
    for i in range(16):
        b = banks.acquire(i % 2)
        p.op("tensor", lambda e, b=b, i=i: e.matmul(b[:, 0:128], mbf[:, i * 128:(i + 1) * 128],
                                                    ident_bf[:], start=True, stop=True),
             inc=(sPE, 1))
        p.wait("vector", sPE)
        p.op("vector", lambda e, b=b: e.reduce_max(out=mcol[:], in_=b[:, 0:128], axis=AX),
             inc=(sDVE, 1))
        banks.release(i % 2, sDVE)
        p.wait("vector", sDVE)
        p.op("vector", lambda e: e.tensor_copy(out=mcol_bf[:], in_=mcol[:]), inc=(sDVE, 1))
        p.wait("tensor", sDVE)
        b2 = banks.acquire(2 + i % 2)
        p.op("tensor", lambda e, b2=b2: e.matmul(b2[0:1, 0:128], mcol_bf[:], ident_bf[:],
                                                 start=True, stop=True), inc=(sPE, 1))
        p.wait("scalar", sPE)
        p.op("scalar", lambda e, b2=b2, i=i: e.copy(out=mrow[:, i * 128:(i + 1) * 128],
                                                    in_=b2[0:1, 0:128]), inc=(sACT, 1))
        banks.release(2 + i % 2, sACT)
    p.wait("vector", sACT)
    p.op("vector", lambda e: e.tensor_scalar(out=mrow, in0=mrow, scalar1=1e-5, scalar2=None,
                                             op0=ALU.max), inc=(sDVE, 1))
    p.wait("vector", sDVE)
    p.op("vector", lambda e: e.tensor_scalar(out=invrow, in0=mrow, scalar1=1.0 / 127.0,
                                             scalar2=winv[0:1, 0:1], op0=ALU.mult, op1=ALU.mult),
         inc=(sDVE, 1))
    p.op("vector", lambda e: e.reciprocal(out=mrow, in_=mrow), inc=(sDVE, 1))
    p.wait("vector", sDVE)
    p.op("vector", lambda e: e.tensor_scalar(out=mrow, in0=mrow, scalar1=127.0, scalar2=None,
                                             op0=ALU.mult), inc=(sDVE, 1))
    p.wait("tensor", sDVE)
    p.wait("tensor", sACT, wmean_done)
    for i in range(4):
        b = banks.acquire(4 + i % 2)
        p.op("tensor", lambda e, b=b, i=i: e.matmul(b[:, :], ones_f[0:1, :],
                                                    mrow[:, i * 512:(i + 1) * 512],
                                                    start=True, stop=True))
        b2 = banks.acquire(6 + i % 2)
        p.op("tensor", lambda e, b2=b2, i=i: e.matmul(b2[:, :], ones_f[0:1, :],
                                                      invrow[:, i * 512:(i + 1) * 512],
                                                      start=True, stop=True), inc=(sPE, 1))
        p.wait("scalar", sPE)
        p.op("scalar", lambda e, b=b, i=i: e.copy(out=sc_bc[:, i * 512:(i + 1) * 512],
                                                  in_=b[:, :]))
        p.op("scalar", lambda e, b2=b2, i=i: e.copy(out=inv_bc[:, i * 512:(i + 1) * 512],
                                                    in_=b2[:, :]), inc=(sACT, 1))
        banks.release(4 + i % 2, sACT)
        banks.release(6 + i % 2, sACT)
    p.wait("vector", sACT)
    qscr = [mfold, ystg]
    for c in range(NCH):
        par = c % 2
        p.op("vector", lambda e, c=c, par=par: e.tensor_mul(out=qscr[par][:, 0:2048],
                                                            in0=xn[:, cs(c)], in1=sc_bc[:]))
        p.op("vector", lambda e, c=c, par=par: e.tensor_scalar(out=xq[:, cs(c)],
                                                               in0=qscr[par][:, 0:2048],
                                                               scalar1=MAGIC, scalar2=MAGIC,
                                                               op0=ALU.add, op1=ALU.subtract),
             inc=(sDVE, 1))
    xq_done = sDVE.n

    ystg_reader = 0
    w1q_reader = {0: 0, 1: 0}
    clip_done = {0: xq_done, 1: xq_done}
    mq_free = {0: 0, 1: 0}
    stg_free = {0: (sDVE, 0), 1: (sDVE, 0)}
    sST = [p.sem("sST0"), p.sem("sST1")]
    sST2 = [p.sem("sST2a"), p.sem("sST2b")]
    mq = [mfold[:, 1024:2048], ystg[:, 1024:2048]]
    absq = [sc_bc[:, 0:1024], sc_bc[:, 1024:2048]]
    runmax = ystg[:, 0:1024]
    w1r = W["w1T"].rearrange("(c p) o -> p c o", p=128)
    w2r = W["w2T"].rearrange("(c p) o -> p c o", p=128)
    cw = bias_sb["convw"]
    first_issue = True

    for half in range(2):
        tok0 = 0 if half == 0 else 1023
        out0 = half * 1024
        um0 = 1 - half
        p.wait("tensor", sDVE, xq_done)
        p.wait("sync", sDVE)
        p.wait("scalar", sW)
        p.wait("vector", sW)
        stg_free = {0: (sACT, sACT.n), 1: (sACT, sACT.n)}

        def w1_issue(oc):
            nonlocal first_issue
            par = oc % 2
            sem0, cnt0 = stg_free[par]
            p.wait("sync", sem0, cnt0)
            if first_issue:
                p.wait("sync", sW, attn_accum_done)
                first_issue = False
            stg0 = 2048 + par * 1024
            for k in range(2):
                p.op("sync", lambda e, oc=oc, stg0=stg0, k=k: e.dma_start(
                    out=xstg[:, stg0 + k * 512: stg0 + (k + 1) * 512].rearrange(
                        "p (c o) -> p c o", c=4),
                    in_=w1r[:, k * 4:(k + 1) * 4, oc * 128:(oc + 1) * 128]),
                    inc=(sST[par], 16))

        def w1_quant(oc):
            par = oc % 2
            stg0 = 2048 + par * 1024
            p.wait("scalar", sST[par])
            p.wait("scalar", sACT, wmean_done)
            p.wait("scalar", sDVE, mq_free[par])
            p.op("scalar", lambda e, stg0=stg0, par=par: e.activation(
                out=mq[par][:], in_=xstg[:, stg0:stg0 + 1024], func=AF.Identity,
                scale=swcol[:, 0:1]), inc=(sACT, 1))
            stg_free[par] = (sACT, sACT.n)
            p.wait("vector", sACT)
            p.wait("vector", sPE, w1q_reader[par])
            p.op("vector", lambda e, par=par: e.tensor_scalar(
                out=mq[par][:], in0=mq[par][:], scalar1=1.0, scalar2=-1.0,
                op0=ALU.min, op1=ALU.max))
            p.op("vector", lambda e, par=par: e.tensor_scalar(
                out=w1q[:, par * 1024:(par + 1) * 1024], in0=mq[par][:],
                scalar1=MAGIC, scalar2=MAGIC, op0=ALU.add, op1=ALU.subtract), inc=(sDVE, 1))
            clip_done[par] = sDVE.n
            mq_free[par] = sDVE.n

        def h_mm(oc):
            par = oc % 2
            p.wait("tensor", sDVE, clip_done[par])
            segs = []
            for si, (s0, swd) in enumerate(((0, 512), (512, 512), (1024, 1))):
                bi = (oc * 3 + si) % 4
                b = banks.acquire(bi)
                for d in range(NCH):
                    p.op("tensor", lambda e, b=b, d=d, s0=s0, swd=swd, par=par, tok0=tok0:
                         e.matmul(b[:, 0:swd],
                                  w1q[:, par * 1024 + d * 128: par * 1024 + d * 128 + 128],
                                  xq[:, cs(d, tok0 + s0, swd)],
                                  start=(d == 0), stop=(d == NCH - 1)),
                         inc=(sPE, 1) if d == NCH - 1 else None)
                segs.append((bi, b, s0, swd, sPE.n))
            w1q_reader[par] = sPE.n
            return segs

        hb = L["bwtail"][:, 6144:7170]

        def flush_max():
            while pend_max:
                oc_, par_, cnt_ = pend_max.pop(0)
                p.wait("vector", sACT, cnt_)
                if oc_ == 0:
                    p.op("vector", lambda e, par_=par_: e.tensor_copy(
                        out=runmax[:], in_=absq[par_][:]), inc=(sDVE, 1))
                else:
                    p.op("vector", lambda e, par_=par_: e.tensor_max(
                        out=runmax[:], in0=runmax[:], in1=absq[par_][:]), inc=(sDVE, 1))
                max_read[par_] = sDVE.n

        def flush_silu():
            while pend_silu:
                oc_, cnt_ = pend_silu.pop(0)
                par_ = oc_ % 2
                p.wait("scalar", sDVE, cnt_)
                p.op("scalar", lambda e, oc_=oc_: e.activation(
                    out=s_sb[:, oc_ * 1024:(oc_ + 1) * 1024], in_=ubuf[:], func=AF.Silu,
                    bias=bias_sb["convb"][:, oc_:oc_ + 1]), inc=(sACT, 1))
                silu_done[0] = sACT.n
                p.wait("scalar", sDVE, max_read[par_])
                p.op("scalar", lambda e, oc_=oc_, par_=par_: e.activation(
                    out=absq[par_][:], in_=s_sb[:, oc_ * 1024:(oc_ + 1) * 1024], func=AF.Abs),
                    inc=(sACT, 1))
                pend_max.append((oc_, par_, sACT.n))

        def h_post(oc, segs):
            flush_max()
            for bi, b, s0, swd, pn in segs:
                p.wait("vector", sPE, pn)
                p.op("vector", lambda e, b=b, s0=s0, swd=swd, tok0=tok0: e.tensor_mul(
                    out=hb[:, um0 + s0: um0 + s0 + swd], in0=b[:, 0:swd],
                    in1=inv_bc[:, tok0 + s0: tok0 + s0 + swd]), inc=(sDVE, 1))
                banks.release(bi, sDVE)
            flush_silu()
            p.wait("vector", sACT, silu_done[0])
            p.op("vector", lambda e, oc=oc: e.tensor_scalar(
                out=ubuf[:, 0:1024], in0=hb[:, 0:1024],
                scalar1=cw[:, oc * 3: oc * 3 + 1], scalar2=None, op0=ALU.mult))
            p.op("vector", lambda e, oc=oc: e.scalar_tensor_tensor(
                out=ubuf[:, 0:1024], in0=hb[:, 1:1025],
                scalar=cw[:, oc * 3 + 1: oc * 3 + 2], in1=ubuf[:, 0:1024],
                op0=ALU.mult, op1=ALU.add))
            p.op("vector", lambda e, oc=oc: e.scalar_tensor_tensor(
                out=ubuf[:, 0:1024], in0=hb[:, 2:1026],
                scalar=cw[:, oc * 3 + 2: oc * 3 + 3], in1=ubuf[:, 0:1024],
                op0=ALU.mult, op1=ALU.add), inc=(sDVE, 1))
            pend_silu.append((oc, sDVE.n))

        pend_max = []
        pend_silu = []
        max_read = {0: 0, 1: 0}
        silu_done = [0]
        padc = 0 if half == 0 else 1025
        p.op("vector", lambda e, padc=padc: e.memset(hb[:, padc:padc + 1], 0.0),
             inc=(sDVE, 1))
        w1_issue(0)
        w1_issue(1)
        w1_quant(0)
        for oc in range(32):
            segs = h_mm(oc)
            if oc + 2 < 32:
                w1_issue(oc + 2)
            if oc + 1 < 32:
                w1_quant(oc + 1)
            h_post(oc, segs)
        flush_silu()
        flush_max()
        s_done = sACT.n

        m2row = ystg[0:1, 0:1024]
        inv2row = mfold[0:1, 0:1024]
        p.wait("vector", sACT, s_done)
        p.wait("vector", sPE)
        p.op("vector", lambda e: e.tensor_copy(out=mbf[:, 0:1024], in_=runmax[:]),
             inc=(sDVE, 1))
        p.wait("tensor", sDVE)
        for i in range(8):
            b = banks.acquire(i % 2)
            p.op("tensor", lambda e, b=b, i=i: e.matmul(b[:, 0:128], mbf[:, i * 128:(i + 1) * 128],
                                                        ident_bf[:], start=True, stop=True),
                 inc=(sPE, 1))
            p.wait("vector", sPE)
            p.op("vector", lambda e, b=b: e.reduce_max(out=mcol[:], in_=b[:, 0:128], axis=AX),
                 inc=(sDVE, 1))
            banks.release(i % 2, sDVE)
            p.wait("vector", sDVE)
            p.op("vector", lambda e: e.tensor_copy(out=mcol_bf[:], in_=mcol[:]), inc=(sDVE, 1))
            p.wait("tensor", sDVE)
            b2 = banks.acquire(2 + i % 2)
            p.op("tensor", lambda e, b2=b2: e.matmul(b2[0:1, 0:128], mcol_bf[:], ident_bf[:],
                                                     start=True, stop=True), inc=(sPE, 1))
            p.wait("scalar", sPE)
            p.op("scalar", lambda e, b2=b2, i=i: e.copy(out=m2row[:, i * 128:(i + 1) * 128],
                                                        in_=b2[0:1, 0:128]), inc=(sACT, 1))
            banks.release(2 + i % 2, sACT)
        p.wait("vector", sACT)
        p.op("vector", lambda e: e.tensor_scalar(out=m2row, in0=m2row, scalar1=1e-5, scalar2=None,
                                                 op0=ALU.max), inc=(sDVE, 1))
        p.wait("vector", sDVE)
        p.op("vector", lambda e: e.tensor_scalar(out=inv2row, in0=m2row, scalar1=1.0 / 127.0,
                                                 scalar2=winv[0:1, 1:2], op0=ALU.mult,
                                                 op1=ALU.mult), inc=(sDVE, 1))
        p.op("vector", lambda e: e.reciprocal(out=m2row, in_=m2row), inc=(sDVE, 1))
        p.wait("vector", sDVE)
        p.op("vector", lambda e: e.tensor_scalar(out=m2row, in0=m2row, scalar1=127.0,
                                                 scalar2=None, op0=ALU.mult), inc=(sDVE, 1))
        p.wait("tensor", sDVE)
        for i in range(2):
            b = banks.acquire(4 + i)
            p.op("tensor", lambda e, b=b, i=i: e.matmul(b[:, :], ones_f[0:1, :],
                                                        m2row[:, i * 512:(i + 1) * 512],
                                                        start=True, stop=True))
            b2 = banks.acquire(6 + i)
            p.op("tensor", lambda e, b2=b2, i=i: e.matmul(b2[:, :], ones_f[0:1, :],
                                                          inv2row[:, i * 512:(i + 1) * 512],
                                                          start=True, stop=True), inc=(sPE, 1))
            p.wait("scalar", sPE)
            p.op("scalar", lambda e, b=b, i=i: e.copy(out=sc2_bc[:, i * 512:(i + 1) * 512],
                                                      in_=b[:, :]))
            p.op("scalar", lambda e, b2=b2, i=i: e.copy(out=inv2_bc[:, i * 512:(i + 1) * 512],
                                                        in_=b2[:, :]), inc=(sACT, 1))
            banks.release(4 + i, sACT)
            banks.release(6 + i, sACT)
        qs2 = [mfold[:, 0:1024], ystg[:, 0:1024]]
        w2qp = [w2q[:, 0:2048], w2q[:, 2048:4096]]
        wpart_reader = {0: 0, 1: 0}
        wstg2_free = {0: (sACT, sACT.n), 1: (sACT, sACT.n)}
        quant2_done = {0: 0, 1: 0}
        qscr_free = sDVE.n

        def w2_issue(oc2, part):
            sem0, cnt0 = wstg2_free[part]
            p.wait("sync", sem0, cnt0)
            for k in range(4):
                p.op("sync", lambda e, oc2=oc2, part=part, k=k: e.dma_start(
                    out=xstg[:, part * 2048 + k * 512: part * 2048 + (k + 1) * 512].rearrange(
                        "p (c o) -> p c o", c=4),
                    in_=w2r[:, part * 16 + k * 4: part * 16 + (k + 1) * 4,
                            oc2 * 128:(oc2 + 1) * 128]),
                    inc=(sST2[part], 16))

        def w2_quant(oc2, part):
            nonlocal qscr_free
            p.wait("scalar", sST2[part])
            p.wait("scalar", sDVE, qscr_free)
            for g in range(2):
                p.op("scalar", lambda e, g=g, part=part: e.activation(
                    out=qs2[g], in_=xstg[:, part * 2048 + g * 1024: part * 2048 + (g + 1) * 1024],
                    func=AF.Identity, scale=swcol[:, 1:2]), inc=(sACT, 1))
            wstg2_free[part] = (sACT, sACT.n)
            p.wait("vector", sACT)
            p.wait("vector", sPE, wpart_reader[part])
            for g in range(2):
                p.op("vector", lambda e, g=g, part=part: e.tensor_scalar(
                    out=qs2[g], in0=qs2[g], scalar1=1.0, scalar2=-1.0,
                    op0=ALU.min, op1=ALU.max))
                p.op("vector", lambda e, g=g, part=part: e.tensor_scalar(
                    out=w2qp[part][:, g * 1024:(g + 1) * 1024], in0=qs2[g],
                    scalar1=MAGIC, scalar2=MAGIC, op0=ALU.add, op1=ALU.subtract),
                    inc=(sDVE, 1))
            quant2_done[part] = sDVE.n
            qscr_free = sDVE.n
            if oc2 + 1 < NCH:
                w2_issue(oc2 + 1, part)

        bnk2 = {}

        def w2_mm(oc2, part):
            p.wait("tensor", sDVE, quant2_done[part])
            done = []
            for ts2 in range(2):
                bi = (oc2 * 2 + ts2) % 4
                if part == 0:
                    bnk2[ts2] = (bi, banks.acquire(bi))
                b = bnk2[ts2][1]
                for dd in range(16):
                    d32 = part * 16 + dd
                    p.op("tensor", lambda e, b=b, d32=d32, dd=dd, ts2=ts2, part=part: e.matmul(
                        b[:, :], w2qp[part][:, dd * 128: dd * 128 + 128],
                        s_sb[:, d32 * 1024 + ts2 * 512: d32 * 1024 + ts2 * 512 + 512],
                        start=(part == 0 and dd == 0), stop=(part == 1 and dd == 15)),
                        inc=(sPE, 1) if dd == 15 else None)
                done.append(sPE.n)
            wpart_reader[part] = sPE.n
            return done

        def w2_post(oc2, done):
            nonlocal ystg_reader
            for ts2 in range(2):
                bi, b = bnk2[ts2]
                p.wait("vector", sPE, done[ts2])
                p.wait("vector", sW, ystg_reader)
                p.op("vector", lambda e, b=b, ts2=ts2: e.tensor_mul(
                    out=ystg[:, ts2 * 512 + 1024: ts2 * 512 + 1536],
                    in0=b[:, :], in1=inv2_bc[:, ts2 * 512:(ts2 + 1) * 512]), inc=(sDVE, 1))
                banks.release(bi, sDVE)
            p.wait("gpsimd", sDVE)
            p.wait("gpsimd", sW)
            if "sOI" in L:
                p.wait("gpsimd", L["sOI"], L["out_init"])
            p.op("gpsimd", lambda e, oc2=oc2, out0=out0: e.dma_start(
                out=outT.rearrange("(c p) t -> p c t", p=128)[:, oc2, out0:out0 + 1024],
                in_=ystg[:, 1024:2048], accum_op=ALU.add), inc=(sW, 16))
            ystg_reader = sW.n

        p.wait("scalar", sPE)
        w2_issue(0, 0)
        w2_issue(0, 1)
        w2_quant(0, 0)
        w2_quant(0, 1)

        p.wait("vector", sACT)
        sq_mid = 0
        for c in range(32):
            par = c % 2
            p.op("vector", lambda e, c=c, par=par: e.tensor_mul(out=qs2[par],
                                                                in0=s_sb[:, c * 1024:(c + 1) * 1024],
                                                                in1=sc2_bc[:]))
            p.op("vector", lambda e, c=c, par=par: e.tensor_scalar(
                out=s_sb[:, c * 1024:(c + 1) * 1024], in0=qs2[par],
                scalar1=MAGIC, scalar2=MAGIC, op0=ALU.add, op1=ALU.subtract), inc=(sDVE, 1))
            if c == 15:
                sq_mid = sDVE.n
        sq_done = sDVE.n
        qscr_free = sq_done

        for oc2 in range(NCH):
            if oc2 == 0:
                p.wait("tensor", sDVE, sq_mid)
            done0 = w2_mm(oc2, 0)
            if oc2 == 0:
                p.wait("tensor", sDVE, sq_done)
            else:
                w2_quant(oc2, 1)
            done1 = w2_mm(oc2, 1)
            if oc2 + 1 < NCH:
                w2_quant(oc2 + 1, 0)
            w2_post(oc2, done1)




def build_fused(debug=()):
    nc = bass.Bass("TRN2", target_bir_lowering=False)
    xT = nc.dram_tensor("xT", [D, S], F32, kind="ExternalInput").ap()
    projT = nc.dram_tensor("projT", [NH, D, HD], F32, kind="ExternalInput").ap()
    metric = nc.dram_tensor("metric", [NH, HD, HD], F32, kind="ExternalInput").ap()
    mixer_wT = nc.dram_tensor("mixer_wT", [D, D], F32, kind="ExternalInput").ap()
    mixer_b = nc.dram_tensor("mixer_b", [128, NCH], F32, kind="ExternalInput").ap()
    gate_wT = nc.dram_tensor("gate_wT", [D, 8], F32, kind="ExternalInput").ap()
    gate_b = nc.dram_tensor("gate_b", [1, 8], F32, kind="ExternalInput").ap()
    expert_wT = nc.dram_tensor("expert_wT", [8, D, D], F32, kind="ExternalInput").ap()
    expert_b = nc.dram_tensor("expert_b", [8, D], F32, kind="ExternalInput").ap()
    W = {}
    for name, shape in [("wq", [D, 256]), ("bq", [128, 2]), ("wqc", [256, 896]), ("bqc", [128, 7]),
                        ("wqr", [D, 128]), ("bqr", [128, 1]), ("wkv", [D, 256]), ("bkv", [128, 2]),
                        ("wkr", [D, 128]), ("bkr", [128, 1]), ("wkc", [256, 896]), ("bkc", [128, 7]),
                        ("wv", [256, D]), ("bv", [128, 8]), ("wo", [D, D]), ("bo", [128, 8]),
                        ("w1T", [D, 4096]), ("convw", [128, 96]), ("convb", [128, 32]),
                        ("w2T", [4096, D])]:
        W[name] = nc.dram_tensor(name, shape, F32, kind="ExternalInput").ap()
    outT = nc.dram_tensor("outT", [D, S], F32, kind="ExternalOutput").ap()
    dbg = {}
    for name in debug:
        shp = {"x1": [D, S], "xn": [D, S]}[name]
        dbg[name] = nc.dram_tensor("dbg_" + name, shp, F32, kind="ExternalOutput").ap()

    with ExitStack() as es:
        sb = lambda name, shape, dt=F32, **kw: es.enter_context(nc.sbuf_tensor(name, shape, dt, **kw))
        A = sb("A", [128, 49408], BF16)
        B = sb("B", [128, 16384], F32)
        xn = sb("xn", [128, NCH * S], BF16)
        Dd = sb("Dd", [128, 6144], BF16)
        x = B[:]
        pfm = A[:, 0:16384]
        wbuf = A[:, 16384:32768]
        atail = 32768
        ptokb = Dd[:, 0:2048]
        gacc4 = A[:, atail + 2048: atail + 6144].bitcast(F32)
        gacc = gacc4[0:64, 0:1024]
        gbf = A[0:64, atail + 6144: atail + 7168]
        metric_sb = A[0:64, atail + 7168: atail + 8192]
        mg = A[:, atail + 8192: atail + 8704]
        g_sb = A[0:8, atail + 8704: atail + 10752]
        g_row = A[0:1, atail + 10752: atail + 12800]
        gbc = A[:, atail + 12800: atail + 13824]
        xne = A[:, atail + 13824: atail + 13824 + 4096] if False else None
        xne2 = [Dd[:, 2048: 2048 + 4096], None]
        eb_bf = A[0:8, atail + 13824: atail + 14848]
        x2s = A[:, 1024:1536]
        scr = A[:, 0:1024].bitcast(F32)
        rows = A[:, 3584:6656].bitcast(F32)
        rowm = rows[0:1, 0:512]
        rowe = rows[0:1, 512:1024]
        rowt = rows[0:1, 1024:1536]
        ab = A[:, 1536:3584].bitcast(F32)
        l0small_f = A[:, atail + 14848: atail + 14912].bitcast(F32)
        mixb_sb = l0small_f[:, 0:8]
        gtok = l0small_f[:, 8:16]
        gtok2 = l0small_f[:, 16:24]
        gatew_bf = A[:, atail + 14912: atail + 14976]
        gb_bf = A[0:1, atail + 14976: atail + 14984]
        gtok_bf = A[:, atail + 14984: atail + 14992]
        col1 = sb("col1", [128, 1], F32)
        col2 = sb("col2", [128, 1], F32)
        col3 = sb("col3", [128, 1], F32)
        ones_bf = sb("ones_bf", [128, 128], BF16)
        ones_f = sb("ones_f", [1, 128], F32)
        ident_bf = sb("ident_bf", [128, 128], BF16)
        tri_bf = sb("tri_bf", [128, 128], BF16)
        hpi = sb("hpi", [64, 1], F32)
        invf = sb("invf", [64, 1], F32)
        Q = A[:, 0:16384]
        Kb = A[:, 16384:32768]
        Vb = A[:, 32768:49408]
        xq = A[:, 0:16384]
        s_sb = A[:, 16384:49152]
        w1q = Dd[:, 0:2048]
        w2q = Dd[:, 2048:6144]
        woS = [Dd[:, 0:1024], Dd[:, 1024:2048]]
        arena3 = B[:, 0:8192]
        cosT = arena3[:, 0:2048]
        sinT = arena3[:, 2048:4096]
        awbuf = arena3[:, 4096:8192].bitcast(BF16)
        pwq = awbuf[:, 0:2048]
        pwkv = awbuf[:, 2048:4096]
        pwqc = awbuf[:, 4096:5888]
        pwkc = awbuf[:, 5888:7680]
        qrS = arena3[:, 4096:6144]
        rotS = arena3[:, 6144:8192]
        attn_out = arena3[:].bitcast(BF16)
        inv_bc = arena3[:, 0:2048]
        sc_bc = arena3[:, 2048:4096]
        sc2_bc = arena3[:, 2048:3072]
        inv2_bc = arena3[:, 3072:4096]
        mfold = arena3[:, 4096:6144]
        ubuf = arena3[:, 4096:4608].bitcast(BF16)
        ystg = arena3[:, 6144:8192]
        xstg = B[:, 8192:12288]
        xstg_bf = B[:, 8192:12288].bitcast(BF16)
        cq = xstg_bf[:, 0:4096]
        ckv = xstg_bf[:, 4096:8192]
        bwtail = B[:, 12288:16384].bitcast(BF16)
        wqr_sb = bwtail[:, 0:1024]
        wkr_sb = bwtail[:, 1024:2048]
        wv_sb = bwtail[:, 2048:4096]
        ptb = bwtail[:, 4096:8192]
        hbuf = bwtail[:, 6144:7169]
        rrowA = xstg[0:1, 3584:4096]
        rrowB = xstg[0:1, 2560:3072]
        rstgA = xstg[0:64, 2560:3072]
        rstgB = xstg[0:64, 3072:3584]
        scrA_A = xstg[0:64, 2048:2304].bitcast(BF16)
        scrA_B = xstg[0:64, 2304:2560].bitcast(BF16)
        bias_all = sb("bias_all", [128, 164], F32)
        bias_sb = {}
        _off = 0
        for name, n in (("bq", 2), ("bqc", 7), ("bqr", 1), ("bkv", 2), ("bkr", 1), ("bkc", 7),
                        ("bv", 8), ("bo", 8), ("convw", 96), ("convb", 32)):
            bias_sb[name] = bias_all[:, _off:_off + n]
            _off += n
        bv_bf = sb("bv_bf", [128, 8], BF16)
        bprime = sb("bprime", [128, 8], F32)
        qmaxc = sb("qmaxc", [128, 9], F32)
        qkmax = sb("qkmax", [128, 2], BF16)
        negb = sb("negb", [128, 1], F32)
        brow = sb("brow", [1, 4], F32)
        wcols = sb("wcols", [128, 33], F32)
        wtot_bf = sb("wtot_bf", [128, 1], BF16)
        winv = sb("winv", [1, 2], F32)
        wscal = sb("wscal", [1, 2], F32)
        swcol = sb("swcol", [128, 2], F32)
        mcol = sb("mcol", [128, 1], F32)
        mcol_bf = sb("mcol_bf", [128, 1], BF16)
        pbD0 = es.enter_context(nc.psum_tensor("pbD0", [128, 1024], F32))
        pbD1 = es.enter_context(nc.psum_tensor("pbD1", [128, 1024], F32))
        pb = [pbD0[:, 0:512], pbD0[:, 512:1024], pbD1[:, 0:512], pbD1[:, 512:1024]] + [
            es.enter_context(nc.psum_tensor(f"pb{i}", [128, 512], F32))[:] for i in range(4, 8)]
        pbD = [pbD0, pbD1]

        p = Prog(nc, es)
        sW = p.sem("sW")
        sX = p.sem("sX")
        sPE = p.sem("sPE")
        sDVE = p.sem("sDVE")
        sACT = p.sem("sACT")
        sPOOL = p.sem("sPOOL")
        sWE = [p.sem("sWE0"), p.sem("sWE1")]
        sB = p.sem("sB")
        sOI = p.sem("sOI")
        banks = Banks(p, pb)
        block = es.enter_context(nc.Block())

        p.op("gpsimd", lambda e: e.memset(ones_f[:], 1.0), inc=(sPOOL, 1))
        p.op("gpsimd", lambda e: e.memset(ones_bf[:], 1.0), inc=(sPOOL, 1))
        p.op("gpsimd", lambda e: e.memset(ident_bf[:], 1.0), inc=(sPOOL, 1))
        p.op("gpsimd", lambda e: e.affine_select(ident_bf[:], ident_bf[:], pattern=[[-1, 128]],
                                                 compare_op=ALU.is_equal, fill=0.0, base=0,
                                                 channel_multiplier=1), inc=(sPOOL, 1))
        p.op("gpsimd", lambda e: e.memset(tri_bf[:], 1.0), inc=(sPOOL, 1))
        p.op("gpsimd", lambda e: e.affine_select(tri_bf[:], tri_bf[:], pattern=[[1, 128]],
                                                 compare_op=ALU.is_ge, fill=0.0, base=0,
                                                 channel_multiplier=-1), inc=(sPOOL, 1))
        p.op("gpsimd", lambda e: e.memset(hpi[:], math.pi / 2), inc=(sPOOL, 1))
        p.op("gpsimd", lambda e: e.iota(invf[:], pattern=[[0, 1]], base=0, channel_multiplier=1,
                                        allow_small_or_imprecise_dtypes=True), inc=(sPOOL, 1))
        p.op("gpsimd", lambda e: e.memset(gacc[:], 0.0), inc=(sPOOL, 1))
        consts_done = sPOOL.n

        p.op("sync", lambda e: e.dma_start(
            out=x.rearrange("p (c t) -> p c t", c=NCH),
            in_=xT.rearrange("(c p) t -> p c t", p=128)), inc=(sX, 16))
        p.op("sync", lambda e: e.dma_start(out=mixb_sb[:], in_=mixer_b), inc=(sX, 16))
        x_loaded = sX.n
        for c in range(NCH):
            p.op("gpsimd", lambda e, c=c: e.dma_start(
                out=wbuf[:, c * 1024:(c + 1) * 1024].rearrange("p (h k) -> p h k", h=NH),
                in_=projT[:, c * 128:(c + 1) * 128, :].rearrange("h p k -> p h k")), inc=(sW, 16))
        p.op("gpsimd", lambda e: e.dma_start(
            out=metric_sb.rearrange("m (h l) -> m h l", h=NH),
            in_=metric.rearrange("h m l -> m h l")), inc=(sW, 16))
        p.op("gpsimd", lambda e: e.dma_start(
            out=gatew_bf[:].rearrange("p (c e) -> p c e", c=NCH),
            in_=gate_wT.rearrange("(c p) e -> p c e", p=128)), inc=(sW, 16))
        p.op("gpsimd", lambda e: e.dma_start(out=gb_bf[:], in_=gate_b), inc=(sW, 16))
        p.op("gpsimd", lambda e: e.dma_start(out=eb_bf[:], in_=expert_b), inc=(sW, 16))
        p.op("gpsimd", lambda e: e.dma_start(
            out=wbuf[:, 8192:16384].rearrange("p (c o) -> p c o", c=NCH),
            in_=mixer_wT.rearrange("(c p) o -> p c o", p=128)), inc=(sW, 16))
        weights_loaded = sW.n

        def proj_tile(c, h):
            return wbuf[:, (c * NH + h) * HD:(c * NH + h + 1) * HD]

        def mix_tile(c, oc):
            return wbuf[:, 8192 + c * D + oc * 128: 8192 + c * D + oc * 128 + 128]

        def emit_ln():
            for tt in range(NTT):
                t0 = tt * 512
                for c in range(NCH):
                    p.op("vector", lambda e, c=c, t0=t0: e.tensor_copy(
                        out=xn[:, cs(c, t0, 512)], in_=x[:, cs(c, t0, 512)]),
                        inc=(sDVE, 1) if c == NCH - 1 else None)
                p.wait("tensor", sDVE)
                b0 = banks.acquire(0)
                b1 = banks.acquire(1)
                for c in range(NCH):
                    p.wait("scalar", sPE)
                    p.op("scalar", lambda e, c=c, t0=t0: e.activation(
                        out=x2s[:], in_=x[:, cs(c, t0, 512)], func=AF.Square), inc=(sACT, 1))
                    p.op("tensor", lambda e, c=c, t0=t0, b0=b0: e.matmul(
                        b0[0:1, :], ones_bf[:, 0:1], xn[:, cs(c, t0, 512)],
                        start=(c == 0), stop=(c == NCH - 1)))
                    p.wait("tensor", sACT)
                    p.op("tensor", lambda e, c=c, b1=b1: e.matmul(
                        b1[0:1, :], ones_bf[:, 0:1], x2s[:],
                        start=(c == 0), stop=(c == NCH - 1)), inc=(sPE, 1))
                p.wait("vector", sPE)
                p.op("vector", lambda e, b0=b0: e.tensor_scalar(out=rowm, in0=b0[0:1, :],
                                                                scalar1=1.0 / D, scalar2=None, op0=ALU.mult))
                p.op("vector", lambda e, b1=b1: e.tensor_scalar(out=rowe, in0=b1[0:1, :],
                                                                scalar1=1.0 / D, scalar2=None, op0=ALU.mult),
                     inc=(sDVE, 1))
                banks.release(0, sDVE)
                banks.release(1, sDVE)
                p.op("vector", lambda e: e.tensor_mul(out=rowt, in0=rowm, in1=rowm))
                p.op("vector", lambda e: e.tensor_sub(out=rowe, in0=rowe, in1=rowt))
                p.op("vector", lambda e: e.tensor_scalar(out=rowe, in0=rowe, scalar1=1e-5,
                                                         scalar2=None, op0=ALU.add), inc=(sDVE, 1))
                p.wait("scalar", sDVE)
                p.op("scalar", lambda e: e.activation(out=rowt, in_=rowe, func=AF.Sqrt),
                     inc=(sACT, 1))
                p.wait("vector", sACT)
                p.op("vector", lambda e: e.reciprocal(out=rowt, in_=rowt))
                p.op("vector", lambda e: e.tensor_mul(out=rowm, in0=rowm, in1=rowt))
                p.op("vector", lambda e: e.tensor_scalar(out=rowm, in0=rowm, scalar1=-1.0,
                                                         scalar2=None, op0=ALU.mult), inc=(sDVE, 1))
                p.wait("tensor", sDVE)
                b2 = banks.acquire(2)
                b3 = banks.acquire(3)
                p.op("tensor", lambda e, b2=b2: e.matmul(b2[:, :], ones_f[0:1, :], rowt,
                                                         start=True, stop=True))
                p.op("tensor", lambda e, b3=b3: e.matmul(b3[:, :], ones_f[0:1, :], rowm,
                                                         start=True, stop=True), inc=(sPE, 1))
                p.wait("scalar", sPE)
                p.op("scalar", lambda e, b2=b2: e.copy(out=ab[:, 0:512], in_=b2[:, :]))
                p.op("scalar", lambda e, b3=b3: e.copy(out=ab[:, 512:1024], in_=b3[:, :]), inc=(sACT, 1))
                banks.release(2, sACT)
                banks.release(3, sACT)
                p.wait("vector", sACT)
                for c in range(NCH):
                    p.op("vector", lambda e, c=c, t0=t0: e.tensor_mul(
                        out=scr[:, 0:512], in0=x[:, cs(c, t0, 512)], in1=ab[:, 0:512]))
                    p.op("vector", lambda e, c=c, t0=t0: e.tensor_add(
                        out=xn[:, cs(c, t0, 512)], in0=scr[:, 0:512], in1=ab[:, 512:1024]),
                        inc=(sDVE, 1) if c == NCH - 1 else None)

        p.wait("vector", sX, x_loaded)
        p.wait("scalar", sX, x_loaded)
        p.wait("tensor", sPOOL, consts_done)
        emit_ln()
        xn_done = sDVE.n

        p.wait("tensor", sW, weights_loaded)
        p.wait("tensor", sDVE, xn_done)
        for j in range(NCH):
            for tt in range(NTT):
                b = 4 + ((j * NTT + tt) % 2)
                pbx = banks.acquire(b)
                for half in range(2):
                    h = 2 * j + half
                    for c in range(NCH):
                        p.op("tensor", lambda e, pbx=pbx, half=half, h=h, c=c, tt=tt: e.matmul(
                            pbx[half * 64:(half + 1) * 64, :], proj_tile(c, h),
                            xn[:, cs(c, tt * 512, 512)],
                            start=(c == 0), stop=(c == NCH - 1)),
                            inc=(sPE, 1) if (half == 1 and c == NCH - 1) else None)
                p.wait("scalar", sPE)
                p.op("scalar", lambda e, pbx=pbx, j=j, tt=tt: e.copy(
                    out=pfm[:, cs(j, tt * 512, 512)], in_=pbx[:, :]), inc=(sACT, 1))
                banks.release(b, sACT)

        g_mm_done = {}
        for tc in range(NTC):
            pt = tc % 2
            pb6 = banks.acquire(6)
            pb7 = banks.acquire(7)
            for c in range(NCH):
                p.op("tensor", lambda e, pb6=pb6, c=c, tc=tc: e.matmul(
                    pb6[:, :], xn[:, cs(c, tc * 128, 128)], wbuf[:, c * 1024: c * 1024 + 512],
                    start=(c == 0), stop=(c == NCH - 1)))
                p.op("tensor", lambda e, pb7=pb7, c=c, tc=tc: e.matmul(
                    pb7[:, :], xn[:, cs(c, tc * 128, 128)], wbuf[:, c * 1024 + 512: c * 1024 + 1024],
                    start=(c == 0), stop=(c == NCH - 1)),
                    inc=(sPE, 1) if c == NCH - 1 else None)
            p.wait("scalar", sPE)
            if tc >= 2:
                p.wait("scalar", sPE, g_mm_done[tc - 2])
            p.op("scalar", lambda e, pb6=pb6, pt=pt: e.copy(
                out=ptokb[:, pt * 1024: pt * 1024 + 512], in_=pb6[:, :]))
            p.op("scalar", lambda e, pb7=pb7, pt=pt: e.copy(
                out=ptokb[:, pt * 1024 + 512: pt * 1024 + 1024], in_=pb7[:, :]), inc=(sACT, 1))
            banks.release(6, sACT)
            banks.release(7, sACT)
            p.wait("tensor", sACT)
            b0 = banks.acquire(0)
            b1 = banks.acquire(1)
            for h in range(NH):
                pbx = b0 if h < 8 else b1
                p.op("tensor", lambda e, pbx=pbx, h=h, pt=pt: e.matmul(
                    pbx[0:64, (h % 8) * 64:(h % 8) * 64 + 64],
                    ptokb[:, pt * 1024 + h * 64: pt * 1024 + h * 64 + 64],
                    ptokb[:, pt * 1024 + h * 64: pt * 1024 + h * 64 + 64],
                    start=True, stop=True), inc=(sPE, 1) if h == NH - 1 else None)
            g_mm_done[tc] = sPE.n
            p.wait("vector", sPE)
            p.wait("vector", sPOOL, consts_done)
            p.op("vector", lambda e, b0=b0: e.tensor_add(out=gacc[0:64, 0:512],
                                                         in0=gacc[0:64, 0:512], in1=b0[0:64, :]))
            p.op("vector", lambda e, b1=b1: e.tensor_add(out=gacc[0:64, 512:1024],
                                                         in0=gacc[0:64, 512:1024], in1=b1[0:64, :]),
                 inc=(sDVE, 1))
            banks.release(0, sDVE)
            banks.release(1, sDVE)
        ptok_mm_done = sPE.n
        p.op("vector", lambda e: e.tensor_copy(out=gbf[:], in_=gacc[:]), inc=(sDVE, 1))
        g_done = sDVE.n

        p.wait("tensor", sDVE, g_done)
        b2 = banks.acquire(2)
        b3 = banks.acquire(3)
        for h in range(NH):
            pbx = b2 if h < 8 else b3
            p.op("tensor", lambda e, pbx=pbx, h=h: e.matmul(
                pbx[0:64, (h % 8) * 64:(h % 8) * 64 + 64],
                metric_sb[:, h * 64: h * 64 + 64], gbf[:, h * 64: h * 64 + 64],
                start=True, stop=True), inc=(sPE, 1) if h == NH - 1 else None)
        p.wait("scalar", sPE)
        for h in range(NH):
            pbx = b2 if h < 8 else b3
            p.op("scalar", lambda e, pbx=pbx, h=h: e.copy(
                out=mg[(h % 2) * 64:(h % 2) * 64 + 64, (h // 2) * 64:(h // 2) * 64 + 64],
                in_=pbx[0:64, (h % 8) * 64:(h % 8) * 64 + 64]),
                inc=(sACT, 1) if h == NH - 1 else None)
        banks.release(2, sACT)
        banks.release(3, sACT)
        mg_done = sACT.n

        p.wait("tensor", sACT, mg_done)
        for h in range(NH):
            base = (h % 2) * 64
            j = h // 2
            for tt in range(NTT):
                b = 2 + ((h * NTT + tt) % 2)
                pbx = banks.acquire(b)
                p.op("tensor", lambda e, pbx=pbx, j=j, tt=tt, base=base: e.matmul(
                    pbx[0:64, :], mg[base:base + 64, j * 64: j * 64 + 64],
                    pfm[base:base + 64, cs(j, tt * 512, 512)],
                    start=True, stop=True), inc=(sPE, 1))
                p.wait("scalar", sPE)
                p.op("scalar", lambda e, pbx=pbx, j=j, tt=tt, base=base: e.copy(
                    out=pfm[base:base + 64, cs(j, tt * 512, 512)], in_=pbx[0:64, :]),
                    inc=(sACT, 1))
                banks.release(b, sACT)
        nudged_done = sACT.n

        p.wait("tensor", sDVE, xn_done)
        for tc in range(NTC):
            b = 4 + (tc % 2)
            pbx = banks.acquire(b)
            for c in range(NCH):
                p.op("tensor", lambda e, pbx=pbx, c=c, tc=tc: e.matmul(
                    pbx[:, 0:8], xn[:, cs(c, tc * 128, 128)], gatew_bf[:, c * 8: c * 8 + 8],
                    start=(c == 0), stop=False))
            p.op("tensor", lambda e, pbx=pbx: e.matmul(
                pbx[:, 0:8], ones_bf[0:1, :], gb_bf[:], start=False, stop=True), inc=(sPE, 1))
            p.wait("vector", sPE)
            p.op("vector", lambda e, pbx=pbx: e.reduce_max(out=col1[:], in_=pbx[:, 0:8], axis=AX,
                                                           negate=True), inc=(sDVE, 1))
            p.wait("scalar", sDVE)
            p.op("scalar", lambda e, pbx=pbx: e.activation(out=gtok[:], in_=pbx[:, 0:8],
                                                           func=AF.Exp, bias=col1[:],
                                                           accum_out=col2[:]), inc=(sACT, 1))
            banks.release(b, sACT)
            p.wait("vector", sACT)
            p.op("vector", lambda e: e.reciprocal(out=col2[:], in_=col2[:]), inc=(sDVE, 1))
            p.wait("vector", sDVE)
            p.op("vector", lambda e: e.tensor_scalar(out=gtok[:], in0=gtok[:], scalar1=col2[:],
                                                     scalar2=None, op0=ALU.mult), inc=(sDVE, 1))
            p.wait("vector", sDVE)
            p.op("vector", lambda e: e.reduce_max(out=col1[:], in_=gtok[:], axis=AX),
                 inc=(sDVE, 1))
            p.wait("vector", sDVE)
            p.op("vector", lambda e: e.tensor_scalar(out=gtok2[:], in0=gtok[:], scalar1=col1[:],
                                                     scalar2=None, op0=ALU.is_equal), inc=(sDVE, 1))
            p.wait("vector", sDVE)
            p.op("vector", lambda e: e.scalar_tensor_tensor(out=gtok2[:], in0=gtok2[:], scalar=-2.0,
                                                            in1=gtok[:], op0=ALU.mult, op1=ALU.add),
                 inc=(sDVE, 1))
            p.wait("vector", sDVE)
            p.op("vector", lambda e: e.reduce_max(out=col3[:], in_=gtok2[:], axis=AX),
                 inc=(sDVE, 1))
            p.wait("vector", sDVE)
            p.op("vector", lambda e: e.tensor_scalar(out=gtok2[:], in0=gtok[:], scalar1=col3[:],
                                                     scalar2=None, op0=ALU.is_ge), inc=(sDVE, 1))
            p.wait("vector", sDVE)
            p.op("vector", lambda e: e.tensor_mul(out=gtok[:], in0=gtok[:], in1=gtok2[:]),
                 inc=(sDVE, 1))
            p.wait("vector", sDVE)
            p.op("vector", lambda e: e.tensor_copy(out=gtok_bf[:], in_=gtok[:]), inc=(sDVE, 1))
            p.wait("tensor", sDVE)
            bT = 6 + (tc % 2)
            pbt = banks.acquire(bT)
            p.op("tensor", lambda e, pbt=pbt: e.matmul(pbt[0:8, 0:128], gtok_bf[:], ident_bf[:],
                                                       start=True, stop=True), inc=(sPE, 1))
            p.wait("scalar", sPE)
            p.op("scalar", lambda e, pbt=pbt, tc=tc: e.copy(out=g_sb[:, tc * 128:(tc + 1) * 128],
                                                            in_=pbt[0:8, 0:128]), inc=(sACT, 1))
            banks.release(bT, sACT)
        gates_done = sACT.n

        p.wait("tensor", sACT, nudged_done)
        p.wait("vector", sX, x_loaded)
        for oc in range(NCH):
            for tt in range(NTT):
                b = (oc * NTT + tt) % 4
                pbx = banks.acquire(b)
                for c in range(NCH):
                    p.op("tensor", lambda e, pbx=pbx, c=c, oc=oc, tt=tt: e.matmul(
                        pbx[:, :], mix_tile(c, oc), pfm[:, cs(c, tt * 512, 512)],
                        start=(c == 0), stop=(c == NCH - 1)),
                        inc=(sPE, 1) if c == NCH - 1 else None)
                p.wait("vector", sPE)
                p.op("vector", lambda e, pbx=pbx, oc=oc, tt=tt: e.scalar_tensor_tensor(
                    out=x[:, cs(oc, tt * 512, 512)], in0=pbx[:, :], scalar=mixb_sb[:, oc:oc + 1],
                    in1=x[:, cs(oc, tt * 512, 512)], op0=ALU.add, op1=ALU.add), inc=(sDVE, 1))
                banks.release(b, sDVE)
        pe_after_mixer = sPE.n

        expert_bufs = [wbuf[:, 0:8192], wbuf[:, 8192:16384]]

        def ebuf_tile(ebuf, c, oc):
            return ebuf[:, c * D + oc * 128: c * D + oc * 128 + 128]

        w_loaded = {}
        p.wait("gpsimd", sPE, ptok_mm_done)
        p.op("gpsimd", lambda e: e.dma_start(
            out=expert_bufs[0].rearrange("p (c o) -> p c o", c=NCH),
            in_=expert_wT[0].rearrange("(c p) o -> p c o", p=128)), inc=(sWE[0], 16))
        w_loaded[0] = (0, sWE[0].n)
        p.wait("gpsimd", sPE, pe_after_mixer)
        p.op("gpsimd", lambda e: e.dma_start(
            out=expert_bufs[1].rearrange("p (c o) -> p c o", c=NCH),
            in_=expert_wT[1].rearrange("(c p) o -> p c o", p=128)), inc=(sWE[1], 16))
        w_loaded[1] = (1, sWE[1].n)
        p.wait("sync", sACT, gates_done)
        grow_loaded = {}
        p.op("sync", lambda e: e.dma_start(out=g_row[:], in_=g_sb[0:1, :]), inc=(sX, 16))
        grow_loaded[0] = sX.n

        p.wait("vector", sDVE)
        xne2[1] = pfm[:, 0:4096]
        last_xne_reader = {0: 0, 1: pe_after_mixer}
        last_gbc_reader = {0: 0, 1: 0}
        last_grow_reader = {0: pe_after_mixer}
        for e_i in range(8):
            ebuf = expert_bufs[e_i % 2]
            p.wait("tensor", sWE[w_loaded[e_i][0]], w_loaded[e_i][1])
            p.wait("tensor", sX, grow_loaded[e_i])
            for tt in range(NTT):
                par = tt % 2
                bg = par
                pbg = banks.acquire(bg)
                p.op("tensor", lambda e, pbg=pbg, e_i=e_i, tt=tt: e.matmul(
                    pbg[:, :], ones_bf[0:1, :], g_row[0:1, tt * 512:tt * 512 + 512],
                    start=True, stop=True), inc=(sPE, 1))
                last_grow_reader[0] = sPE.n
                p.wait("scalar", sPE)
                p.wait("scalar", sDVE, last_gbc_reader[par])
                p.op("scalar", lambda e, pbg=pbg, par=par: e.copy(
                    out=gbc[:, par * 512:par * 512 + 512], in_=pbg[:, :]), inc=(sACT, 1))
                banks.release(bg, sACT)
                xneb = xne2[par]
                p.wait("vector", sACT)
                p.wait("vector", sPE, last_xne_reader[par])
                for c in range(NCH):
                    p.op("vector", lambda e, c=c, tt=tt, par=par, xneb=xneb: e.tensor_mul(
                        out=xneb[:, c * 512: c * 512 + 512],
                        in0=xn[:, cs(c, tt * 512, 512)],
                        in1=gbc[:, par * 512:par * 512 + 512]),
                        inc=(sDVE, 1) if c == NCH - 1 else None)
                last_gbc_reader[par] = sDVE.n
                p.wait("tensor", sDVE)
                for oc in range(NCH):
                    pbx = banks.acquire(oc)
                    for c in range(NCH):
                        p.op("tensor", lambda e, pbx=pbx, ebuf=ebuf, c=c, oc=oc, e_i=e_i,
                             xneb=xneb: e.matmul(
                            pbx[:, :], ebuf_tile(ebuf, c, oc),
                            xneb[:, c * 512: c * 512 + 512],
                            start=(c == 0), stop=(c == NCH - 1 and e_i != 0)),
                            inc=(sPE, 1) if (c == NCH - 1 and e_i != 0) else None)
                    if e_i == 0:
                        p.op("tensor", lambda e, pbx=pbx, oc=oc, tt=tt: e.matmul(
                            pbx[:, :], eb_bf[:, oc * 128:oc * 128 + 128],
                            g_sb[:, tt * 512:tt * 512 + 512],
                            start=False, stop=True), inc=(sPE, 1))
                    p.wait("vector", sPE)
                    p.op("vector", lambda e, pbx=pbx, oc=oc, tt=tt: e.tensor_add(
                        out=x[:, cs(oc, tt * 512, 512)], in0=x[:, cs(oc, tt * 512, 512)],
                        in1=pbx[:, :]), inc=(sDVE, 1))
                    banks.release(oc, sDVE)
                last_xne_reader[par] = sPE.n
            if e_i + 2 < 8:
                p.wait("gpsimd", sPE, sPE.n)
                p.op("gpsimd", lambda e, e_i=e_i: e.dma_start(
                    out=expert_bufs[e_i % 2].rearrange("p (c o) -> p c o", c=NCH),
                    in_=expert_wT[e_i + 2].rearrange("(c p) o -> p c o", p=128)),
                    inc=(sWE[e_i % 2], 16))
                w_loaded[e_i + 2] = (e_i % 2, sWE[e_i % 2].n)
            if e_i + 1 < 8:
                p.wait("sync", sPE, last_grow_reader[0])
                p.op("sync", lambda e, e_i=e_i: e.dma_start(
                    out=g_row[:], in_=g_sb[e_i + 1:e_i + 2, :]), inc=(sX, 16))
                grow_loaded[e_i + 1] = sX.n
        x1_done = sDVE.n

        if "x1" in dbg:
            p.wait("gpsimd", sDVE, x1_done)
            p.op("gpsimd", lambda e: e.dma_start(
                out=dbg["x1"].rearrange("(c p) t -> p c t", p=128),
                in_=x.rearrange("p (c t) -> p c t", c=NCH)), inc=(sW, 16))

        p.wait("vector", sPE)
        p.wait("vector", sDVE, x1_done)
        p.wait("scalar", sPE)
        p.wait("tensor", sDVE, x1_done)
        emit_ln()
        ln1_done = sDVE.n

        p.wait("sync", sDVE, ln1_done)
        p.op("sync", lambda e: e.dma_start(
            out=outT.rearrange("(c p) t -> p c t", p=128),
            in_=x.rearrange("p (c t) -> p c t", c=NCH)), inc=(sOI, 16))
        out_init = sOI.n

        p.wait("gpsimd", sOI, out_init)
        p.op("gpsimd", lambda e: e.iota(xstg[0:64, 0:2048], pattern=[[1, 2048]], base=0,
                                        channel_multiplier=0,
                                        allow_small_or_imprecise_dtypes=True), inc=(sPOOL, 1))
        p.wait("scalar", sPOOL)
        p.op("scalar", lambda e: e.activation(out=invf[:], in_=invf[:], func=AF.Exp,
                                              scale=-math.log(10000.0) / 64.0), inc=(sACT, 1))
        p.wait("vector", sACT)
        p.wait("vector", sPOOL)
        tmpA = xstg[0:64, 0:2048]
        tmpB = xstg[0:64, 2048:4096]
        p.op("vector", lambda e: e.tensor_scalar(out=tmpA, in0=tmpA, scalar1=invf[0:64, 0:1],
                                                 scalar2=None, op0=ALU.mult), inc=(sDVE, 1))
        p.wait("vector", sDVE)
        p.op("vector", lambda e: e.tensor_scalar(out=tmpB, in0=tmpA, scalar1=1.0 / (2 * math.pi),
                                                 scalar2=MAGIC, op0=ALU.mult, op1=ALU.add))
        p.op("vector", lambda e: e.tensor_scalar(out=tmpB, in0=tmpB, scalar1=MAGIC,
                                                 scalar2=None, op0=ALU.subtract))
        p.op("vector", lambda e: e.scalar_tensor_tensor(out=tmpA, in0=tmpB, scalar=-2 * math.pi,
                                                        in1=tmpA, op0=ALU.mult, op1=ALU.add),
             inc=(sDVE, 1))
        p.wait("scalar", sDVE)
        p.op("scalar", lambda e: e.activation(out=tmpB, in_=tmpA, func=AF.Sin, scale=0.5))
        p.op("scalar", lambda e: e.activation(out=sinT[0:64, :], in_=tmpA, func=AF.Sin,
                                              scale=-0.5, bias=hpi[:, 0:1]))
        p.op("scalar", lambda e: e.activation(out=cosT[0:64, :], in_=tmpB, func=AF.Square),
             inc=(sACT, 1))
        p.wait("vector", sACT)
        p.op("vector", lambda e: e.scalar_tensor_tensor(out=sinT[0:64, :], in0=tmpB, scalar=2.0,
                                                        in1=sinT[0:64, :], op0=ALU.mult,
                                                        op1=ALU.mult))
        p.op("vector", lambda e: e.tensor_scalar(out=cosT[0:64, :], in0=cosT[0:64, :],
                                                 scalar1=-2.0, scalar2=1.0,
                                                 op0=ALU.mult, op1=ALU.add), inc=(sDVE, 1))
        p.wait("sync", sDVE)
        p.op("sync", lambda e: e.dma_start(out=cosT[64:128, :], in_=cosT[0:64, :]), inc=(sX, 16))
        p.op("sync", lambda e: e.dma_start(out=sinT[64:128, :], in_=sinT[0:64, :]), inc=(sX, 16))
        p.wait("vector", sX)
        p.op("vector", lambda e: e.tensor_scalar(out=sinT[0:64, :], in0=sinT[0:64, :],
                                                 scalar1=-1.0, scalar2=None, op0=ALU.mult),
             inc=(sDVE, 1))
        rope_done = sDVE.n

        p.wait("gpsimd", sOI, out_init)
        p.wait("gpsimd", sWE[0])
        for name, dst, c in [("wq", pwq, NCH), ("wkv", pwkv, NCH), ("wqc", pwqc, 2),
                             ("wkc", pwkc, 2), ("wqr", wqr_sb, NCH), ("wkr", wkr_sb, NCH),
                             ("wv", wv_sb, 2)]:
            p.op("gpsimd", lambda e, name=name, dst=dst, c=c: e.dma_start(
                out=dst.rearrange("p (c o) -> p c o", c=c),
                in_=W[name].rearrange("(c p) o -> p c o", p=128)), inc=(sW, 16))
        attw_loaded = sW.n
        for name in ("bq", "bqc", "bqr", "bkv", "bkr", "bkc", "bv", "bo", "convw", "convb"):
            p.op("sync", lambda e, name=name: e.dma_start(out=bias_sb[name][:], in_=W[name]),
                 inc=(sB, 16))
        bias_loaded = sB.n

        p.wait("tensor", sW, attw_loaded)
        p.wait("tensor", sDVE, ln1_done)
        p.wait("scalar", sB, bias_loaded)
        p.wait("scalar", sDVE, rope_done)

        def proj_block(dst_fn, w_tile_fn, nin_ch, bias_name, noc, func, src_fn):
            for oc in range(noc):
                for tt in range(NTT):
                    b = (oc * NTT + tt) % 4
                    pbx = banks.acquire(b)
                    for c in range(nin_ch):
                        p.op("tensor", lambda e, pbx=pbx, c=c, oc=oc, tt=tt: e.matmul(
                            pbx[:, :], w_tile_fn(c, oc), src_fn(c, tt),
                            start=(c == 0), stop=(c == nin_ch - 1)),
                            inc=(sPE, 1) if c == nin_ch - 1 else None)
                    p.wait("scalar", sPE)
                    p.op("scalar", lambda e, pbx=pbx, oc=oc, tt=tt: e.activation(
                        out=dst_fn(oc, tt), in_=pbx[:, :], func=func,
                        bias=bias_sb[bias_name][:, oc:oc + 1]), inc=(sACT, 1))
                    banks.release(b, sACT)

        def xn_src(c, tt):
            return xn[:, cs(c, tt * 512, 512)]

        def sl(buf, oc, tt):
            return buf[:, oc * 2048 + tt * 512: oc * 2048 + tt * 512 + 512]

        proj_block(lambda oc, tt: sl(cq, oc, tt),
                   lambda c, oc: pwq[:, c * 256 + oc * 128: c * 256 + oc * 128 + 128],
                   NCH, "bq", 2, AF.Gelu, xn_src)
        proj_block(lambda oc, tt: sl(ckv, oc, tt),
                   lambda c, oc: pwkv[:, c * 256 + oc * 128: c * 256 + oc * 128 + 128],
                   NCH, "bkv", 2, AF.Gelu, xn_src)
        cqkv_done = sACT.n

        p.wait("tensor", sACT, cqkv_done)
        proj_block(lambda oc, tt: sl(Q, oc, tt),
                   lambda c, oc: pwqc[:, c * 896 + oc * 128: c * 896 + oc * 128 + 128],
                   2, "bqc", 7, AF.Gelu, lambda c, tt: sl(cq, c, tt))
        proj_block(lambda oc, tt: sl(Kb, oc, tt),
                   lambda c, oc: pwkc[:, c * 896 + oc * 128: c * 896 + oc * 128 + 128],
                   2, "bkc", 7, AF.Gelu, lambda c, tt: sl(ckv, c, tt))

        Vv = Vb.rearrange("p (t h o) -> p t h o", t=NTC, h=NH)
        p.op("gpsimd", lambda e: e.memset(Vv[:, :, :, 64:65], 1.0), inc=(sPOOL, 1))
        for tc in range(NTC):
            for n2 in range(2):
                b = 4 + (tc * 2 + n2) % 2
                pbx = banks.acquire(b)
                for c in range(2):
                    p.op("tensor", lambda e, pbx=pbx, c=c, tc=tc, n2=n2: e.matmul(
                        pbx[:, :], ckv[:, c * 2048 + tc * 128: c * 2048 + tc * 128 + 128],
                        wv_sb[:, c * 1024 + n2 * 512: c * 1024 + n2 * 512 + 512],
                        start=(c == 0), stop=(c == 1)), inc=(sPE, 1) if c == 1 else None)
                p.wait("scalar", sPE)
                p.op("scalar", lambda e, pbx=pbx, tc=tc, n2=n2: e.activation(
                    out=Vv[:, tc, n2 * 8:(n2 + 1) * 8, 0:64],
                    in_=pbx[:, :].rearrange("p (h o) -> p h o", h=8), func=AF.Copy),
                    inc=(sACT, 1))
                banks.release(b, sACT)
        v_done = sACT.n
        qkv_mm_done = sPE.n

        p.wait("vector", sDVE, rope_done)
        for which, w_sb, bias_name, dstc in (("q", wqr_sb, "bqr", Q), ("k", wkr_sb, "bkr", Kb)):
            for tt in range(NTT):
                b = (tt % 2)
                pbx = banks.acquire(b)
                for c in range(NCH):
                    p.op("tensor", lambda e, pbx=pbx, c=c, tt=tt, w_sb=w_sb: e.matmul(
                        pbx[:, :], w_sb[:, c * 128: c * 128 + 128], xn_src(c, tt),
                        start=(c == 0), stop=(c == NCH - 1)),
                        inc=(sPE, 1) if c == NCH - 1 else None)
                if which == "q" and tt == 0:
                    p.wait("scalar", sPE, qkv_mm_done)
                p.wait("scalar", sPE)
                p.op("scalar", lambda e, pbx=pbx, tt=tt, bias_name=bias_name: e.activation(
                    out=qrS[:, tt * 512:(tt + 1) * 512], in_=pbx[:, :], func=AF.Identity,
                    bias=bias_sb[bias_name][:, 0:1]), inc=(sACT, 1))
                banks.release(b, sACT)
            p.wait("sync", sACT)
            p.op("sync", lambda e: e.dma_start(out=rotS[0:64, :], in_=qrS[64:128, :]),
                 inc=(sX, 16))
            p.op("sync", lambda e: e.dma_start(out=rotS[64:128, :], in_=qrS[0:64, :]),
                 inc=(sX, 16))
            p.wait("vector", sX)
            p.wait("vector", sACT)
            p.op("vector", lambda e: e.tensor_mul(out=qrS[:], in0=qrS[:], in1=cosT[:]))
            p.op("vector", lambda e: e.tensor_mul(out=rotS[:], in0=rotS[:], in1=sinT[:]))
            p.op("vector", lambda e: e.tensor_add(out=qrS[:], in0=qrS[:], in1=rotS[:]),
                 inc=(sDVE, 1))
            p.wait("scalar", sDVE)
            for tt in range(NTT):
                p.op("scalar", lambda e, tt=tt, dstc=dstc: e.activation(
                    out=dstc[:, 7 * 2048 + tt * 512: 7 * 2048 + tt * 512 + 512],
                    in_=qrS[:, tt * 512:(tt + 1) * 512], func=AF.Gelu),
                    inc=(sACT, 1) if tt == NTT - 1 else None)
        qk_done = sACT.n

        p.wait("vector", sACT, qk_done)
        for i, buf in enumerate((Q, Kb)):
            for c in range(NCH):
                p.op("vector", lambda e, c=c, buf=buf: e.tensor_reduce(
                    out=qmaxc[:, c:c + 1], in_=buf[:, c * 2048:(c + 1) * 2048], axis=AX,
                    op=ALU.max, apply_absolute_value=True),
                    inc=(sDVE, 1) if c == NCH - 1 else None)
            p.wait("vector", sDVE)
            p.op("vector", lambda e: e.tensor_reduce(out=qmaxc[:, 8:9], in_=qmaxc[:, 0:8],
                                                     axis=AX, op=ALU.max), inc=(sDVE, 1))
            p.wait("vector", sDVE)
            p.op("vector", lambda e, i=i: e.tensor_copy(out=qkmax[:, i:i + 1],
                                                        in_=qmaxc[:, 8:9]), inc=(sDVE, 1))
            p.wait("vector", sDVE)
        p.wait("tensor", sDVE)
        b0 = banks.acquire(0)
        p.op("tensor", lambda e, b0=b0: e.matmul(b0[0:1, 0:128], qkmax[:, 0:1], ident_bf[:],
                                                 start=True, stop=True))
        p.op("tensor", lambda e, b0=b0: e.matmul(b0[0:1, 128:256], qkmax[:, 1:2], ident_bf[:],
                                                 start=True, stop=True), inc=(sPE, 1))
        p.wait("vector", sPE)
        p.op("vector", lambda e, b0=b0: e.reduce_max(out=brow[:, 0:1], in_=b0[0:1, 0:128],
                                                     axis=AX))
        p.op("vector", lambda e, b0=b0: e.reduce_max(out=brow[:, 1:2], in_=b0[0:1, 128:256],
                                                     axis=AX), inc=(sDVE, 1))
        banks.release(0, sDVE)
        p.wait("vector", sDVE)
        p.op("vector", lambda e: e.tensor_mul(out=brow[:, 2:3], in0=brow[:, 0:1],
                                              in1=brow[:, 1:2]), inc=(sDVE, 1))
        p.wait("vector", sDVE)
        p.op("vector", lambda e: e.tensor_scalar(out=brow[:, 2:3], in0=brow[:, 2:3],
                                                 scalar1=-8.0, scalar2=None, op0=ALU.mult),
             inc=(sDVE, 1))
        p.wait("tensor", sDVE)
        b0 = banks.acquire(0)
        p.op("tensor", lambda e, b0=b0: e.matmul(b0[:, 0:1], ones_f[0:1, :], brow[:, 2:3],
                                                 start=True, stop=True), inc=(sPE, 1))
        p.wait("scalar", sPE)
        p.op("scalar", lambda e, b0=b0: e.copy(out=negb[:], in_=b0[:, 0:1]), inc=(sACT, 1))
        banks.release(0, sACT)
        bound_done = sACT.n

        _wdone = {}
        _wgen = gen_wmean(p, banks, locals(), _wdone)

        p.wait("tensor", sACT, bound_done)
        p.wait("tensor", sACT, v_done)
        p.wait("tensor", sPOOL)
        ptb_reader = [0, 0, 0, 0]
        scrA_reader = 0
        ep_stage = []
        sPV = p.sem("sPV")

        def make_stages(pbO0_, bO0_, pbO1_, bO1_, jc_, qg_, pv_done_):
            col0 = jc_ * 2048 + qg_ * 512
            st = {}

            def s1():
                p.wait("vector", sPV, pv_done_)
                p.op("vector", lambda e: e.reciprocal(out=rrowA[:], in_=pbO0_[64:65, :]),
                     inc=(sDVE, 1))
                p.op("vector", lambda e: e.reciprocal(out=rrowB[:], in_=pbO1_[64:65, :]),
                     inc=(sDVE, 1))
                st["recip"] = sDVE.n

            def s2():
                p.wait("tensor", sDVE, st["recip"])
                p.op("tensor", lambda e: e.matmul(pbO0_[64:128, :], ones_f[0:1, 0:64],
                                                  rrowA[:], start=True, stop=True),
                     inc=(sPE, 1))
                p.op("tensor", lambda e: e.matmul(pbO1_[64:128, :], ones_f[0:1, 0:64],
                                                  rrowB[:], start=True, stop=True),
                     inc=(sPE, 1))
                st["bcast"] = sPE.n

            def s3():
                p.wait("scalar", sPE, st["bcast"])
                p.op("scalar", lambda e: e.copy(out=rstgA[:], in_=pbO0_[64:128, :]),
                     inc=(sACT, 1))
                p.op("scalar", lambda e: e.copy(out=rstgB[:], in_=pbO1_[64:128, :]),
                     inc=(sACT, 1))
                st["evac"] = sACT.n

            def s4():
                p.wait("vector", sACT, st["evac"])
                p.wait("vector", sACT, scrA_reader)
                p.op("vector", lambda e: e.tensor_mul(out=scrA_A[:], in0=pbO0_[0:64, :],
                                                      in1=rstgA[:]), inc=(sDVE, 1))
                p.op("vector", lambda e: e.tensor_mul(out=scrA_B[:], in0=pbO1_[0:64, :],
                                                      in1=rstgB[:]), inc=(sDVE, 1))
                banks.release(bO0_, sDVE)
                banks.release(bO1_, sDVE)
                st["mul"] = sDVE.n

            def s5():
                nonlocal scrA_reader
                p.wait("scalar", sDVE, st["mul"])
                p.op("scalar", lambda e: e.activation(
                    out=attn_out[0:64, col0: col0 + 512], in_=scrA_A[:], func=AF.Copy),
                    inc=(sACT, 1))
                p.op("scalar", lambda e: e.activation(
                    out=attn_out[64:128, col0: col0 + 512], in_=scrA_B[:], func=AF.Copy),
                    inc=(sACT, 1))
                scrA_reader = sACT.n

            return [s1, s2, s3, s4, s5]

        for jc in range(NCH):
            for qg in range(NTT):
                nkt = 4 * qg + 4
                bO0 = 4 + 2 * ((jc * NTT + qg) % 2)
                bO1 = bO0 + 1
                pbO0 = banks.acquire(bO0)
                pbO1 = banks.acquire(bO1)

                def emit_S(kt, jc=jc, qg=qg):
                    bp = kt % 2
                    banks.acquire(2 * bp)
                    banks.acquire(2 * bp + 1)
                    for half in range(2):
                        p.op("tensor", lambda e, bp=bp, half=half, kt=kt: e.matmul(
                            pbD[bp][:, half * 512:(half + 1) * 512],
                            Kb[half * 64:half * 64 + 64,
                               jc * 2048 + kt * 128: jc * 2048 + kt * 128 + 128],
                            Q[half * 64:half * 64 + 64,
                              jc * 2048 + qg * 512: jc * 2048 + qg * 512 + 512],
                            start=True, stop=True), inc=(sPE, 1))
                    slot = kt % 4
                    p.wait("scalar", sPE)
                    p.wait("scalar", sPV, ptb_reader[slot])
                    p.op("scalar", lambda e, bp=bp, slot=slot: e.activation(
                        out=ptb[:, slot * 1024:(slot + 1) * 1024], in_=pbD[bp][:, :], func=AF.Exp,
                        scale=0.125, bias=negb[:, 0:1]), inc=(sACT, 1))
                    banks.release(2 * bp, sACT)
                    banks.release(2 * bp + 1, sACT)
                    ktrel = kt - 4 * qg
                    if ktrel >= 0:
                        p.wait("vector", sACT)
                        for half in range(2):
                            c0 = slot * 1024 + half * 512
                            if ktrel > 0:
                                p.op("vector", lambda e, c0=c0, ktrel=ktrel: e.memset(
                                    ptb[:, c0: c0 + ktrel * 128], 0.0))
                            p.op("vector", lambda e, c0=c0, ktrel=ktrel: e.tensor_mul(
                                out=ptb[:, c0 + ktrel * 128: c0 + (ktrel + 1) * 128],
                                in0=ptb[:, c0 + ktrel * 128: c0 + (ktrel + 1) * 128],
                                in1=tri_bf[:]), inc=(sDVE, 1))
                        return ("dve", sDVE.n)
                    return ("act", sACT.n)

                def emit_PV(kt, ready, jc=jc, nkt=nkt, pbO0=pbO0, pbO1=pbO1):
                    slot = kt % 4
                    if ready[0] == "dve":
                        p.wait("tensor", sDVE, ready[1])
                    else:
                        p.wait("tensor", sACT, ready[1])
                    for half, pbO_ in ((0, pbO0), (1, pbO1)):
                        p.op("tensor", lambda e, kt=kt, slot=slot, half=half, pbO_=pbO_: e.matmul(
                            pbO_[0:65, :], Vv[:, kt, 2 * jc + half, :],
                            ptb[:, slot * 1024 + half * 512: slot * 1024 + half * 512 + 512],
                            start=(kt == 0), stop=(kt == nkt - 1)), inc=(sPV, 1))
                    ptb_reader[slot] = sPV.n

                ready = emit_S(0)
                for kt in range(1, nkt):
                    ready_next = emit_S(kt)
                    emit_PV(kt - 1, ready)
                    ready = ready_next
                    if ep_stage:
                        ep_stage.pop(0)()
                emit_PV(nkt - 1, ready)
                while ep_stage:
                    ep_stage.pop(0)()
                pv_done = sPV.n
                for _ in range((1, 2, 2, 3)[qg]):
                    next(_wgen, None)
                ep_stage.extend(make_stages(pbO0, bO0, pbO1, bO1, jc, qg, pv_done))
        while ep_stage:
            ep_stage.pop(0)()
        for _ in _wgen:
            pass
        wmean_done = _wdone["act"]
        scores_done = sACT.n
        pe_attn_end = sPE.n

        sWO = [p.sem("sWO0"), p.sem("sWO1")]
        p.wait("vector", sB, bias_loaded)
        p.op("vector", lambda e: e.tensor_copy(out=bv_bf[:], in_=bias_sb["bv"][:]),
             inc=(sDVE, 1))
        wo_reader = {0: ptok_mm_done, 1: ptok_mm_done}
        wo_loaded = {}
        for oc in (0, 1):
            p.wait("gpsimd", sPE, wo_reader[oc])
            p.op("gpsimd", lambda e, oc=oc: e.dma_start(
                out=woS[oc % 2].rearrange("p (c o) -> p c o", c=NCH),
                in_=W["wo"].rearrange("(c p) o -> p c o", p=128)[:, :, oc * 128:(oc + 1) * 128]),
                inc=(sWO[oc % 2], 16))
            wo_loaded[oc] = sWO[oc % 2].n

        p.wait("tensor", sACT, scores_done)
        p.wait("tensor", sDVE)
        p.wait("scalar", sDVE)
        xstg_f = xstg[:]
        astg_reader = {0: 0, 1: 0}
        for oc in range(NCH):
            par = oc % 2
            p.wait("tensor", sWO[par], wo_loaded[oc])
            bpb = banks.acquire(6 + par)
            for c in range(NCH):
                p.op("tensor", lambda e, bpb=bpb, c=c, par=par: e.matmul(
                    bpb[:, 0:1], woS[par][:, c * 128: c * 128 + 128],
                    bv_bf[:, c:c + 1], start=(c == 0), stop=(c == NCH - 1)),
                    inc=(sPE, 1) if c == NCH - 1 else None)
            p.wait("scalar", sPE)
            p.op("scalar", lambda e, bpb=bpb, oc=oc: e.activation(
                out=bprime[:, oc:oc + 1], in_=bpb[:, 0:1], func=AF.Identity,
                bias=bias_sb["bo"][:, oc:oc + 1]), inc=(sACT, 1))
            banks.release(6 + par, sACT)
            p.wait("scalar", sACT)
            for tt in range(NTT):
                b = 2 + (oc * NTT + tt) % 4
                pbx = banks.acquire(b)
                for c in range(NCH):
                    p.op("tensor", lambda e, pbx=pbx, c=c, tt=tt, par=par: e.matmul(
                        pbx[:, :], woS[par][:, c * 128: c * 128 + 128],
                        attn_out[:, cs(c, tt * 512, 512)],
                        start=(c == 0), stop=(c == NCH - 1)),
                        inc=(sPE, 1) if c == NCH - 1 else None)
                p.wait("scalar", sPE)
                p.wait("scalar", sW, astg_reader[par] and sW.n)
                p.op("scalar", lambda e, pbx=pbx, oc=oc, tt=tt, par=par: e.activation(
                    out=xstg_f[:, par * 2048 + tt * 512: par * 2048 + tt * 512 + 512],
                    in_=pbx[:, :], func=AF.Identity, bias=bprime[:, oc:oc + 1]), inc=(sACT, 1))
                banks.release(b, sACT)
            wo_reader[par] = sPE.n
            if oc + 2 < NCH:
                p.wait("gpsimd", sPE, wo_reader[par])
                p.op("gpsimd", lambda e, oc=oc, par=par: e.dma_start(
                    out=woS[par].rearrange("p (c o) -> p c o", c=NCH),
                    in_=W["wo"].rearrange("(c p) o -> p c o", p=128)[:, :, (oc + 2) * 128:(oc + 3) * 128]),
                    inc=(sWO[par], 16))
                wo_loaded[oc + 2] = sWO[par].n
            p.wait("gpsimd", sACT)
            p.wait("gpsimd", sW)
            p.wait("gpsimd", sOI, out_init)
            p.op("gpsimd", lambda e, oc=oc, par=par: e.dma_start(
                out=outT.rearrange("(c p) t -> p c t", p=128)[:, oc, :],
                in_=xstg_f[:, par * 2048: par * 2048 + 2048],
                accum_op=ALU.add), inc=(sW, 16))
            astg_reader[par] = sW.n
        attn_accum_done = sW.n

        arena2 = Dd
        emit_ffn(p, banks, locals())

        p.wait("gpsimd", sDVE)
        p.wait("gpsimd", sACT)
        if "xn" in dbg:
            p.op("gpsimd", lambda e: e.dma_start(
                out=dbg["xn"].rearrange("(c p) t -> p c t", p=128),
                in_=xn[:].rearrange("p (c t) -> p c t", c=NCH)), inc=(sW, 16))
        p.wait("sync", sW)
        p.wait("sync", sX)
        p.wait("sync", sOI, out_init)

        p.finalize(block)
    return nc


def fused_in_map(g, xb):
    m = l0_in_map(g, xb)
    for k in L1_WEIGHT_KEYS:
        m[k] = g[k]
    return m


L1_WEIGHT_KEYS = ("wq", "bq", "wqc", "bqc", "wqr", "bqr", "wkv", "bkv", "wkr", "bkr",
                  "wkc", "bkc", "wv", "bv", "wo", "bo", "w1T", "convw", "convb", "w2T")


def kernel(**inputs):
    g = host_prepare(inputs)
    x = np.ascontiguousarray(np.asarray(inputs["L0_attn_in_sequence_bwc"], np.float32))
    nc = build_fused()
    in_maps = [fused_in_map(g, x[b]) for b in range(NCORES)]
    res = run_bass_kernel_spmd(nc, in_maps, core_ids=list(range(NCORES)))
    out = np.stack([np.asarray(res.results[b]["outT"], np.float32).T for b in range(NCORES)])
    return np.ascontiguousarray(out)


# version 24 (final)
# speedup vs baseline: 1.0075x; 1.0075x over previous
import numpy as np
import ml_dtypes
from contextlib import ExitStack

import concourse.bass as bass
import concourse.mybir as mybir
from concourse.bass_utils import run_bass_kernel_spmd

F32 = mybir.dt.float32
BF16 = mybir.dt.bfloat16
AF = mybir.ActivationFunctionType
ALU = mybir.AluOpType
AX = mybir.AxisListType.X

NCORES = 8
S = 2048
D = 1024
NCH = D // 128
NTT = S // 512
NTC = S // 128
NH = 16
HD = 64
MAGIC = 12582912.0


def cs(c, t0=0, n=S):
    return slice(c * S + t0, c * S + t0 + n)


class Sem:
    def __init__(self, handle, name):
        self.h = handle
        self.name = name
        self.n = 0


class Prog:
    ENGINES = ("tensor", "vector", "scalar", "gpsimd", "sync")
    STRICT = ("vector", "scalar", "gpsimd")

    def __init__(self, nc, es):
        self.nc = nc
        self.es = es
        self.q = {e: [] for e in self.ENGINES}
        self.esem = {}

    def sem(self, name):
        return Sem(self.es.enter_context(self.nc.semaphore(name)), name)

    def op(self, eng, fn, inc=None):
        incs = []
        if inc is not None:
            incs = inc if isinstance(inc, list) else [inc]
        if eng in self.STRICT:
            if not incs:
                es_ = self.esem.get(eng)
                if es_ is None:
                    es_ = self.esem[eng] = self.sem("self_" + eng)
                incs = [(es_, 1)]
            last = self.esem.get(("last", eng))
            if last is not None:
                self.q[eng].append(("wait", last[0], last[1]))
        for sem, d in incs:
            sem.n += d
        if eng in self.STRICT:
            self.esem[("last", eng)] = (incs[0][0], incs[0][0].n)
        self.q[eng].append(("op", fn, incs))

    def wait(self, eng, sem, val=None):
        v = sem.n if val is None else val
        if v > 0:
            self.q[eng].append(("wait", sem, v))

    def finalize(self, block):
        def make_body(items):
            def body(e):
                for kind, a, b in items:
                    if kind == "op":
                        ins = a(e)
                        for sem, d in b:
                            ins = ins.then_inc(sem.h, d)
                    else:
                        e.wait_ge(a.h, b)
            return body
        for eng in self.ENGINES:
            if self.q[eng]:
                getattr(block, eng)(make_body(self.q[eng]))


class Banks:
    def __init__(self, p, pb):
        self.p = p
        self.pb = pb
        self.tag = [None] * len(pb)

    def acquire(self, b):
        if self.tag[b] is not None:
            sem, cnt = self.tag[b]
            self.p.wait("tensor", sem, cnt)
            self.tag[b] = None
        return self.pb[b]

    def release(self, b, sem, cnt=None):
        self.tag[b] = (sem, sem.n if cnt is None else cnt)


def _bias_cols(b):
    b = np.asarray(b, np.float32)
    n = b.shape[0] // 128
    return np.ascontiguousarray(b.reshape(n, 128).T)


def host_prepare(inputs):
    g = {}
    proj = np.asarray(inputs["L0_attn_projection_1nck"], np.float32)[0]
    g["projT"] = np.ascontiguousarray(proj)
    halves = np.asarray(inputs["L0_attn_halves"], np.float32)
    diag = np.asarray(inputs["L0_attn_diagonals_nk"], np.float32)
    H, K = diag.shape
    iu, ju = np.triu_indices(K, k=1)
    metric = np.zeros((H, K, K), np.float32)
    metric[:, iu, ju] = halves
    metric[:, ju, iu] = halves
    dd = np.arange(K)
    metric[:, dd, dd] = diag
    g["metric"] = np.ascontiguousarray(metric)
    g["mixer_wT"] = np.ascontiguousarray(np.asarray(inputs["L0_attn_mixer_w"], np.float32).T)
    g["mixer_b"] = _bias_cols(inputs["L0_attn_mixer_b"])
    g["gate_wT"] = np.ascontiguousarray(np.asarray(inputs["L0_ffn_gate_w"], np.float32).T)
    g["gate_b"] = np.ascontiguousarray(np.asarray(inputs["L0_ffn_gate_b"], np.float32).reshape(1, 8))
    ew = np.asarray(inputs["L0_ffn_expert_w"], np.float32)
    g["expert_wT"] = np.ascontiguousarray(ew.transpose(0, 2, 1))
    g["expert_b"] = np.ascontiguousarray(np.asarray(inputs["L0_ffn_expert_b"], np.float32))
    for k in ("wq", "wqc", "wqr", "wkv", "wkr", "wkc", "wv", "wo"):
        g[k] = np.ascontiguousarray(np.asarray(inputs["L1_attn_" + k], np.float32))
    for k, n in (("bq", 2), ("bqc", 7), ("bqr", 1), ("bkv", 2), ("bkr", 1), ("bkc", 7),
                 ("bv", 8), ("bo", 8)):
        b = np.asarray(inputs["L1_attn_" + k], np.float32)
        g[k] = np.ascontiguousarray(b.reshape(n, 128).T)
    g["w1T"] = np.ascontiguousarray(np.asarray(inputs["L1_ffn_w1"], np.float32).T)
    g["w2T"] = np.ascontiguousarray(np.asarray(inputs["L1_ffn_w2"], np.float32).T)
    cw = np.asarray(inputs["L1_ffn_conv_w"], np.float32).reshape(4096, 3)
    g["convw"] = np.ascontiguousarray(cw.reshape(32, 128, 3).transpose(1, 0, 2).reshape(128, 96))
    g["convb"] = _bias_cols(inputs["L1_ffn_conv_b"])
    return g


def build_l0(debug=(), phases=("attn", "gates", "mixer", "moe")):
    nc = bass.Bass("TRN2", target_bir_lowering=False)
    xT = nc.dram_tensor("xT", [D, S], F32, kind="ExternalInput").ap()
    projT = nc.dram_tensor("projT", [NH, D, HD], F32, kind="ExternalInput").ap()
    metric = nc.dram_tensor("metric", [NH, HD, HD], F32, kind="ExternalInput").ap()
    mixer_wT = nc.dram_tensor("mixer_wT", [D, D], F32, kind="ExternalInput").ap()
    mixer_b = nc.dram_tensor("mixer_b", [128, NCH], F32, kind="ExternalInput").ap()
    gate_wT = nc.dram_tensor("gate_wT", [D, 8], F32, kind="ExternalInput").ap()
    gate_b = nc.dram_tensor("gate_b", [1, 8], F32, kind="ExternalInput").ap()
    expert_wT = nc.dram_tensor("expert_wT", [8, D, D], F32, kind="ExternalInput").ap()
    expert_b = nc.dram_tensor("expert_b", [8, D], F32, kind="ExternalInput").ap()
    x1T = nc.dram_tensor("x1T", [D, S], F32, kind="ExternalOutput").ap()
    dbg = {}
    for name in debug:
        shp = {"xn": [D, S], "pfm": [D, S], "gacc": [64, 1024], "mg": [128, 512],
               "gates": [8, S]}[name]
        dbg[name] = nc.dram_tensor("dbg_" + name, shp, F32, kind="ExternalOutput").ap()

    with ExitStack() as es:
        sb = lambda name, shape, dt=F32, **kw: es.enter_context(nc.sbuf_tensor(name, shape, dt, **kw))
        x = sb("x_res", [128, NCH * S], F32)
        xn = sb("xn", [128, NCH * S], BF16)
        pfm = sb("pfm", [128, NCH * S], BF16)
        wbuf = sb("wbuf", [128, 16384], BF16)
        ptokb = sb("ptokb", [128, 2 * 1024], BF16)
        gacc = sb("gacc", [64, 1024], F32)
        gbf = sb("gbf", [64, 1024], BF16)
        metric_sb = sb("metric_sb", [64, 1024], BF16)
        mg = sb("mg", [128, 512], BF16)
        ones_bf = sb("ones_bf", [128, 128], BF16)
        ones_f = sb("ones_f", [128, 128], F32)
        ident_bf = sb("ident_bf", [128, 128], BF16)
        x2s = sb("x2s", [128, 512], BF16)
        scr = sb("scr", [128, 512], F32)
        rowm = sb("rowm", [1, 512], F32)
        rowe = sb("rowe", [1, 512], F32)
        rowt = sb("rowt", [1, 512], F32)
        ab = sb("ab", [128, 1024], F32)
        mixb_sb = sb("mixb_sb", [128, NCH], F32)
        gatew_bf = sb("gatew_bf", [128, 64], BF16)
        gb_bf = sb("gb_bf", [1, 8], BF16)
        gtok = sb("gtok", [128, 8], F32)
        gtok2 = sb("gtok2", [128, 8], F32)
        gtok_bf = sb("gtok_bf", [128, 8], BF16)
        col1 = sb("col1", [128, 1], F32)
        col2 = sb("col2", [128, 1], F32)
        col3 = sb("col3", [128, 1], F32)
        g_sb = sb("g_sb", [8, S], BF16)
        g_row = sb("g_row", [1, S], BF16)
        gbc = sb("gbc", [128, 2 * 512], BF16)
        xne = sb("xne", [128, NCH * 512], BF16)
        eb_bf = sb("eb_bf", [8, D], BF16)
        pb = [es.enter_context(nc.psum_tensor(f"pb{i}", [128, 512], F32)) for i in range(8)]

        p = Prog(nc, es)
        sW = p.sem("sW")
        sX = p.sem("sX")
        sPE = p.sem("sPE")
        sDVE = p.sem("sDVE")
        sACT = p.sem("sACT")
        sPOOL = p.sem("sPOOL")
        sWE = [p.sem("sWE0"), p.sem("sWE1")]
        banks = Banks(p, pb)
        block = es.enter_context(nc.Block())

        p.op("gpsimd", lambda e: e.memset(ones_f[:], 1.0))
        p.op("gpsimd", lambda e: e.tensor_copy(out=ones_bf[:], in_=ones_f[:]))
        p.op("gpsimd", lambda e: e.tensor_copy(out=ident_bf[:], in_=ones_f[:]))
        p.op("gpsimd", lambda e: e.affine_select(ident_bf[:], ident_bf[:], pattern=[[-1, 128]],
                                                 compare_op=ALU.is_equal, fill=0.0, base=0,
                                                 channel_multiplier=1))
        p.op("gpsimd", lambda e: e.memset(gacc[:], 0.0), inc=(sPOOL, 1))

        p.op("sync", lambda e: e.dma_start(
            out=x[:].rearrange("p (c t) -> p c t", c=NCH),
            in_=xT.rearrange("(c p) t -> p c t", p=128)), inc=(sX, 16))
        p.op("sync", lambda e: e.dma_start(out=mixb_sb[:], in_=mixer_b), inc=(sX, 16))
        x_loaded = sX.n

        for c in range(NCH):
            p.op("gpsimd", lambda e, c=c: e.dma_start(
                out=wbuf[:, c * 1024:(c + 1) * 1024].rearrange("p (h k) -> p h k", h=NH),
                in_=projT[:, c * 128:(c + 1) * 128, :].rearrange("h p k -> p h k")), inc=(sW, 16))
        p.op("gpsimd", lambda e: e.dma_start(
            out=metric_sb[:].rearrange("m (h l) -> m h l", h=NH),
            in_=metric.rearrange("h m l -> m h l")), inc=(sW, 16))
        p.op("gpsimd", lambda e: e.dma_start(
            out=gatew_bf[:].rearrange("p (c e) -> p c e", c=NCH),
            in_=gate_wT.rearrange("(c p) e -> p c e", p=128)), inc=(sW, 16))
        p.op("gpsimd", lambda e: e.dma_start(out=gb_bf[:], in_=gate_b), inc=(sW, 16))
        p.op("gpsimd", lambda e: e.dma_start(out=eb_bf[:], in_=expert_b), inc=(sW, 16))
        p.op("gpsimd", lambda e: e.dma_start(
            out=wbuf[:, 8192:16384].rearrange("p (c o) -> p c o", c=NCH),
            in_=mixer_wT.rearrange("(c p) o -> p c o", p=128)), inc=(sW, 16))
        weights_loaded = sW.n

        def proj_tile(c, h):
            return wbuf[:, (c * NH + h) * HD:(c * NH + h + 1) * HD]

        def mix_tile(c, oc):
            return wbuf[:, 8192 + c * D + oc * 128: 8192 + c * D + oc * 128 + 128]

        p.wait("vector", sX, x_loaded)
        p.wait("scalar", sX, x_loaded)
        p.wait("tensor", sPOOL, 1)
        for tt in range(NTT):
            t0 = tt * 512
            for c in range(NCH):
                p.op("vector", lambda e, c=c, t0=t0: e.tensor_copy(
                    out=xn[:, cs(c, t0, 512)], in_=x[:, cs(c, t0, 512)]),
                    inc=(sDVE, 1) if c == NCH - 1 else None)
            p.wait("tensor", sDVE)
            b0 = banks.acquire(0)
            b1 = banks.acquire(1)
            for c in range(NCH):
                p.wait("scalar", sPE)
                p.op("scalar", lambda e, c=c, t0=t0: e.activation(
                    out=x2s[:], in_=x[:, cs(c, t0, 512)], func=AF.Square), inc=(sACT, 1))
                p.op("tensor", lambda e, c=c, t0=t0, b0=b0: e.matmul(
                    b0[0:1, :], ones_bf[:, 0:1], xn[:, cs(c, t0, 512)],
                    start=(c == 0), stop=(c == NCH - 1)))
                p.wait("tensor", sACT)
                p.op("tensor", lambda e, c=c, b1=b1: e.matmul(
                    b1[0:1, :], ones_bf[:, 0:1], x2s[:],
                    start=(c == 0), stop=(c == NCH - 1)), inc=(sPE, 1))
            p.wait("vector", sPE)
            p.op("vector", lambda e, b0=b0: e.tensor_scalar(out=rowm[:], in0=b0[0:1, :],
                                                            scalar1=1.0 / D, scalar2=None, op0=ALU.mult))
            p.op("vector", lambda e, b1=b1: e.tensor_scalar(out=rowe[:], in0=b1[0:1, :],
                                                            scalar1=1.0 / D, scalar2=None, op0=ALU.mult),
                 inc=(sDVE, 1))
            banks.release(0, sDVE)
            banks.release(1, sDVE)
            p.op("vector", lambda e: e.tensor_mul(out=rowt[:], in0=rowm[:], in1=rowm[:]))
            p.op("vector", lambda e: e.tensor_sub(out=rowe[:], in0=rowe[:], in1=rowt[:]))
            p.op("vector", lambda e: e.tensor_scalar(out=rowe[:], in0=rowe[:], scalar1=1e-5,
                                                     scalar2=None, op0=ALU.add), inc=(sDVE, 1))
            p.wait("scalar", sDVE)
            p.op("scalar", lambda e: e.activation(out=rowt[:], in_=rowe[:], func=AF.Sqrt),
                 inc=(sACT, 1))
            p.wait("vector", sACT)
            p.op("vector", lambda e: e.reciprocal(out=rowt[:], in_=rowt[:]))
            p.op("vector", lambda e: e.tensor_mul(out=rowm[:], in0=rowm[:], in1=rowt[:]))
            p.op("vector", lambda e: e.tensor_scalar(out=rowm[:], in0=rowm[:], scalar1=-1.0,
                                                     scalar2=None, op0=ALU.mult), inc=(sDVE, 1))
            p.wait("tensor", sDVE)
            b2 = banks.acquire(2)
            b3 = banks.acquire(3)
            p.op("tensor", lambda e, b2=b2: e.matmul(b2[:, :], ones_f[0:1, :], rowt[:],
                                                     start=True, stop=True))
            p.op("tensor", lambda e, b3=b3: e.matmul(b3[:, :], ones_f[0:1, :], rowm[:],
                                                     start=True, stop=True), inc=(sPE, 1))
            p.wait("scalar", sPE)
            p.op("scalar", lambda e, b2=b2: e.copy(out=ab[:, 0:512], in_=b2[:, :]))
            p.op("scalar", lambda e, b3=b3: e.copy(out=ab[:, 512:1024], in_=b3[:, :]), inc=(sACT, 1))
            banks.release(2, sACT)
            banks.release(3, sACT)
            p.wait("vector", sACT)
            for c in range(NCH):
                p.op("vector", lambda e, c=c, t0=t0: e.tensor_mul(
                    out=scr[:], in0=x[:, cs(c, t0, 512)], in1=ab[:, 0:512]))
                p.op("vector", lambda e, c=c, t0=t0: e.tensor_add(
                    out=xn[:, cs(c, t0, 512)], in0=scr[:], in1=ab[:, 512:1024]),
                    inc=(sDVE, 1) if c == NCH - 1 else None)
        xn_done = sDVE.n

        ptok_mm_done = None
        nudged_done = None
        if "attn" in phases:
            p.wait("tensor", sW, weights_loaded)
            p.wait("tensor", sDVE, xn_done)
            for j in range(NCH):
                for tt in range(NTT):
                    b = 4 + ((j * NTT + tt) % 2)
                    pbx = banks.acquire(b)
                    for half in range(2):
                        h = 2 * j + half
                        for c in range(NCH):
                            p.op("tensor", lambda e, pbx=pbx, half=half, h=h, c=c, tt=tt: e.matmul(
                                pbx[half * 64:(half + 1) * 64, :], proj_tile(c, h),
                                xn[:, cs(c, tt * 512, 512)],
                                start=(c == 0), stop=(c == NCH - 1)),
                                inc=(sPE, 1) if (half == 1 and c == NCH - 1) else None)
                    p.wait("scalar", sPE)
                    p.op("scalar", lambda e, pbx=pbx, j=j, tt=tt: e.copy(
                        out=pfm[:, cs(j, tt * 512, 512)], in_=pbx[:, :]), inc=(sACT, 1))
                    banks.release(b, sACT)

            g_mm_done = {}
            for tc in range(NTC):
                pt = tc % 2
                pb6 = banks.acquire(6)
                pb7 = banks.acquire(7)
                for c in range(NCH):
                    p.op("tensor", lambda e, pb6=pb6, c=c, tc=tc: e.matmul(
                        pb6[:, :], xn[:, cs(c, tc * 128, 128)], wbuf[:, c * 1024: c * 1024 + 512],
                        start=(c == 0), stop=(c == NCH - 1)))
                    p.op("tensor", lambda e, pb7=pb7, c=c, tc=tc: e.matmul(
                        pb7[:, :], xn[:, cs(c, tc * 128, 128)], wbuf[:, c * 1024 + 512: c * 1024 + 1024],
                        start=(c == 0), stop=(c == NCH - 1)),
                        inc=(sPE, 1) if c == NCH - 1 else None)
                p.wait("scalar", sPE)
                if tc >= 2:
                    p.wait("scalar", sPE, g_mm_done[tc - 2])
                p.op("scalar", lambda e, pb6=pb6, pt=pt: e.copy(
                    out=ptokb[:, pt * 1024: pt * 1024 + 512], in_=pb6[:, :]))
                p.op("scalar", lambda e, pb7=pb7, pt=pt: e.copy(
                    out=ptokb[:, pt * 1024 + 512: pt * 1024 + 1024], in_=pb7[:, :]), inc=(sACT, 1))
                banks.release(6, sACT)
                banks.release(7, sACT)
                p.wait("tensor", sACT)
                b0 = banks.acquire(0)
                b1 = banks.acquire(1)
                for h in range(NH):
                    pbx = b0 if h < 8 else b1
                    p.op("tensor", lambda e, pbx=pbx, h=h, pt=pt: e.matmul(
                        pbx[0:64, (h % 8) * 64:(h % 8) * 64 + 64],
                        ptokb[:, pt * 1024 + h * 64: pt * 1024 + h * 64 + 64],
                        ptokb[:, pt * 1024 + h * 64: pt * 1024 + h * 64 + 64],
                        start=True, stop=True), inc=(sPE, 1) if h == NH - 1 else None)
                g_mm_done[tc] = sPE.n
                p.wait("vector", sPE)
                p.wait("vector", sPOOL, 1)
                p.op("vector", lambda e, b0=b0: e.tensor_add(out=gacc[0:64, 0:512],
                                                             in0=gacc[0:64, 0:512], in1=b0[0:64, :]))
                p.op("vector", lambda e, b1=b1: e.tensor_add(out=gacc[0:64, 512:1024],
                                                             in0=gacc[0:64, 512:1024], in1=b1[0:64, :]),
                     inc=(sDVE, 1))
                banks.release(0, sDVE)
                banks.release(1, sDVE)
            ptok_mm_done = sPE.n
            p.op("vector", lambda e: e.tensor_copy(out=gbf[:], in_=gacc[:]), inc=(sDVE, 1))
            g_done = sDVE.n

            p.wait("tensor", sW, weights_loaded)
            p.wait("tensor", sDVE, g_done)
            b2 = banks.acquire(2)
            b3 = banks.acquire(3)
            for h in range(NH):
                pbx = b2 if h < 8 else b3
                p.op("tensor", lambda e, pbx=pbx, h=h: e.matmul(
                    pbx[0:64, (h % 8) * 64:(h % 8) * 64 + 64],
                    metric_sb[:, h * 64: h * 64 + 64], gbf[:, h * 64: h * 64 + 64],
                    start=True, stop=True), inc=(sPE, 1) if h == NH - 1 else None)
            p.wait("scalar", sPE)
            for h in range(NH):
                pbx = b2 if h < 8 else b3
                p.op("scalar", lambda e, pbx=pbx, h=h: e.copy(
                    out=mg[(h % 2) * 64:(h % 2) * 64 + 64, (h // 2) * 64:(h // 2) * 64 + 64],
                    in_=pbx[0:64, (h % 8) * 64:(h % 8) * 64 + 64]),
                    inc=(sACT, 1) if h == NH - 1 else None)
            banks.release(2, sACT)
            banks.release(3, sACT)
            mg_done = sACT.n

            p.wait("tensor", sACT, mg_done)
            for h in range(NH):
                base = (h % 2) * 64
                j = h // 2
                for tt in range(NTT):
                    b = 2 + ((h * NTT + tt) % 2)
                    pbx = banks.acquire(b)
                    p.op("tensor", lambda e, pbx=pbx, j=j, tt=tt, base=base: e.matmul(
                        pbx[0:64, :], mg[base:base + 64, j * 64: j * 64 + 64],
                        pfm[base:base + 64, cs(j, tt * 512, 512)],
                        start=True, stop=True), inc=(sPE, 1))
                    p.wait("scalar", sPE)
                    p.op("scalar", lambda e, pbx=pbx, j=j, tt=tt, base=base: e.copy(
                        out=pfm[base:base + 64, cs(j, tt * 512, 512)], in_=pbx[0:64, :]),
                        inc=(sACT, 1))
                    banks.release(b, sACT)
            nudged_done = sACT.n

        gates_done = None
        if "gates" in phases:
            p.wait("tensor", sW, weights_loaded)
            p.wait("tensor", sDVE, xn_done)
            for tc in range(NTC):
                b = 4 + (tc % 2)
                pbx = banks.acquire(b)
                for c in range(NCH):
                    p.op("tensor", lambda e, pbx=pbx, c=c, tc=tc: e.matmul(
                        pbx[:, 0:8], xn[:, cs(c, tc * 128, 128)], gatew_bf[:, c * 8: c * 8 + 8],
                        start=(c == 0), stop=False))
                p.op("tensor", lambda e, pbx=pbx: e.matmul(
                    pbx[:, 0:8], ones_bf[0:1, :], gb_bf[:], start=False, stop=True), inc=(sPE, 1))
                p.wait("vector", sPE)
                p.op("vector", lambda e, pbx=pbx: e.reduce_max(out=col1[:], in_=pbx[:, 0:8], axis=AX,
                                                               negate=True), inc=(sDVE, 1))
                p.wait("scalar", sDVE)
                p.op("scalar", lambda e, pbx=pbx: e.activation(out=gtok[:], in_=pbx[:, 0:8],
                                                               func=AF.Exp, bias=col1[:],
                                                               accum_out=col2[:]), inc=(sACT, 1))
                banks.release(b, sACT)
                p.wait("vector", sACT)
                p.op("vector", lambda e: e.reciprocal(out=col2[:], in_=col2[:]), inc=(sDVE, 1))
                p.wait("vector", sDVE)
                p.op("vector", lambda e: e.tensor_scalar(out=gtok[:], in0=gtok[:], scalar1=col2[:],
                                                         scalar2=None, op0=ALU.mult), inc=(sDVE, 1))
                p.wait("vector", sDVE)
                p.op("vector", lambda e: e.reduce_max(out=col1[:], in_=gtok[:], axis=AX),
                     inc=(sDVE, 1))
                p.wait("vector", sDVE)
                p.op("vector", lambda e: e.tensor_scalar(out=gtok2[:], in0=gtok[:], scalar1=col1[:],
                                                         scalar2=None, op0=ALU.is_equal), inc=(sDVE, 1))
                p.wait("vector", sDVE)
                p.op("vector", lambda e: e.scalar_tensor_tensor(out=gtok2[:], in0=gtok2[:], scalar=-2.0,
                                                                in1=gtok[:], op0=ALU.mult, op1=ALU.add),
                     inc=(sDVE, 1))
                p.wait("vector", sDVE)
                p.op("vector", lambda e: e.reduce_max(out=col3[:], in_=gtok2[:], axis=AX),
                     inc=(sDVE, 1))
                p.wait("vector", sDVE)
                p.op("vector", lambda e: e.tensor_scalar(out=gtok2[:], in0=gtok[:], scalar1=col3[:],
                                                         scalar2=None, op0=ALU.is_ge), inc=(sDVE, 1))
                p.wait("vector", sDVE)
                p.op("vector", lambda e: e.tensor_mul(out=gtok[:], in0=gtok[:], in1=gtok2[:]),
                     inc=(sDVE, 1))
                p.wait("vector", sDVE)
                p.op("vector", lambda e: e.tensor_copy(out=gtok_bf[:], in_=gtok[:]), inc=(sDVE, 1))
                p.wait("tensor", sDVE)
                bT = 6 + (tc % 2)
                pbt = banks.acquire(bT)
                p.op("tensor", lambda e, pbt=pbt: e.matmul(pbt[0:8, 0:128], gtok_bf[:], ident_bf[:],
                                                           start=True, stop=True), inc=(sPE, 1))
                p.wait("scalar", sPE)
                p.op("scalar", lambda e, pbt=pbt, tc=tc: e.copy(out=g_sb[:, tc * 128:(tc + 1) * 128],
                                                                in_=pbt[0:8, 0:128]), inc=(sACT, 1))
                banks.release(bT, sACT)
            gates_done = sACT.n

        if "mixer" in phases:
            p.wait("tensor", sW, weights_loaded)
            p.wait("tensor", sACT, nudged_done)
            p.wait("vector", sX, x_loaded)
            for oc in range(NCH):
                for tt in range(NTT):
                    b = (oc * NTT + tt) % 4
                    pbx = banks.acquire(b)
                    for c in range(NCH):
                        p.op("tensor", lambda e, pbx=pbx, c=c, oc=oc, tt=tt: e.matmul(
                            pbx[:, :], mix_tile(c, oc), pfm[:, cs(c, tt * 512, 512)],
                            start=(c == 0), stop=(c == NCH - 1)),
                            inc=(sPE, 1) if c == NCH - 1 else None)
                    p.wait("vector", sPE)
                    p.op("vector", lambda e, pbx=pbx, oc=oc, tt=tt: e.scalar_tensor_tensor(
                        out=x[:, cs(oc, tt * 512, 512)], in0=pbx[:, :], scalar=mixb_sb[:, oc:oc + 1],
                        in1=x[:, cs(oc, tt * 512, 512)], op0=ALU.add, op1=ALU.add), inc=(sDVE, 1))
                    banks.release(b, sDVE)
        pe_after_mixer = sPE.n

        if "moe" in phases:
            expert_bufs = [wbuf[:, 0:8192], wbuf[:, 8192:16384]]

            def ebuf_tile(ebuf, c, oc):
                return ebuf[:, c * D + oc * 128: c * D + oc * 128 + 128]

            w_loaded = {}
            p.wait("gpsimd", sPE, ptok_mm_done if ptok_mm_done is not None else 0)
            p.op("gpsimd", lambda e: e.dma_start(
                out=expert_bufs[0].rearrange("p (c o) -> p c o", c=NCH),
                in_=expert_wT[0].rearrange("(c p) o -> p c o", p=128)), inc=(sWE[0], 16))
            w_loaded[0] = (0, sWE[0].n)
            p.wait("gpsimd", sPE, pe_after_mixer)
            p.op("gpsimd", lambda e: e.dma_start(
                out=expert_bufs[1].rearrange("p (c o) -> p c o", c=NCH),
                in_=expert_wT[1].rearrange("(c p) o -> p c o", p=128)), inc=(sWE[1], 16))
            w_loaded[1] = (1, sWE[1].n)
            if gates_done:
                p.wait("sync", sACT, gates_done)
            grow_loaded = {}
            p.op("sync", lambda e: e.dma_start(out=g_row[:], in_=g_sb[0:1, :]), inc=(sX, 16))
            grow_loaded[0] = sX.n

            p.wait("vector", sDVE)
            last_xne_reader = {0: 0, 1: 0}
            last_gbc_reader = {0: 0, 1: 0}
            last_grow_reader = {0: pe_after_mixer}
            for e_i in range(8):
                ebuf = expert_bufs[e_i % 2]
                p.wait("tensor", sWE[w_loaded[e_i][0]], w_loaded[e_i][1])
                p.wait("tensor", sX, grow_loaded[e_i])
                for tt in range(NTT):
                    par = tt % 2
                    bg = par
                    pbg = banks.acquire(bg)
                    p.op("tensor", lambda e, pbg=pbg, e_i=e_i, tt=tt: e.matmul(
                        pbg[:, :], ones_bf[0:1, :], g_row[0:1, tt * 512:tt * 512 + 512],
                        start=True, stop=True), inc=(sPE, 1))
                    last_grow_reader[0] = sPE.n
                    p.wait("scalar", sPE)
                    p.wait("scalar", sDVE, last_gbc_reader[par])
                    p.op("scalar", lambda e, pbg=pbg, par=par: e.copy(
                        out=gbc[:, par * 512:par * 512 + 512], in_=pbg[:, :]), inc=(sACT, 1))
                    banks.release(bg, sACT)
                    p.wait("vector", sACT)
                    p.wait("vector", sPE, last_xne_reader[0])
                    for c in range(NCH):
                        p.op("vector", lambda e, c=c, tt=tt, par=par: e.tensor_mul(
                            out=xne[:, c * 512: c * 512 + 512],
                            in0=xn[:, cs(c, tt * 512, 512)],
                            in1=gbc[:, par * 512:par * 512 + 512]),
                            inc=(sDVE, 1) if c == NCH - 1 else None)
                    last_gbc_reader[par] = sDVE.n
                    p.wait("tensor", sDVE)
                    for oc in range(NCH):
                        pbx = banks.acquire(oc)
                        for c in range(NCH):
                            p.op("tensor", lambda e, pbx=pbx, ebuf=ebuf, c=c, oc=oc, par=par, e_i=e_i: e.matmul(
                                pbx[:, :], ebuf_tile(ebuf, c, oc),
                                xne[:, c * 512: c * 512 + 512],
                                start=(c == 0), stop=(c == NCH - 1 and e_i != 0)),
                                inc=(sPE, 1) if (c == NCH - 1 and e_i != 0) else None)
                        if e_i == 0:
                            p.op("tensor", lambda e, pbx=pbx, oc=oc, tt=tt: e.matmul(
                                pbx[:, :], eb_bf[:, oc * 128:oc * 128 + 128],
                                g_sb[:, tt * 512:tt * 512 + 512],
                                start=False, stop=True), inc=(sPE, 1))
                        p.wait("vector", sPE)
                        p.op("vector", lambda e, pbx=pbx, oc=oc, tt=tt: e.tensor_add(
                            out=x[:, cs(oc, tt * 512, 512)], in0=x[:, cs(oc, tt * 512, 512)],
                            in1=pbx[:, :]), inc=(sDVE, 1))
                        banks.release(oc, sDVE)
                    last_xne_reader[0] = sPE.n
                if e_i + 2 < 8:
                    p.wait("gpsimd", sPE, sPE.n)
                    p.op("gpsimd", lambda e, e_i=e_i: e.dma_start(
                        out=expert_bufs[e_i % 2].rearrange("p (c o) -> p c o", c=NCH),
                        in_=expert_wT[e_i + 2].rearrange("(c p) o -> p c o", p=128)),
                        inc=(sWE[e_i % 2], 16))
                    w_loaded[e_i + 2] = (e_i % 2, sWE[e_i % 2].n)
                if e_i + 1 < 8:
                    p.wait("sync", sPE, last_grow_reader[0])
                    p.op("sync", lambda e, e_i=e_i: e.dma_start(
                        out=g_row[:], in_=g_sb[e_i + 1:e_i + 2, :]), inc=(sX, 16))
                    grow_loaded[e_i + 1] = sX.n

        p.wait("sync", sDVE)
        p.wait("sync", sACT)
        p.op("sync", lambda e: e.dma_start(
            out=x1T.rearrange("(c p) t -> p c t", p=128),
            in_=x[:].rearrange("p (c t) -> p c t", c=NCH)), inc=(sX, 16))
        if dbg:
            p.wait("gpsimd", sDVE)
            p.wait("gpsimd", sACT)
        if "xn" in dbg:
            p.op("gpsimd", lambda e: e.dma_start(
                out=dbg["xn"].rearrange("(c p) t -> p c t", p=128),
                in_=xn[:].rearrange("p (c t) -> p c t", c=NCH)), inc=(sW, 16))
        if "pfm" in dbg:
            p.op("gpsimd", lambda e: e.dma_start(
                out=dbg["pfm"].rearrange("(c p) t -> p c t", p=128),
                in_=pfm[:].rearrange("p (c t) -> p c t", c=NCH)), inc=(sW, 16))
        if "gacc" in dbg:
            p.op("gpsimd", lambda e: e.dma_start(out=dbg["gacc"], in_=gacc[:]), inc=(sW, 16))
        if "mg" in dbg:
            p.op("gpsimd", lambda e: e.dma_start(out=dbg["mg"], in_=mg[:]), inc=(sW, 16))
        if "gates" in dbg:
            p.op("gpsimd", lambda e: e.dma_start(out=dbg["gates"], in_=g_sb[:]), inc=(sW, 16))
        p.wait("sync", sW)
        p.wait("sync", sX)

        p.finalize(block)
    return nc


def l0_in_map(g, xb):
    return {
        "xT": np.ascontiguousarray(xb.T),
        "projT": g["projT"], "metric": g["metric"],
        "mixer_wT": g["mixer_wT"], "mixer_b": g["mixer_b"],
        "gate_wT": g["gate_wT"], "gate_b": g["gate_b"],
        "expert_wT": g["expert_wT"], "expert_b": g["expert_b"],
    }


import math


def build_l1(debug=(), phases=("attn", "ffn")):
    nc = bass.Bass("TRN2", target_bir_lowering=False)
    x1T = nc.dram_tensor("x1T", [D, S], F32, kind="ExternalInput").ap()
    W = {}
    for name, shape in [("wq", [D, 256]), ("bq", [128, 2]), ("wqc", [256, 896]), ("bqc", [128, 7]),
                        ("wqr", [D, 128]), ("bqr", [128, 1]), ("wkv", [D, 256]), ("bkv", [128, 2]),
                        ("wkr", [D, 128]), ("bkr", [128, 1]), ("wkc", [256, 896]), ("bkc", [128, 7]),
                        ("wv", [256, D]), ("bv", [128, 8]), ("wo", [D, D]), ("bo", [128, 8]),
                        ("w1T", [D, 4096]), ("convw", [128, 96]), ("convb", [128, 32]),
                        ("w2T", [4096, D])]:
        W[name] = nc.dram_tensor(name, shape, F32, kind="ExternalInput").ap()
    outT = nc.dram_tensor("outT", [D, S], F32, kind="ExternalOutput").ap()
    dbg = {}
    for name in debug:
        shp = {"xn": [D, S], "q": [D, S], "k": [D, S], "attnfm": [D, S], "xq": [D, S],
               "s0": [128, 32 * 1024], "mrow": [1, S], "woacc": [128, 4096],
               "bprime": [128, 8], "h0": [128, 1025], "w1q": [128, 2048],
               "w2q": [128, 4096]}[name]
        dbg[name] = nc.dram_tensor("dbg_" + name, shp, F32, kind="ExternalOutput").ap()

    with ExitStack() as es:
        sb = lambda name, shape, dt=F32, **kw: es.enter_context(nc.sbuf_tensor(name, shape, dt, **kw))
        xn = sb("xn", [128, NCH * S], BF16)
        arena = sb("arena", [128, 49664], BF16)
        Q = arena[:, 0:16384]
        Kb = arena[:, 16384:32768]
        Vb = arena[:, 32768:49408]
        xq = arena[:, 0:16384]
        s_sb = arena[:, 16384:49152]
        arena2 = sb("arena2", [128, 8192], BF16)
        cq = arena2[:, 0:4096]
        ckv = arena2[:, 4096:8192]
        wo_sb = arena2[:, 0:8192]
        w1q = arena2[:, 0:2048]
        w2q = arena2[:, 2048:6144]
        hbuf = arena2[:, 6144:7169]
        arena3 = sb("arena3", [128, 8192], F32)
        cosT = arena3[:, 0:2048]
        sinT = arena3[:, 2048:4096]
        awbuf = arena3[:, 4096:8192].bitcast(BF16)
        pwq = awbuf[:, 0:2048]
        pwkv = awbuf[:, 2048:4096]
        pwqc = awbuf[:, 4096:5888]
        pwkc = awbuf[:, 5888:7680]
        qrS = arena3[:, 4096:6144]
        rotS = arena3[:, 6144:8192]
        attn_out = arena3[:].bitcast(BF16)
        inv_bc = arena3[:, 0:2048]
        sc_bc = arena3[:, 2048:4096]
        sc2_bc = arena3[:, 2048:3072]
        inv2_bc = arena3[:, 3072:4096]
        mfold = arena3[:, 4096:6144]
        ubuf = arena3[:, 4096:5120]
        ystg = arena3[:, 6144:8192]
        xstg = sb("xstg", [128, 4096], F32)
        wqr_sb = sb("wqr_sb", [128, 1024], BF16)
        wkr_sb = sb("wkr_sb", [128, 1024], BF16)
        wv_sb = sb("wv_sb", [128, 2048], BF16)
        bias_sb = {}
        for name, n in (("bq", 2), ("bqc", 7), ("bqr", 1), ("bkv", 2), ("bkr", 1), ("bkc", 7),
                        ("bv", 8), ("bo", 8), ("convw", 96), ("convb", 32)):
            bias_sb[name] = sb("b_" + name, [128, n], F32)
        bv_bf = sb("bv_bf", [128, 8], BF16)
        bprime = sb("bprime", [128, 8], F32)
        ones_bf = sb("ones_bf", [128, 128], BF16)
        ones_f = sb("ones_f", [128, 128], F32)
        ident_bf = sb("ident_bf", [128, 128], BF16)
        tri_bf = sb("tri_bf", [128, 128], BF16)
        hpi = sb("hpi", [64, 1], F32)
        invf = sb("invf", [64, 1], F32)
        x2s = arena[:, 1024:1536]
        scr = arena[:, 0:1024].bitcast(F32)
        rows = arena[:, 3584:6656].bitcast(F32)
        rowm = rows[0:1, 0:512]
        rowe = rows[0:1, 512:1024]
        rowt = rows[0:1, 1024:1536]
        ab = arena[:, 1536:3584].bitcast(F32)
        ptb = sb("ptb", [128, 3 * 512], BF16)
        rrow = xstg[0:1, 3584:4096]
        rstg = xstg[0:64, 2560:3072]
        scrA = xstg[0:64, 2048:2304].bitcast(BF16)
        qmaxc = sb("qmaxc", [128, 9], F32)
        qkmax = sb("qkmax", [128, 2], BF16)
        negb = sb("negb", [128, 1], F32)
        brow = sb("brow", [1, 4], F32)
        wcols = sb("wcols", [128, 33], F32)
        wtot_bf = sb("wtot_bf", [128, 1], BF16)
        winv = sb("winv", [1, 2], F32)
        wscal = sb("wscal", [1, 2], F32)
        swcol = sb("swcol", [128, 2], F32)
        mcol = sb("mcol", [128, 1], F32)
        mcol_bf = sb("mcol_bf", [128, 1], BF16)
        pb = [es.enter_context(nc.psum_tensor(f"pb{i}", [128, 512], F32)) for i in range(8)]

        p = Prog(nc, es)
        sW = p.sem("sW")
        sX = p.sem("sX")
        sPE = p.sem("sPE")
        sDVE = p.sem("sDVE")
        sACT = p.sem("sACT")
        sPOOL = p.sem("sPOOL")
        banks = Banks(p, pb)
        block = es.enter_context(nc.Block())

        p.op("gpsimd", lambda e: e.memset(ones_f[:], 1.0))
        p.op("gpsimd", lambda e: e.tensor_copy(out=ones_bf[:], in_=ones_f[:]))
        p.op("gpsimd", lambda e: e.tensor_copy(out=ident_bf[:], in_=ones_f[:]))
        p.op("gpsimd", lambda e: e.affine_select(ident_bf[:], ident_bf[:], pattern=[[-1, 128]],
                                                 compare_op=ALU.is_equal, fill=0.0, base=0,
                                                 channel_multiplier=1))
        p.op("gpsimd", lambda e: e.tensor_copy(out=tri_bf[:], in_=ones_f[:]))
        p.op("gpsimd", lambda e: e.affine_select(tri_bf[:], tri_bf[:], pattern=[[1, 128]],
                                                 compare_op=ALU.is_ge, fill=0.0, base=0,
                                                 channel_multiplier=-1))
        p.op("gpsimd", lambda e: e.memset(hpi[:], math.pi / 2))
        p.op("gpsimd", lambda e: e.iota(invf[:], pattern=[[0, 1]], base=0, channel_multiplier=1,
                                        allow_small_or_imprecise_dtypes=True))
        p.op("gpsimd", lambda e: e.iota(xstg[0:64, 0:2048], pattern=[[1, 2048]], base=0,
                                        channel_multiplier=0,
                                        allow_small_or_imprecise_dtypes=True), inc=(sPOOL, 1))
        p.wait("scalar", sPOOL, 1)
        p.op("scalar", lambda e: e.activation(out=invf[:], in_=invf[:], func=AF.Exp,
                                              scale=-math.log(10000.0) / 64.0), inc=(sACT, 1))
        p.wait("vector", sACT)
        tmpA = xstg[0:64, 0:2048]
        tmpB = xstg[0:64, 2048:4096]
        p.op("vector", lambda e: e.tensor_scalar(out=tmpA, in0=tmpA, scalar1=invf[0:64, 0:1],
                                                 scalar2=None, op0=ALU.mult), inc=(sDVE, 1))
        p.wait("vector", sDVE)
        p.op("vector", lambda e: e.tensor_scalar(out=tmpB, in0=tmpA, scalar1=1.0 / (2 * math.pi),
                                                 scalar2=MAGIC, op0=ALU.mult, op1=ALU.add))
        p.op("vector", lambda e: e.tensor_scalar(out=tmpB, in0=tmpB, scalar1=MAGIC,
                                                 scalar2=None, op0=ALU.subtract))
        p.op("vector", lambda e: e.scalar_tensor_tensor(out=tmpA, in0=tmpB, scalar=-2 * math.pi,
                                                        in1=tmpA, op0=ALU.mult, op1=ALU.add),
             inc=(sDVE, 1))
        p.wait("scalar", sDVE)
        p.op("scalar", lambda e: e.activation(out=tmpB, in_=tmpA, func=AF.Sin, scale=0.5))
        p.op("scalar", lambda e: e.activation(out=sinT[0:64, :], in_=tmpA, func=AF.Sin,
                                              scale=-0.5, bias=hpi[:, 0:1]))
        p.op("scalar", lambda e: e.activation(out=cosT[0:64, :], in_=tmpB, func=AF.Square),
             inc=(sACT, 1))
        p.wait("vector", sACT)
        p.op("vector", lambda e: e.scalar_tensor_tensor(out=sinT[0:64, :], in0=tmpB, scalar=2.0,
                                                        in1=sinT[0:64, :], op0=ALU.mult,
                                                        op1=ALU.mult))
        p.op("vector", lambda e: e.tensor_scalar(out=cosT[0:64, :], in0=cosT[0:64, :],
                                                 scalar1=-2.0, scalar2=1.0,
                                                 op0=ALU.mult, op1=ALU.add), inc=(sDVE, 1))
        p.wait("sync", sDVE)
        p.op("sync", lambda e: e.dma_start(out=cosT[64:128, :], in_=cosT[0:64, :]), inc=(sX, 16))
        p.op("sync", lambda e: e.dma_start(out=sinT[64:128, :], in_=sinT[0:64, :]), inc=(sX, 16))
        p.wait("vector", sX)
        p.op("vector", lambda e: e.tensor_scalar(out=sinT[0:64, :], in0=sinT[0:64, :],
                                                 scalar1=-1.0, scalar2=None, op0=ALU.mult),
             inc=(sDVE, 1))
        rope_done = sDVE.n

        p.op("gpsimd", lambda e: e.dma_start(out=outT, in_=x1T), inc=(sW, 16))
        out_init = sW.n
        for name, dst, c in [("wq", pwq, NCH), ("wkv", pwkv, NCH), ("wqc", pwqc, 2),
                             ("wkc", pwkc, 2), ("wqr", wqr_sb, NCH), ("wkr", wkr_sb, NCH),
                             ("wv", wv_sb, 2)]:
            p.op("gpsimd", lambda e, name=name, dst=dst, c=c: e.dma_start(
                out=dst.rearrange("p (c o) -> p c o", c=c),
                in_=W[name].rearrange("(c p) o -> p c o", p=128)), inc=(sW, 16))
        attw_loaded = sW.n
        sB = p.sem("sB")
        for name in ("bq", "bqc", "bqr", "bkv", "bkr", "bkc", "bv", "bo", "convw", "convb"):
            p.op("sync", lambda e, name=name: e.dma_start(out=bias_sb[name][:], in_=W[name]),
                 inc=(sB, 16))
        bias_loaded = sB.n

        xstg_free = (sDVE, rope_done)
        for tt in range(NTT):
            sem0, cnt0 = xstg_free
            p.wait("sync", sem0, cnt0)
            p.op("sync", lambda e, tt=tt: e.dma_start(
                out=xstg[:].rearrange("p (c t) -> p c t", c=NCH),
                in_=x1T.rearrange("(c p) t -> p c t", p=128)[:, :, tt * 512:(tt + 1) * 512]),
                inc=(sX, 16))
            xld = sX.n

            def xs(c):
                return xstg[:, c * 512: c * 512 + 512]

            p.wait("vector", sX, xld)
            p.wait("scalar", sX, xld)
            for c in range(NCH):
                p.op("vector", lambda e, c=c, tt=tt: e.tensor_copy(
                    out=xn[:, cs(c, tt * 512, 512)], in_=xs(c)),
                    inc=(sDVE, 1) if c == NCH - 1 else None)
            p.wait("tensor", sDVE)
            b0 = banks.acquire(0)
            b1 = banks.acquire(1)
            for c in range(NCH):
                p.wait("scalar", sPE)
                p.op("scalar", lambda e, c=c: e.activation(
                    out=x2s[:], in_=xs(c), func=AF.Square), inc=(sACT, 1))
                p.op("tensor", lambda e, c=c, tt=tt, b0=b0: e.matmul(
                    b0[0:1, :], ones_bf[:, 0:1], xn[:, cs(c, tt * 512, 512)],
                    start=(c == 0), stop=(c == NCH - 1)))
                p.wait("tensor", sACT)
                p.op("tensor", lambda e, c=c, b1=b1: e.matmul(
                    b1[0:1, :], ones_bf[:, 0:1], x2s[:],
                    start=(c == 0), stop=(c == NCH - 1)), inc=(sPE, 1))
            p.wait("vector", sPE)
            p.op("vector", lambda e, b0=b0: e.tensor_scalar(out=rowm, in0=b0[0:1, :],
                                                            scalar1=1.0 / D, scalar2=None, op0=ALU.mult))
            p.op("vector", lambda e, b1=b1: e.tensor_scalar(out=rowe, in0=b1[0:1, :],
                                                            scalar1=1.0 / D, scalar2=None, op0=ALU.mult),
                 inc=(sDVE, 1))
            banks.release(0, sDVE)
            banks.release(1, sDVE)
            p.op("vector", lambda e: e.tensor_mul(out=rowt, in0=rowm, in1=rowm))
            p.op("vector", lambda e: e.tensor_sub(out=rowe, in0=rowe, in1=rowt))
            p.op("vector", lambda e: e.tensor_scalar(out=rowe, in0=rowe, scalar1=1e-5,
                                                     scalar2=None, op0=ALU.add), inc=(sDVE, 1))
            p.wait("scalar", sDVE)
            p.op("scalar", lambda e: e.activation(out=rowt, in_=rowe, func=AF.Sqrt),
                 inc=(sACT, 1))
            p.wait("vector", sACT)
            p.op("vector", lambda e: e.reciprocal(out=rowt, in_=rowt))
            p.op("vector", lambda e: e.tensor_mul(out=rowm, in0=rowm, in1=rowt))
            p.op("vector", lambda e: e.tensor_scalar(out=rowm, in0=rowm, scalar1=-1.0,
                                                     scalar2=None, op0=ALU.mult), inc=(sDVE, 1))
            p.wait("tensor", sDVE)
            b2 = banks.acquire(2)
            b3 = banks.acquire(3)
            p.op("tensor", lambda e, b2=b2: e.matmul(b2[:, :], ones_f[0:1, :], rowt,
                                                     start=True, stop=True))
            p.op("tensor", lambda e, b3=b3: e.matmul(b3[:, :], ones_f[0:1, :], rowm,
                                                     start=True, stop=True), inc=(sPE, 1))
            p.wait("scalar", sPE)
            p.op("scalar", lambda e, b2=b2: e.copy(out=ab[:, 0:512], in_=b2[:, :]))
            p.op("scalar", lambda e, b3=b3: e.copy(out=ab[:, 512:1024], in_=b3[:, :]), inc=(sACT, 1))
            banks.release(2, sACT)
            banks.release(3, sACT)
            p.wait("vector", sACT)
            for c in range(NCH):
                p.op("vector", lambda e, c=c, tt=tt: e.tensor_mul(
                    out=scr[:], in0=xs(c), in1=ab[:, 0:512]))
                p.op("vector", lambda e, c=c, tt=tt: e.tensor_add(
                    out=xn[:, cs(c, tt * 512, 512)], in0=scr[:], in1=ab[:, 512:1024]),
                    inc=(sDVE, 1) if c == NCH - 1 else None)
            xstg_free = (sDVE, sDVE.n)
        ln_done = sDVE.n

        attn_accum_done = 0
        pe_attn_end = 0
        if "attn" in phases:
            p.wait("tensor", sW, attw_loaded)
            p.wait("tensor", sDVE, ln_done)
            p.wait("scalar", sB, bias_loaded)

            def proj_block(dst_fn, w_tile_fn, nin_ch, bias_name, noc, func, src_fn):
                for oc in range(noc):
                    for tt in range(NTT):
                        b = (oc * NTT + tt) % 4
                        pbx = banks.acquire(b)
                        for c in range(nin_ch):
                            p.op("tensor", lambda e, pbx=pbx, c=c, oc=oc, tt=tt: e.matmul(
                                pbx[:, :], w_tile_fn(c, oc), src_fn(c, tt),
                                start=(c == 0), stop=(c == nin_ch - 1)),
                                inc=(sPE, 1) if c == nin_ch - 1 else None)
                        p.wait("scalar", sPE)
                        p.op("scalar", lambda e, pbx=pbx, oc=oc, tt=tt: e.activation(
                            out=dst_fn(oc, tt), in_=pbx[:, :], func=func,
                            bias=bias_sb[bias_name][:, oc:oc + 1]), inc=(sACT, 1))
                        banks.release(b, sACT)

            def xn_src(c, tt):
                return xn[:, cs(c, tt * 512, 512)]

            def sl(buf, oc, tt):
                return buf[:, oc * 2048 + tt * 512: oc * 2048 + tt * 512 + 512]

            proj_block(lambda oc, tt: sl(cq, oc, tt),
                       lambda c, oc: pwq[:, c * 256 + oc * 128: c * 256 + oc * 128 + 128],
                       NCH, "bq", 2, AF.Gelu, xn_src)
            proj_block(lambda oc, tt: sl(ckv, oc, tt),
                       lambda c, oc: pwkv[:, c * 256 + oc * 128: c * 256 + oc * 128 + 128],
                       NCH, "bkv", 2, AF.Gelu, xn_src)
            cqkv_done = sACT.n

            p.wait("tensor", sACT, cqkv_done)
            proj_block(lambda oc, tt: sl(Q, oc, tt),
                       lambda c, oc: pwqc[:, c * 896 + oc * 128: c * 896 + oc * 128 + 128],
                       2, "bqc", 7, AF.Gelu, lambda c, tt: sl(cq, c, tt))
            proj_block(lambda oc, tt: sl(Kb, oc, tt),
                       lambda c, oc: pwkc[:, c * 896 + oc * 128: c * 896 + oc * 128 + 128],
                       2, "bkc", 7, AF.Gelu, lambda c, tt: sl(ckv, c, tt))

            Vv = Vb.rearrange("p (t h o) -> p t h o", t=NTC, h=NH)
            p.op("gpsimd", lambda e: e.memset(Vv[:, :, :, 64:65], 1.0), inc=(sPOOL, 1))
            for tc in range(NTC):
                for n2 in range(2):
                    b = 4 + (tc * 2 + n2) % 2
                    pbx = banks.acquire(b)
                    for c in range(2):
                        p.op("tensor", lambda e, pbx=pbx, c=c, tc=tc, n2=n2: e.matmul(
                            pbx[:, :], ckv[:, c * 2048 + tc * 128: c * 2048 + tc * 128 + 128],
                            wv_sb[:, c * 1024 + n2 * 512: c * 1024 + n2 * 512 + 512],
                            start=(c == 0), stop=(c == 1)), inc=(sPE, 1) if c == 1 else None)
                    p.wait("scalar", sPE)
                    p.op("scalar", lambda e, pbx=pbx, tc=tc, n2=n2: e.activation(
                        out=Vv[:, tc, n2 * 8:(n2 + 1) * 8, 0:64],
                        in_=pbx[:, :].rearrange("p (h o) -> p h o", h=8), func=AF.Copy),
                        inc=(sACT, 1))
                    banks.release(b, sACT)
            v_done = sACT.n
            qkv_mm_done = sPE.n

            p.wait("vector", sDVE, rope_done)
            for which, w_sb, bias_name, dstc in (("q", wqr_sb, "bqr", Q), ("k", wkr_sb, "bkr", Kb)):
                for tt in range(NTT):
                    b = (tt % 2)
                    pbx = banks.acquire(b)
                    for c in range(NCH):
                        p.op("tensor", lambda e, pbx=pbx, c=c, tt=tt, w_sb=w_sb: e.matmul(
                            pbx[:, :], w_sb[:, c * 128: c * 128 + 128], xn_src(c, tt),
                            start=(c == 0), stop=(c == NCH - 1)),
                            inc=(sPE, 1) if c == NCH - 1 else None)
                    if which == "q" and tt == 0:
                        p.wait("scalar", sPE, qkv_mm_done)
                    p.wait("scalar", sPE)
                    p.op("scalar", lambda e, pbx=pbx, tt=tt, bias_name=bias_name: e.activation(
                        out=qrS[:, tt * 512:(tt + 1) * 512], in_=pbx[:, :], func=AF.Identity,
                        bias=bias_sb[bias_name][:, 0:1]), inc=(sACT, 1))
                    banks.release(b, sACT)
                p.wait("sync", sACT)
                p.op("sync", lambda e: e.dma_start(out=rotS[0:64, :], in_=qrS[64:128, :]),
                     inc=(sX, 16))
                p.op("sync", lambda e: e.dma_start(out=rotS[64:128, :], in_=qrS[0:64, :]),
                     inc=(sX, 16))
                p.wait("vector", sX)
                p.wait("vector", sACT)
                p.op("vector", lambda e: e.tensor_mul(out=qrS[:], in0=qrS[:], in1=cosT[:]))
                p.op("vector", lambda e: e.tensor_mul(out=rotS[:], in0=rotS[:], in1=sinT[:]))
                p.op("vector", lambda e: e.tensor_add(out=qrS[:], in0=qrS[:], in1=rotS[:]),
                     inc=(sDVE, 1))
                p.wait("scalar", sDVE)
                for tt in range(NTT):
                    p.op("scalar", lambda e, tt=tt, dstc=dstc: e.activation(
                        out=dstc[:, 7 * 2048 + tt * 512: 7 * 2048 + tt * 512 + 512],
                        in_=qrS[:, tt * 512:(tt + 1) * 512], func=AF.Gelu),
                        inc=(sACT, 1) if tt == NTT - 1 else None)
            qk_done = sACT.n

            p.wait("vector", sACT, qk_done)
            for i, buf in enumerate((Q, Kb)):
                for c in range(NCH):
                    p.op("vector", lambda e, c=c, buf=buf: e.tensor_reduce(
                        out=qmaxc[:, c:c + 1], in_=buf[:, c * 2048:(c + 1) * 2048], axis=AX,
                        op=ALU.max, apply_absolute_value=True),
                        inc=(sDVE, 1) if c == NCH - 1 else None)
                p.wait("vector", sDVE)
                p.op("vector", lambda e: e.tensor_reduce(out=qmaxc[:, 8:9], in_=qmaxc[:, 0:8],
                                                         axis=AX, op=ALU.max), inc=(sDVE, 1))
                p.wait("vector", sDVE)
                p.op("vector", lambda e, i=i: e.tensor_copy(out=qkmax[:, i:i + 1],
                                                            in_=qmaxc[:, 8:9]), inc=(sDVE, 1))
                p.wait("vector", sDVE)
            p.wait("tensor", sDVE)
            b0 = banks.acquire(0)
            p.op("tensor", lambda e, b0=b0: e.matmul(b0[0:1, 0:128], qkmax[:, 0:1], ident_bf[:],
                                                     start=True, stop=True))
            p.op("tensor", lambda e, b0=b0: e.matmul(b0[0:1, 128:256], qkmax[:, 1:2], ident_bf[:],
                                                     start=True, stop=True), inc=(sPE, 1))
            p.wait("vector", sPE)
            p.op("vector", lambda e, b0=b0: e.reduce_max(out=brow[:, 0:1], in_=b0[0:1, 0:128],
                                                         axis=AX))
            p.op("vector", lambda e, b0=b0: e.reduce_max(out=brow[:, 1:2], in_=b0[0:1, 128:256],
                                                         axis=AX), inc=(sDVE, 1))
            banks.release(0, sDVE)
            p.wait("vector", sDVE)
            p.op("vector", lambda e: e.tensor_mul(out=brow[:, 2:3], in0=brow[:, 0:1],
                                                  in1=brow[:, 1:2]), inc=(sDVE, 1))
            p.wait("vector", sDVE)
            p.op("vector", lambda e: e.tensor_scalar(out=brow[:, 2:3], in0=brow[:, 2:3],
                                                     scalar1=-8.0, scalar2=None, op0=ALU.mult),
                 inc=(sDVE, 1))
            p.wait("tensor", sDVE)
            b0 = banks.acquire(0)
            p.op("tensor", lambda e, b0=b0: e.matmul(b0[:, 0:1], ones_f[0:1, :], brow[:, 2:3],
                                                     start=True, stop=True), inc=(sPE, 1))
            p.wait("scalar", sPE)
            p.op("scalar", lambda e, b0=b0: e.copy(out=negb[:], in_=b0[:, 0:1]), inc=(sACT, 1))
            banks.release(0, sACT)
            bound_done = sACT.n

            p.wait("tensor", sACT, bound_done)
            p.wait("tensor", sACT, v_done)
            p.wait("tensor", sPOOL, 2)
            ptb_reader = [0, 0, 0]
            scrA_reader = 0
            for h in range(NH):
                base = (h % 2) * 64
                jc = h // 2
                for qg in range(NTT):
                    nkt = 4 * qg + 4
                    bO = 4 + (h * NTT + qg) % 2
                    pbO = banks.acquire(bO)
                    for kt in range(nkt):
                        bS = (kt % 2)
                        pbS = banks.acquire(bS)
                        p.op("tensor", lambda e, pbS=pbS, base=base, jc=jc, kt=kt, qg=qg: e.matmul(
                            pbS[:, :],
                            Kb[base:base + 64, jc * 2048 + kt * 128: jc * 2048 + kt * 128 + 128],
                            Q[base:base + 64, jc * 2048 + qg * 512: jc * 2048 + qg * 512 + 512],
                            start=True, stop=True), inc=(sPE, 1))
                        slot = kt % 3
                        p.wait("scalar", sPE)
                        p.wait("scalar", sPE, ptb_reader[slot])
                        p.op("scalar", lambda e, pbS=pbS, slot=slot: e.activation(
                            out=ptb[:, slot * 512:(slot + 1) * 512], in_=pbS[:, :], func=AF.Exp,
                            scale=0.125, bias=negb[:, 0:1]), inc=(sACT, 1))
                        banks.release(bS, sACT)
                        ktrel = kt - 4 * qg
                        if ktrel >= 0:
                            p.wait("vector", sACT)
                            if ktrel > 0:
                                p.op("vector", lambda e, slot=slot, ktrel=ktrel: e.memset(
                                    ptb[:, slot * 512: slot * 512 + ktrel * 128], 0.0))
                            p.op("vector", lambda e, slot=slot, ktrel=ktrel: e.tensor_mul(
                                out=ptb[:, slot * 512 + ktrel * 128: slot * 512 + (ktrel + 1) * 128],
                                in0=ptb[:, slot * 512 + ktrel * 128: slot * 512 + (ktrel + 1) * 128],
                                in1=tri_bf[:]), inc=(sDVE, 1))
                            p.wait("tensor", sDVE)
                        else:
                            p.wait("tensor", sACT)
                        p.op("tensor", lambda e, pbO=pbO, h=h, kt=kt, slot=slot, nkt=nkt: e.matmul(
                            pbO[0:65, :], Vv[:, kt, h, :], ptb[:, slot * 512:(slot + 1) * 512],
                            start=(kt == 0), stop=(kt == nkt - 1)), inc=(sPE, 1))
                        ptb_reader[slot] = sPE.n
                    p.wait("vector", sPE)
                    p.op("vector", lambda e, pbO=pbO: e.reciprocal(out=rrow[:], in_=pbO[64:65, :]),
                         inc=(sDVE, 1))
                    p.wait("tensor", sDVE)
                    bR = 6 + (h * NTT + qg) % 2
                    pbR = banks.acquire(bR)
                    p.op("tensor", lambda e, pbR=pbR: e.matmul(pbR[0:64, :], ones_f[0:1, 0:64],
                                                               rrow[:], start=True, stop=True),
                         inc=(sPE, 1))
                    p.wait("scalar", sPE)
                    p.op("scalar", lambda e, pbR=pbR: e.copy(out=rstg[:], in_=pbR[0:64, :]),
                         inc=(sACT, 1))
                    banks.release(bR, sACT)
                    p.wait("vector", sACT)
                    p.wait("vector", sACT, scrA_reader)
                    p.op("vector", lambda e, pbO=pbO: e.tensor_mul(out=scrA[:], in0=pbO[0:64, :],
                                                                   in1=rstg[:]), inc=(sDVE, 1))
                    banks.release(bO, sDVE)
                    p.wait("scalar", sDVE)
                    p.op("scalar", lambda e, base=base, jc=jc, qg=qg: e.activation(
                        out=attn_out[base:base + 64,
                                     jc * 2048 + qg * 512: jc * 2048 + qg * 512 + 512],
                        in_=scrA[:], func=AF.Copy), inc=(sACT, 1))
                    scrA_reader = sACT.n
            scores_done = sACT.n
            pe_attn_end = sPE.n

            p.wait("gpsimd", sPE, pe_attn_end)
            p.op("gpsimd", lambda e: e.dma_start(
                out=wo_sb.rearrange("p (c o) -> p c o", c=NCH),
                in_=W["wo"].rearrange("(c p) o -> p c o", p=128)), inc=(sW, 16))
            wo_loaded = sW.n
            p.wait("vector", sB, bias_loaded)
            p.op("vector", lambda e: e.tensor_copy(out=bv_bf[:], in_=bias_sb["bv"][:]),
                 inc=(sDVE, 1))
            p.wait("tensor", sW, wo_loaded)
            p.wait("tensor", sDVE)
            for oc in range(NCH):
                b = oc % 2
                pbx = banks.acquire(b)
                for c in range(NCH):
                    p.op("tensor", lambda e, pbx=pbx, c=c, oc=oc: e.matmul(
                        pbx[:, 0:1], wo_sb[:, c * 1024 + oc * 128: c * 1024 + oc * 128 + 128],
                        bv_bf[:, c:c + 1], start=(c == 0), stop=(c == NCH - 1)),
                        inc=(sPE, 1) if c == NCH - 1 else None)
                p.wait("scalar", sPE)
                p.op("scalar", lambda e, pbx=pbx, oc=oc: e.activation(
                    out=bprime[:, oc:oc + 1], in_=pbx[:, 0:1], func=AF.Identity,
                    bias=bias_sb["bo"][:, oc:oc + 1]), inc=(sACT, 1))
                banks.release(b, sACT)
            bprime_done = sACT.n

            p.wait("tensor", sACT, scores_done)
            p.wait("tensor", sACT, bprime_done)
            xstg_f = xstg[:]
            astg_reader = {0: 0, 1: 0}
            for oc in range(NCH):
                par = oc % 2
                for tt in range(NTT):
                    b = 2 + (oc * NTT + tt) % 4
                    pbx = banks.acquire(b)
                    for c in range(NCH):
                        p.op("tensor", lambda e, pbx=pbx, c=c, oc=oc, tt=tt: e.matmul(
                            pbx[:, :], wo_sb[:, c * 1024 + oc * 128: c * 1024 + oc * 128 + 128],
                            attn_out[:, cs(c, tt * 512, 512)],
                            start=(c == 0), stop=(c == NCH - 1)),
                            inc=(sPE, 1) if c == NCH - 1 else None)
                    p.wait("scalar", sPE)
                    p.wait("scalar", sW, astg_reader[par] and sW.n)
                    p.op("scalar", lambda e, pbx=pbx, oc=oc, tt=tt, par=par: e.activation(
                        out=xstg_f[:, par * 2048 + tt * 512: par * 2048 + tt * 512 + 512],
                        in_=pbx[:, :], func=AF.Identity, bias=bprime[:, oc:oc + 1]), inc=(sACT, 1))
                    banks.release(b, sACT)
                p.wait("gpsimd", sACT)
                p.wait("gpsimd", sW)
                p.op("gpsimd", lambda e, oc=oc, par=par: e.dma_start(
                    out=outT.rearrange("(c p) t -> p c t", p=128)[:, oc, :],
                    in_=xstg_f[:, par * 2048: par * 2048 + 2048],
                    accum_op=ALU.add), inc=(sW, 16))
                astg_reader[par] = sW.n
            attn_accum_done = sW.n

        if "ffn" in phases:
            emit_ffn(p, banks, locals())

        p.wait("gpsimd", sDVE)
        p.wait("gpsimd", sACT)
        if "xn" in dbg:
            p.op("gpsimd", lambda e: e.dma_start(
                out=dbg["xn"].rearrange("(c p) t -> p c t", p=128),
                in_=xn[:].rearrange("p (c t) -> p c t", c=NCH)), inc=(sW, 16))
        if "q" in dbg:
            p.op("gpsimd", lambda e: e.dma_start(
                out=dbg["q"].rearrange("(c p) t -> p c t", p=128),
                in_=Q.rearrange("p (c t) -> p c t", c=NCH)), inc=(sW, 16))
        if "k" in dbg:
            p.op("gpsimd", lambda e: e.dma_start(
                out=dbg["k"].rearrange("(c p) t -> p c t", p=128),
                in_=Kb.rearrange("p (c t) -> p c t", c=NCH)), inc=(sW, 16))
        if "attnfm" in dbg:
            p.op("gpsimd", lambda e: e.dma_start(
                out=dbg["attnfm"].rearrange("(c p) t -> p c t", p=128),
                in_=attn_out[:].rearrange("p (c t) -> p c t", c=NCH)), inc=(sW, 16))
        if "xq" in dbg:
            p.op("gpsimd", lambda e: e.dma_start(
                out=dbg["xq"].rearrange("(c p) t -> p c t", p=128),
                in_=xq.rearrange("p (c t) -> p c t", c=NCH)), inc=(sW, 16))
        if "s0" in dbg:
            p.op("gpsimd", lambda e: e.dma_start(out=dbg["s0"], in_=s_sb[:]), inc=(sW, 16))
        if "h0" in dbg:
            p.op("gpsimd", lambda e: e.dma_start(out=dbg["h0"], in_=hbuf[:]), inc=(sW, 16))
        if "w1q" in dbg:
            p.op("gpsimd", lambda e: e.dma_start(out=dbg["w1q"], in_=w1q), inc=(sW, 16))
        if "w2q" in dbg:
            p.op("gpsimd", lambda e: e.dma_start(out=dbg["w2q"], in_=w2q), inc=(sW, 16))
        if "mrow" in dbg:
            p.op("gpsimd", lambda e: e.dma_start(out=dbg["mrow"], in_=inv_bc[0:1, :]), inc=(sW, 16))
        if "woacc" in dbg:
            p.op("gpsimd", lambda e: e.dma_start(out=dbg["woacc"], in_=xstg[:]), inc=(sW, 16))
        if "bprime" in dbg:
            p.op("gpsimd", lambda e: e.dma_start(out=dbg["bprime"], in_=bprime[:]), inc=(sW, 16))
        p.wait("sync", sW)
        p.wait("sync", sX)

        p.finalize(block)
    return nc


def gen_wmean(p, banks, L, done):
    (xstg, wcols, wtot_bf, winv, wscal, swcol, W, ones_bf, ones_f) = (
        L["xstg"], L["wcols"], L["wtot_bf"], L["winv"], L["wscal"], L["swcol"], L["W"],
        L["ones_bf"], L["ones_f"])
    sW, sX, sPE, sDVE, sACT = L["sW"], L["sX"], L["sPE"], L["sDVE"], L["sACT"]
    p.wait("sync", sPE, L["qkv_mm_done"])
    sWM = [p.sem("sWM0"), p.sem("sWM1")]
    stg_free = {0: (sDVE, 0), 1: (sDVE, 0)}
    for wi, wname in enumerate(("w1T", "w2T")):
        for i in range(32):
            par = i % 2
            sem0, cnt0 = stg_free[par]
            p.wait("sync", sem0, cnt0)
            if wi == 0:
                srcap = W[wname][(i // 4) * 128:(i // 4 + 1) * 128,
                                 (i % 4) * 1024:(i % 4) * 1024 + 1024]
            else:
                srcap = W[wname][i * 128:(i + 1) * 128, :]
            for h0 in (0, 64):
                p.op("sync", lambda e, srcap=srcap, par=par, h0=h0: e.dma_start(
                    out=xstg[h0:h0 + 64, par * 1024: par * 1024 + 1024],
                    in_=srcap[h0:h0 + 64, :]), inc=(sWM[par], 16))
            p.wait("vector", sWM[par])
            p.op("vector", lambda e, i=i, par=par: e.tensor_reduce(
                out=wcols[:, i:i + 1], in_=xstg[:, par * 1024: par * 1024 + 1024],
                axis=AX, op=ALU.add, apply_absolute_value=True), inc=(sDVE, 1))
            stg_free[par] = (sDVE, sDVE.n)
            yield
        p.wait("vector", sDVE)
        p.op("vector", lambda e: e.tensor_reduce(
            out=wcols[:, 32:33], in_=wcols[:, 0:32], axis=AX, op=ALU.add), inc=(sDVE, 1))
        p.wait("vector", sDVE)
        p.op("vector", lambda e: e.tensor_copy(out=wtot_bf[:], in_=wcols[:, 32:33]),
             inc=(sDVE, 1))
        p.wait("tensor", sDVE)
        b0 = banks.acquire(0)
        p.op("tensor", lambda e, b0=b0: e.matmul(b0[0:1, 0:1], wtot_bf[:], ones_bf[:, 0:1],
                                                 start=True, stop=True), inc=(sPE, 1))
        p.wait("vector", sPE)
        p.op("vector", lambda e, b0=b0, wi=wi: e.tensor_scalar(
            out=winv[:, wi:wi + 1], in0=b0[0:1, 0:1], scalar1=1.0 / (4096.0 * 1024.0),
            scalar2=1e-5, op0=ALU.mult, op1=ALU.max), inc=(sDVE, 1))
        banks.release(0, sDVE)
        p.wait("vector", sDVE)
        p.op("vector", lambda e, wi=wi: e.reciprocal(out=wscal[:, wi:wi + 1],
                                                     in_=winv[:, wi:wi + 1]), inc=(sDVE, 1))
        p.wait("tensor", sDVE)
        b0 = banks.acquire(0)
        p.op("tensor", lambda e, b0=b0, wi=wi: e.matmul(
            b0[:, 0:1], ones_f[0:1, :], wscal[:, wi:wi + 1], start=True, stop=True), inc=(sPE, 1))
        p.wait("scalar", sPE)
        p.op("scalar", lambda e, b0=b0, wi=wi: e.copy(out=swcol[:, wi:wi + 1], in_=b0[:, 0:1]),
             inc=(sACT, 1))
        banks.release(0, sACT)
    done["act"] = sACT.n
    yield


def emit_ffn(p, banks, L):
    (xn, xq, s_sb, w1q, w2q, hbuf, inv_bc, sc_bc, sc2_bc, inv2_bc, mfold, ubuf, ystg,
     xstg, bias_sb, ones_bf, ones_f, ident_bf, wcols, wtot_bf, winv, wscal, swcol,
     mcol, mcol_bf, W, outT, sW, sX, sPE, sDVE, sACT, sPOOL, scr) = (
        L["xn"], L["xq"], L["s_sb"], L["w1q"], L["w2q"], L["hbuf"], L["inv_bc"], L["sc_bc"],
        L["sc2_bc"], L["inv2_bc"], L["mfold"], L["ubuf"], L["ystg"], L["xstg"], L["bias_sb"],
        L["ones_bf"], L["ones_f"], L["ident_bf"], L["wcols"], L["wtot_bf"], L["winv"],
        L["wscal"], L["swcol"], L["mcol"], L["mcol_bf"], L["W"], L["outT"], L["sW"], L["sX"],
        L["sPE"], L["sDVE"], L["sACT"], L["sPOOL"], L["scr"])
    out_init = L["out_init"]
    attn_accum_done = L["attn_accum_done"]
    pe_now = sPE.n
    mbf = L["arena2"][:, 2048:4096]

    wmean_done = L["wmean_done"]

    p.wait("vector", sPE, pe_now)
    p.wait("scalar", sPE, pe_now)
    p.wait("scalar", sDVE)
    p.wait("vector", sACT)
    absf = L["sc_bc"]
    p.op("scalar", lambda e: e.activation(out=mfold[:], in_=xn[:, cs(0)], func=AF.Abs),
         inc=(sACT, 1))
    for c in range(1, NCH):
        p.op("scalar", lambda e, c=c: e.activation(out=absf[:], in_=xn[:, cs(c)], func=AF.Abs),
             inc=(sACT, 1))
        p.wait("vector", sACT)
        p.op("vector", lambda e: e.tensor_max(out=mfold[:], in0=mfold[:], in1=absf[:]),
             inc=(sDVE, 1))
        p.wait("scalar", sDVE)
    p.op("vector", lambda e: e.tensor_copy(out=mbf[:], in_=mfold[:]), inc=(sDVE, 1))
    mrow = ystg[0:1, :]
    invrow = mfold[0:1, :]
    p.wait("tensor", sDVE)
    for i in range(16):
        b = banks.acquire(i % 2)
        p.op("tensor", lambda e, b=b, i=i: e.matmul(b[:, 0:128], mbf[:, i * 128:(i + 1) * 128],
                                                    ident_bf[:], start=True, stop=True),
             inc=(sPE, 1))
        p.wait("vector", sPE)
        p.op("vector", lambda e, b=b: e.reduce_max(out=mcol[:], in_=b[:, 0:128], axis=AX),
             inc=(sDVE, 1))
        banks.release(i % 2, sDVE)
        p.wait("vector", sDVE)
        p.op("vector", lambda e: e.tensor_copy(out=mcol_bf[:], in_=mcol[:]), inc=(sDVE, 1))
        p.wait("tensor", sDVE)
        b2 = banks.acquire(2 + i % 2)
        p.op("tensor", lambda e, b2=b2: e.matmul(b2[0:1, 0:128], mcol_bf[:], ident_bf[:],
                                                 start=True, stop=True), inc=(sPE, 1))
        p.wait("scalar", sPE)
        p.op("scalar", lambda e, b2=b2, i=i: e.copy(out=mrow[:, i * 128:(i + 1) * 128],
                                                    in_=b2[0:1, 0:128]), inc=(sACT, 1))
        banks.release(2 + i % 2, sACT)
    p.wait("vector", sACT)
    p.op("vector", lambda e: e.tensor_scalar(out=mrow, in0=mrow, scalar1=1e-5, scalar2=None,
                                             op0=ALU.max), inc=(sDVE, 1))
    p.wait("vector", sDVE)
    p.op("vector", lambda e: e.tensor_scalar(out=invrow, in0=mrow, scalar1=1.0 / 127.0,
                                             scalar2=winv[0:1, 0:1], op0=ALU.mult, op1=ALU.mult),
         inc=(sDVE, 1))
    p.op("vector", lambda e: e.reciprocal(out=mrow, in_=mrow), inc=(sDVE, 1))
    p.wait("vector", sDVE)
    p.op("vector", lambda e: e.tensor_scalar(out=mrow, in0=mrow, scalar1=127.0, scalar2=None,
                                             op0=ALU.mult), inc=(sDVE, 1))
    p.wait("tensor", sDVE)
    p.wait("tensor", sACT, wmean_done)
    for i in range(4):
        b = banks.acquire(4 + i % 2)
        p.op("tensor", lambda e, b=b, i=i: e.matmul(b[:, :], ones_f[0:1, :],
                                                    mrow[:, i * 512:(i + 1) * 512],
                                                    start=True, stop=True))
        b2 = banks.acquire(6 + i % 2)
        p.op("tensor", lambda e, b2=b2, i=i: e.matmul(b2[:, :], ones_f[0:1, :],
                                                      invrow[:, i * 512:(i + 1) * 512],
                                                      start=True, stop=True), inc=(sPE, 1))
        p.wait("scalar", sPE)
        p.op("scalar", lambda e, b=b, i=i: e.copy(out=sc_bc[:, i * 512:(i + 1) * 512],
                                                  in_=b[:, :]))
        p.op("scalar", lambda e, b2=b2, i=i: e.copy(out=inv_bc[:, i * 512:(i + 1) * 512],
                                                    in_=b2[:, :]), inc=(sACT, 1))
        banks.release(4 + i % 2, sACT)
        banks.release(6 + i % 2, sACT)
    p.wait("vector", sACT)
    qscr = [mfold, ystg]
    for c in range(NCH):
        par = c % 2
        p.op("vector", lambda e, c=c, par=par: e.tensor_mul(out=qscr[par][:, 0:2048],
                                                            in0=xn[:, cs(c)], in1=sc_bc[:]))
        p.op("vector", lambda e, c=c, par=par: e.tensor_scalar(out=xq[:, cs(c)],
                                                               in0=qscr[par][:, 0:2048],
                                                               scalar1=MAGIC, scalar2=MAGIC,
                                                               op0=ALU.add, op1=ALU.subtract),
             inc=(sDVE, 1))
    xq_done = sDVE.n

    ystg_reader = 0
    w1q_reader = {0: 0, 1: 0}
    clip_done = {0: xq_done, 1: xq_done}
    mq_free = {0: 0, 1: 0}
    stg_free = {0: (sDVE, 0), 1: (sDVE, 0)}
    sST = [p.sem("sST0"), p.sem("sST1")]
    sST2 = [p.sem("sST2a"), p.sem("sST2b")]
    mq = [mfold[:, 1024:2048], ystg[:, 1024:2048]]
    absq = [sc_bc[:, 0:1024], sc_bc[:, 1024:2048]]
    runmax = ystg[:, 0:1024]
    w1r = W["w1T"].rearrange("(c p) o -> p c o", p=128)
    w2r = W["w2T"].rearrange("(c p) o -> p c o", p=128)
    cw = bias_sb["convw"]
    first_issue = True

    for half in range(2):
        tok0 = 0 if half == 0 else 1023
        out0 = half * 1024
        um0 = 1 - half
        p.wait("tensor", sDVE, xq_done)
        p.wait("sync", sDVE)
        p.wait("scalar", sW)
        p.wait("vector", sW)
        stg_free = {0: (sACT, sACT.n), 1: (sACT, sACT.n)}

        def w1_issue(oc):
            nonlocal first_issue
            par = oc % 2
            sem0, cnt0 = stg_free[par]
            p.wait("sync", sem0, cnt0)
            if first_issue:
                p.wait("sync", sW, attn_accum_done)
                first_issue = False
            stg0 = 2048 + par * 1024
            for k in range(2):
                p.op("sync", lambda e, oc=oc, stg0=stg0, k=k: e.dma_start(
                    out=xstg[:, stg0 + k * 512: stg0 + (k + 1) * 512].rearrange(
                        "p (c o) -> p c o", c=4),
                    in_=w1r[:, k * 4:(k + 1) * 4, oc * 128:(oc + 1) * 128]),
                    inc=(sST[par], 16))

        def w1_quant(oc):
            par = oc % 2
            stg0 = 2048 + par * 1024
            p.wait("scalar", sST[par])
            p.wait("scalar", sACT, wmean_done)
            p.wait("scalar", sDVE, mq_free[par])
            p.op("scalar", lambda e, stg0=stg0, par=par: e.activation(
                out=mq[par][:], in_=xstg[:, stg0:stg0 + 1024], func=AF.Identity,
                scale=swcol[:, 0:1]), inc=(sACT, 1))
            stg_free[par] = (sACT, sACT.n)
            p.wait("vector", sACT)
            p.wait("vector", sPE, w1q_reader[par])
            p.op("vector", lambda e, par=par: e.tensor_scalar(
                out=mq[par][:], in0=mq[par][:], scalar1=1.0, scalar2=-1.0,
                op0=ALU.min, op1=ALU.max))
            p.op("vector", lambda e, par=par: e.tensor_scalar(
                out=w1q[:, par * 1024:(par + 1) * 1024], in0=mq[par][:],
                scalar1=MAGIC, scalar2=MAGIC, op0=ALU.add, op1=ALU.subtract), inc=(sDVE, 1))
            clip_done[par] = sDVE.n
            mq_free[par] = sDVE.n

        def h_mm(oc):
            par = oc % 2
            p.wait("tensor", sDVE, clip_done[par])
            segs = []
            for si, (s0, swd) in enumerate(((0, 512), (512, 512), (1024, 1))):
                bi = (oc * 3 + si) % 4
                b = banks.acquire(bi)
                for d in range(NCH):
                    p.op("tensor", lambda e, b=b, d=d, s0=s0, swd=swd, par=par, tok0=tok0:
                         e.matmul(b[:, 0:swd],
                                  w1q[:, par * 1024 + d * 128: par * 1024 + d * 128 + 128],
                                  xq[:, cs(d, tok0 + s0, swd)],
                                  start=(d == 0), stop=(d == NCH - 1)),
                         inc=(sPE, 1) if d == NCH - 1 else None)
                segs.append((bi, b, s0, swd, sPE.n))
            w1q_reader[par] = sPE.n
            return segs

        hb = L["bwtail"][:, 6144:7170]

        def flush_max():
            while pend_max:
                oc_, par_, cnt_ = pend_max.pop(0)
                p.wait("vector", sACT, cnt_)
                if oc_ == 0:
                    p.op("vector", lambda e, par_=par_: e.tensor_copy(
                        out=runmax[:], in_=absq[par_][:]), inc=(sDVE, 1))
                else:
                    p.op("vector", lambda e, par_=par_: e.tensor_max(
                        out=runmax[:], in0=runmax[:], in1=absq[par_][:]), inc=(sDVE, 1))
                max_read[par_] = sDVE.n

        def flush_silu():
            while pend_silu:
                oc_, cnt_ = pend_silu.pop(0)
                par_ = oc_ % 2
                p.wait("scalar", sDVE, cnt_)
                p.op("scalar", lambda e, oc_=oc_: e.activation(
                    out=s_sb[:, oc_ * 1024:(oc_ + 1) * 1024], in_=ubuf[:], func=AF.Silu,
                    bias=bias_sb["convb"][:, oc_:oc_ + 1]), inc=(sACT, 1))
                silu_done[0] = sACT.n
                p.wait("scalar", sDVE, max_read[par_])
                p.op("scalar", lambda e, oc_=oc_, par_=par_: e.activation(
                    out=absq[par_][:], in_=s_sb[:, oc_ * 1024:(oc_ + 1) * 1024], func=AF.Abs),
                    inc=(sACT, 1))
                pend_max.append((oc_, par_, sACT.n))

        def h_post(oc, segs):
            flush_max()
            for bi, b, s0, swd, pn in segs:
                p.wait("vector", sPE, pn)
                p.op("vector", lambda e, b=b, s0=s0, swd=swd, tok0=tok0: e.tensor_mul(
                    out=hb[:, um0 + s0: um0 + s0 + swd], in0=b[:, 0:swd],
                    in1=inv_bc[:, tok0 + s0: tok0 + s0 + swd]), inc=(sDVE, 1))
                banks.release(bi, sDVE)
            flush_silu()
            p.wait("vector", sACT, silu_done[0])
            p.op("vector", lambda e, oc=oc: e.tensor_scalar(
                out=ubuf[:, 0:1024], in0=hb[:, 0:1024],
                scalar1=cw[:, oc * 3: oc * 3 + 1], scalar2=None, op0=ALU.mult))
            p.op("vector", lambda e, oc=oc: e.scalar_tensor_tensor(
                out=ubuf[:, 0:1024], in0=hb[:, 1:1025],
                scalar=cw[:, oc * 3 + 1: oc * 3 + 2], in1=ubuf[:, 0:1024],
                op0=ALU.mult, op1=ALU.add))
            p.op("vector", lambda e, oc=oc: e.scalar_tensor_tensor(
                out=ubuf[:, 0:1024], in0=hb[:, 2:1026],
                scalar=cw[:, oc * 3 + 2: oc * 3 + 3], in1=ubuf[:, 0:1024],
                op0=ALU.mult, op1=ALU.add), inc=(sDVE, 1))
            pend_silu.append((oc, sDVE.n))

        pend_max = []
        pend_silu = []
        max_read = {0: 0, 1: 0}
        silu_done = [0]
        padc = 0 if half == 0 else 1025
        p.op("vector", lambda e, padc=padc: e.memset(hb[:, padc:padc + 1], 0.0),
             inc=(sDVE, 1))
        w1_issue(0)
        w1_issue(1)
        w1_quant(0)
        for oc in range(32):
            segs = h_mm(oc)
            if oc + 2 < 32:
                w1_issue(oc + 2)
            if oc + 1 < 32:
                w1_quant(oc + 1)
            h_post(oc, segs)
        flush_silu()
        flush_max()
        s_done = sACT.n

        m2row = ystg[0:1, 0:1024]
        inv2row = mfold[0:1, 0:1024]
        p.wait("vector", sACT, s_done)
        p.wait("vector", sPE)
        p.op("vector", lambda e: e.tensor_copy(out=mbf[:, 0:1024], in_=runmax[:]),
             inc=(sDVE, 1))
        p.wait("tensor", sDVE)
        for i in range(8):
            b = banks.acquire(i % 2)
            p.op("tensor", lambda e, b=b, i=i: e.matmul(b[:, 0:128], mbf[:, i * 128:(i + 1) * 128],
                                                        ident_bf[:], start=True, stop=True),
                 inc=(sPE, 1))
            p.wait("vector", sPE)
            p.op("vector", lambda e, b=b: e.reduce_max(out=mcol[:], in_=b[:, 0:128], axis=AX),
                 inc=(sDVE, 1))
            banks.release(i % 2, sDVE)
            p.wait("vector", sDVE)
            p.op("vector", lambda e: e.tensor_copy(out=mcol_bf[:], in_=mcol[:]), inc=(sDVE, 1))
            p.wait("tensor", sDVE)
            b2 = banks.acquire(2 + i % 2)
            p.op("tensor", lambda e, b2=b2: e.matmul(b2[0:1, 0:128], mcol_bf[:], ident_bf[:],
                                                     start=True, stop=True), inc=(sPE, 1))
            p.wait("scalar", sPE)
            p.op("scalar", lambda e, b2=b2, i=i: e.copy(out=m2row[:, i * 128:(i + 1) * 128],
                                                        in_=b2[0:1, 0:128]), inc=(sACT, 1))
            banks.release(2 + i % 2, sACT)
        p.wait("vector", sACT)
        p.op("vector", lambda e: e.tensor_scalar(out=m2row, in0=m2row, scalar1=1e-5, scalar2=None,
                                                 op0=ALU.max), inc=(sDVE, 1))
        p.wait("vector", sDVE)
        p.op("vector", lambda e: e.tensor_scalar(out=inv2row, in0=m2row, scalar1=1.0 / 127.0,
                                                 scalar2=winv[0:1, 1:2], op0=ALU.mult,
                                                 op1=ALU.mult), inc=(sDVE, 1))
        p.op("vector", lambda e: e.reciprocal(out=m2row, in_=m2row), inc=(sDVE, 1))
        p.wait("vector", sDVE)
        p.op("vector", lambda e: e.tensor_scalar(out=m2row, in0=m2row, scalar1=127.0,
                                                 scalar2=None, op0=ALU.mult), inc=(sDVE, 1))
        p.wait("tensor", sDVE)
        for i in range(2):
            b = banks.acquire(4 + i)
            p.op("tensor", lambda e, b=b, i=i: e.matmul(b[:, :], ones_f[0:1, :],
                                                        m2row[:, i * 512:(i + 1) * 512],
                                                        start=True, stop=True))
            b2 = banks.acquire(6 + i)
            p.op("tensor", lambda e, b2=b2, i=i: e.matmul(b2[:, :], ones_f[0:1, :],
                                                          inv2row[:, i * 512:(i + 1) * 512],
                                                          start=True, stop=True), inc=(sPE, 1))
            p.wait("scalar", sPE)
            p.op("scalar", lambda e, b=b, i=i: e.copy(out=sc2_bc[:, i * 512:(i + 1) * 512],
                                                      in_=b[:, :]))
            p.op("scalar", lambda e, b2=b2, i=i: e.copy(out=inv2_bc[:, i * 512:(i + 1) * 512],
                                                        in_=b2[:, :]), inc=(sACT, 1))
            banks.release(4 + i, sACT)
            banks.release(6 + i, sACT)
        qs2 = [mfold[:, 0:1024], ystg[:, 0:1024]]
        w2qp = [w2q[:, 0:2048], w2q[:, 2048:4096]]
        wpart_reader = {0: 0, 1: 0}
        wstg2_free = {0: (sACT, sACT.n), 1: (sACT, sACT.n)}
        quant2_done = {0: 0, 1: 0}
        qscr_free = sDVE.n

        def w2_issue(oc2, part):
            sem0, cnt0 = wstg2_free[part]
            p.wait("sync", sem0, cnt0)
            for k in range(4):
                p.op("sync", lambda e, oc2=oc2, part=part, k=k: e.dma_start(
                    out=xstg[:, part * 2048 + k * 512: part * 2048 + (k + 1) * 512].rearrange(
                        "p (c o) -> p c o", c=4),
                    in_=w2r[:, part * 16 + k * 4: part * 16 + (k + 1) * 4,
                            oc2 * 128:(oc2 + 1) * 128]),
                    inc=(sST2[part], 16))

        def w2_quant(oc2, part):
            nonlocal qscr_free
            p.wait("scalar", sST2[part])
            p.wait("scalar", sDVE, qscr_free)
            for g in range(2):
                p.op("scalar", lambda e, g=g, part=part: e.activation(
                    out=qs2[g], in_=xstg[:, part * 2048 + g * 1024: part * 2048 + (g + 1) * 1024],
                    func=AF.Identity, scale=swcol[:, 1:2]), inc=(sACT, 1))
            wstg2_free[part] = (sACT, sACT.n)
            p.wait("vector", sACT)
            p.wait("vector", sPE, wpart_reader[part])
            for g in range(2):
                p.op("vector", lambda e, g=g, part=part: e.tensor_scalar(
                    out=qs2[g], in0=qs2[g], scalar1=1.0, scalar2=-1.0,
                    op0=ALU.min, op1=ALU.max))
                p.op("vector", lambda e, g=g, part=part: e.tensor_scalar(
                    out=w2qp[part][:, g * 1024:(g + 1) * 1024], in0=qs2[g],
                    scalar1=MAGIC, scalar2=MAGIC, op0=ALU.add, op1=ALU.subtract),
                    inc=(sDVE, 1))
            quant2_done[part] = sDVE.n
            qscr_free = sDVE.n
            if oc2 + 1 < NCH:
                w2_issue(oc2 + 1, part)

        bnk2 = {}

        def w2_mm(oc2, part):
            p.wait("tensor", sDVE, quant2_done[part])
            done = []
            for ts2 in range(2):
                bi = (oc2 * 2 + ts2) % 4
                if part == 0:
                    bnk2[ts2] = (bi, banks.acquire(bi))
                b = bnk2[ts2][1]
                for dd in range(16):
                    d32 = part * 16 + dd
                    p.op("tensor", lambda e, b=b, d32=d32, dd=dd, ts2=ts2, part=part: e.matmul(
                        b[:, :], w2qp[part][:, dd * 128: dd * 128 + 128],
                        s_sb[:, d32 * 1024 + ts2 * 512: d32 * 1024 + ts2 * 512 + 512],
                        start=(part == 0 and dd == 0), stop=(part == 1 and dd == 15)),
                        inc=(sPE, 1) if dd == 15 else None)
                done.append(sPE.n)
            wpart_reader[part] = sPE.n
            return done

        def w2_post(oc2, done):
            nonlocal ystg_reader
            for ts2 in range(2):
                bi, b = bnk2[ts2]
                p.wait("vector", sPE, done[ts2])
                p.wait("vector", sW, ystg_reader)
                p.op("vector", lambda e, b=b, ts2=ts2: e.tensor_mul(
                    out=ystg[:, ts2 * 512 + 1024: ts2 * 512 + 1536],
                    in0=b[:, :], in1=inv2_bc[:, ts2 * 512:(ts2 + 1) * 512]), inc=(sDVE, 1))
                banks.release(bi, sDVE)
            p.wait("gpsimd", sDVE)
            p.wait("gpsimd", sW)
            if "sOI" in L:
                p.wait("gpsimd", L["sOI"], L["out_init"])
            p.op("gpsimd", lambda e, oc2=oc2, out0=out0: e.dma_start(
                out=outT.rearrange("(c p) t -> p c t", p=128)[:, oc2, out0:out0 + 1024],
                in_=ystg[:, 1024:2048], accum_op=ALU.add), inc=(sW, 16))
            ystg_reader = sW.n

        p.wait("scalar", sPE)
        w2_issue(0, 0)
        w2_issue(0, 1)
        w2_quant(0, 0)
        w2_quant(0, 1)

        p.wait("vector", sACT)
        sq_mid = 0
        for c in range(32):
            par = c % 2
            p.op("vector", lambda e, c=c, par=par: e.tensor_mul(out=qs2[par],
                                                                in0=s_sb[:, c * 1024:(c + 1) * 1024],
                                                                in1=sc2_bc[:]))
            p.op("vector", lambda e, c=c, par=par: e.tensor_scalar(
                out=s_sb[:, c * 1024:(c + 1) * 1024], in0=qs2[par],
                scalar1=MAGIC, scalar2=MAGIC, op0=ALU.add, op1=ALU.subtract), inc=(sDVE, 1))
            if c == 15:
                sq_mid = sDVE.n
        sq_done = sDVE.n
        qscr_free = sq_done

        for oc2 in range(NCH):
            if oc2 == 0:
                p.wait("tensor", sDVE, sq_mid)
            done0 = w2_mm(oc2, 0)
            if oc2 == 0:
                p.wait("tensor", sDVE, sq_done)
            else:
                w2_quant(oc2, 1)
            done1 = w2_mm(oc2, 1)
            if oc2 + 1 < NCH:
                w2_quant(oc2 + 1, 0)
            w2_post(oc2, done1)




def build_fused(debug=()):
    nc = bass.Bass("TRN2", target_bir_lowering=False)
    xT = nc.dram_tensor("xT", [D, S], F32, kind="ExternalInput").ap()
    projT = nc.dram_tensor("projT", [NH, D, HD], F32, kind="ExternalInput").ap()
    metric = nc.dram_tensor("metric", [NH, HD, HD], F32, kind="ExternalInput").ap()
    mixer_wT = nc.dram_tensor("mixer_wT", [D, D], F32, kind="ExternalInput").ap()
    mixer_b = nc.dram_tensor("mixer_b", [128, NCH], F32, kind="ExternalInput").ap()
    gate_wT = nc.dram_tensor("gate_wT", [D, 8], F32, kind="ExternalInput").ap()
    gate_b = nc.dram_tensor("gate_b", [1, 8], F32, kind="ExternalInput").ap()
    expert_wT = nc.dram_tensor("expert_wT", [8, D, D], F32, kind="ExternalInput").ap()
    expert_b = nc.dram_tensor("expert_b", [8, D], F32, kind="ExternalInput").ap()
    W = {}
    for name, shape in [("wq", [D, 256]), ("bq", [128, 2]), ("wqc", [256, 896]), ("bqc", [128, 7]),
                        ("wqr", [D, 128]), ("bqr", [128, 1]), ("wkv", [D, 256]), ("bkv", [128, 2]),
                        ("wkr", [D, 128]), ("bkr", [128, 1]), ("wkc", [256, 896]), ("bkc", [128, 7]),
                        ("wv", [256, D]), ("bv", [128, 8]), ("wo", [D, D]), ("bo", [128, 8]),
                        ("w1T", [D, 4096]), ("convw", [128, 96]), ("convb", [128, 32]),
                        ("w2T", [4096, D])]:
        W[name] = nc.dram_tensor(name, shape, F32, kind="ExternalInput").ap()
    outT = nc.dram_tensor("outT", [D, S], F32, kind="ExternalOutput").ap()
    dbg = {}
    for name in debug:
        shp = {"x1": [D, S], "xn": [D, S]}[name]
        dbg[name] = nc.dram_tensor("dbg_" + name, shp, F32, kind="ExternalOutput").ap()

    with ExitStack() as es:
        sb = lambda name, shape, dt=F32, **kw: es.enter_context(nc.sbuf_tensor(name, shape, dt, **kw))
        A = sb("A", [128, 49408], BF16)
        B = sb("B", [128, 16384], F32)
        xn = sb("xn", [128, NCH * S], BF16)
        Dd = sb("Dd", [128, 6144], BF16)
        x = B[:]
        pfm = A[:, 0:16384]
        wbuf = A[:, 16384:32768]
        atail = 32768
        ptokb = Dd[:, 0:2048]
        gacc4 = A[:, atail + 2048: atail + 6144].bitcast(F32)
        gacc = gacc4[0:64, 0:1024]
        gbf = A[0:64, atail + 6144: atail + 7168]
        metric_sb = A[0:64, atail + 7168: atail + 8192]
        mg = A[:, atail + 8192: atail + 8704]
        g_sb = A[0:8, atail + 8704: atail + 10752]
        g_row = A[0:1, atail + 10752: atail + 12800]
        gbc = A[:, atail + 12800: atail + 13824]
        xne = A[:, atail + 13824: atail + 13824 + 4096] if False else None
        xne2 = [Dd[:, 2048: 2048 + 4096], None]
        eb_bf = A[0:8, atail + 13824: atail + 14848]
        x2s = A[:, 1024:1536]
        scr = A[:, 0:1024].bitcast(F32)
        rows = A[:, 3584:6656].bitcast(F32)
        rowm = rows[0:1, 0:512]
        rowe = rows[0:1, 512:1024]
        rowt = rows[0:1, 1024:1536]
        ab = A[:, 1536:3584].bitcast(F32)
        l0small_f = A[:, atail + 14848: atail + 14912].bitcast(F32)
        mixb_sb = l0small_f[:, 0:8]
        gtok = l0small_f[:, 8:16]
        gtok2 = l0small_f[:, 16:24]
        gatew_bf = A[:, atail + 14912: atail + 14976]
        gb_bf = A[0:1, atail + 14976: atail + 14984]
        gtok_bf = A[:, atail + 14984: atail + 14992]
        col1 = sb("col1", [128, 1], F32)
        col2 = sb("col2", [128, 1], F32)
        col3 = sb("col3", [128, 1], F32)
        ones_bf = sb("ones_bf", [128, 128], BF16)
        ones_f = sb("ones_f", [1, 128], F32)
        ident_bf = sb("ident_bf", [128, 128], BF16)
        tri_bf = sb("tri_bf", [128, 128], BF16)
        hpi = sb("hpi", [64, 1], F32)
        invf = sb("invf", [64, 1], F32)
        Q = A[:, 0:16384]
        Kb = A[:, 16384:32768]
        Vb = A[:, 32768:49408]
        xq = A[:, 0:16384]
        s_sb = A[:, 16384:49152]
        w1q = Dd[:, 0:2048]
        w2q = Dd[:, 2048:6144]
        woS = [Dd[:, 0:1024], Dd[:, 1024:2048]]
        arena3 = B[:, 0:8192]
        cosT = arena3[:, 0:2048]
        sinT = arena3[:, 2048:4096]
        awbuf = arena3[:, 4096:8192].bitcast(BF16)
        pwq = awbuf[:, 0:2048]
        pwkv = awbuf[:, 2048:4096]
        pwqc = awbuf[:, 4096:5888]
        pwkc = awbuf[:, 5888:7680]
        qrS = arena3[:, 4096:6144]
        rotS = arena3[:, 6144:8192]
        attn_out = arena3[:].bitcast(BF16)
        inv_bc = arena3[:, 0:2048]
        sc_bc = arena3[:, 2048:4096]
        sc2_bc = arena3[:, 2048:3072]
        inv2_bc = arena3[:, 3072:4096]
        mfold = arena3[:, 4096:6144]
        ubuf = arena3[:, 4096:4608].bitcast(BF16)
        ystg = arena3[:, 6144:8192]
        xstg = B[:, 8192:12288]
        xstg_bf = B[:, 8192:12288].bitcast(BF16)
        cq = xstg_bf[:, 0:4096]
        ckv = xstg_bf[:, 4096:8192]
        bwtail = B[:, 12288:16384].bitcast(BF16)
        wqr_sb = bwtail[:, 0:1024]
        wkr_sb = bwtail[:, 1024:2048]
        wv_sb = bwtail[:, 2048:4096]
        ptb = bwtail[:, 4096:8192]
        hbuf = bwtail[:, 6144:7169]
        rrowA = xstg[0:1, 3584:4096]
        rrowB = xstg[0:1, 2560:3072]
        rstgA = xstg[0:64, 2560:3072]
        rstgB = xstg[0:64, 3072:3584]
        scrA_A = xstg[0:64, 2048:2304].bitcast(BF16)
        scrA_B = xstg[0:64, 2304:2560].bitcast(BF16)
        bias_all = sb("bias_all", [128, 164], F32)
        bias_sb = {}
        _off = 0
        for name, n in (("bq", 2), ("bqc", 7), ("bqr", 1), ("bkv", 2), ("bkr", 1), ("bkc", 7),
                        ("bv", 8), ("bo", 8), ("convw", 96), ("convb", 32)):
            bias_sb[name] = bias_all[:, _off:_off + n]
            _off += n
        bv_bf = sb("bv_bf", [128, 8], BF16)
        bprime = sb("bprime", [128, 8], F32)
        qmaxc = sb("qmaxc", [128, 9], F32)
        qkmax = sb("qkmax", [128, 2], BF16)
        negb = sb("negb", [128, 1], F32)
        brow = sb("brow", [1, 4], F32)
        wcols = sb("wcols", [128, 33], F32)
        wtot_bf = sb("wtot_bf", [128, 1], BF16)
        winv = sb("winv", [1, 2], F32)
        wscal = sb("wscal", [1, 2], F32)
        swcol = sb("swcol", [128, 2], F32)
        mcol = sb("mcol", [128, 1], F32)
        mcol_bf = sb("mcol_bf", [128, 1], BF16)
        pbD0 = es.enter_context(nc.psum_tensor("pbD0", [128, 1024], F32))
        pbD1 = es.enter_context(nc.psum_tensor("pbD1", [128, 1024], F32))
        pb = [pbD0[:, 0:512], pbD0[:, 512:1024], pbD1[:, 0:512], pbD1[:, 512:1024]] + [
            es.enter_context(nc.psum_tensor(f"pb{i}", [128, 512], F32))[:] for i in range(4, 8)]
        pbD = [pbD0, pbD1]

        p = Prog(nc, es)
        sW = p.sem("sW")
        sX = p.sem("sX")
        sPE = p.sem("sPE")
        sDVE = p.sem("sDVE")
        sACT = p.sem("sACT")
        sPOOL = p.sem("sPOOL")
        sWE = [p.sem("sWE0"), p.sem("sWE1")]
        sB = p.sem("sB")
        sOI = p.sem("sOI")
        banks = Banks(p, pb)
        block = es.enter_context(nc.Block())

        p.op("gpsimd", lambda e: e.memset(ones_f[:], 1.0), inc=(sPOOL, 1))
        p.op("gpsimd", lambda e: e.memset(ones_bf[:], 1.0), inc=(sPOOL, 1))
        p.op("gpsimd", lambda e: e.memset(ident_bf[:], 1.0), inc=(sPOOL, 1))
        p.op("gpsimd", lambda e: e.affine_select(ident_bf[:], ident_bf[:], pattern=[[-1, 128]],
                                                 compare_op=ALU.is_equal, fill=0.0, base=0,
                                                 channel_multiplier=1), inc=(sPOOL, 1))
        p.op("gpsimd", lambda e: e.memset(tri_bf[:], 1.0), inc=(sPOOL, 1))
        p.op("gpsimd", lambda e: e.affine_select(tri_bf[:], tri_bf[:], pattern=[[1, 128]],
                                                 compare_op=ALU.is_ge, fill=0.0, base=0,
                                                 channel_multiplier=-1), inc=(sPOOL, 1))
        p.op("gpsimd", lambda e: e.memset(hpi[:], math.pi / 2), inc=(sPOOL, 1))
        p.op("gpsimd", lambda e: e.iota(invf[:], pattern=[[0, 1]], base=0, channel_multiplier=1,
                                        allow_small_or_imprecise_dtypes=True), inc=(sPOOL, 1))
        p.op("gpsimd", lambda e: e.memset(gacc[:], 0.0), inc=(sPOOL, 1))
        consts_done = sPOOL.n

        p.op("sync", lambda e: e.dma_start(
            out=x.rearrange("p (c t) -> p c t", c=NCH),
            in_=xT.rearrange("(c p) t -> p c t", p=128)), inc=(sX, 16))
        p.op("sync", lambda e: e.dma_start(out=mixb_sb[:], in_=mixer_b), inc=(sX, 16))
        x_loaded = sX.n
        for c in range(NCH):
            p.op("gpsimd", lambda e, c=c: e.dma_start(
                out=wbuf[:, c * 1024:(c + 1) * 1024].rearrange("p (h k) -> p h k", h=NH),
                in_=projT[:, c * 128:(c + 1) * 128, :].rearrange("h p k -> p h k")), inc=(sW, 16))
        p.op("gpsimd", lambda e: e.dma_start(
            out=metric_sb.rearrange("m (h l) -> m h l", h=NH),
            in_=metric.rearrange("h m l -> m h l")), inc=(sW, 16))
        p.op("gpsimd", lambda e: e.dma_start(
            out=gatew_bf[:].rearrange("p (c e) -> p c e", c=NCH),
            in_=gate_wT.rearrange("(c p) e -> p c e", p=128)), inc=(sW, 16))
        p.op("gpsimd", lambda e: e.dma_start(out=gb_bf[:], in_=gate_b), inc=(sW, 16))
        p.op("gpsimd", lambda e: e.dma_start(out=eb_bf[:], in_=expert_b), inc=(sW, 16))
        p.op("gpsimd", lambda e: e.dma_start(
            out=wbuf[:, 8192:16384].rearrange("p (c o) -> p c o", c=NCH),
            in_=mixer_wT.rearrange("(c p) o -> p c o", p=128)), inc=(sW, 16))
        weights_loaded = sW.n

        def proj_tile(c, h):
            return wbuf[:, (c * NH + h) * HD:(c * NH + h + 1) * HD]

        def mix_tile(c, oc):
            return wbuf[:, 8192 + c * D + oc * 128: 8192 + c * D + oc * 128 + 128]

        def emit_ln():
            for tt in range(NTT):
                t0 = tt * 512
                for c in range(NCH):
                    p.op("vector", lambda e, c=c, t0=t0: e.tensor_copy(
                        out=xn[:, cs(c, t0, 512)], in_=x[:, cs(c, t0, 512)]),
                        inc=(sDVE, 1) if c == NCH - 1 else None)
                p.wait("tensor", sDVE)
                b0 = banks.acquire(0)
                b1 = banks.acquire(1)
                for c in range(NCH):
                    p.wait("scalar", sPE)
                    p.op("scalar", lambda e, c=c, t0=t0: e.activation(
                        out=x2s[:], in_=x[:, cs(c, t0, 512)], func=AF.Square), inc=(sACT, 1))
                    p.op("tensor", lambda e, c=c, t0=t0, b0=b0: e.matmul(
                        b0[0:1, :], ones_bf[:, 0:1], xn[:, cs(c, t0, 512)],
                        start=(c == 0), stop=(c == NCH - 1)))
                    p.wait("tensor", sACT)
                    p.op("tensor", lambda e, c=c, b1=b1: e.matmul(
                        b1[0:1, :], ones_bf[:, 0:1], x2s[:],
                        start=(c == 0), stop=(c == NCH - 1)), inc=(sPE, 1))
                p.wait("vector", sPE)
                p.op("vector", lambda e, b0=b0: e.tensor_scalar(out=rowm, in0=b0[0:1, :],
                                                                scalar1=1.0 / D, scalar2=None, op0=ALU.mult))
                p.op("vector", lambda e, b1=b1: e.tensor_scalar(out=rowe, in0=b1[0:1, :],
                                                                scalar1=1.0 / D, scalar2=None, op0=ALU.mult),
                     inc=(sDVE, 1))
                banks.release(0, sDVE)
                banks.release(1, sDVE)
                p.op("vector", lambda e: e.tensor_mul(out=rowt, in0=rowm, in1=rowm))
                p.op("vector", lambda e: e.tensor_sub(out=rowe, in0=rowe, in1=rowt))
                p.op("vector", lambda e: e.tensor_scalar(out=rowe, in0=rowe, scalar1=1e-5,
                                                         scalar2=None, op0=ALU.add), inc=(sDVE, 1))
                p.wait("scalar", sDVE)
                p.op("scalar", lambda e: e.activation(out=rowt, in_=rowe, func=AF.Sqrt),
                     inc=(sACT, 1))
                p.wait("vector", sACT)
                p.op("vector", lambda e: e.reciprocal(out=rowt, in_=rowt))
                p.op("vector", lambda e: e.tensor_mul(out=rowm, in0=rowm, in1=rowt))
                p.op("vector", lambda e: e.tensor_scalar(out=rowm, in0=rowm, scalar1=-1.0,
                                                         scalar2=None, op0=ALU.mult), inc=(sDVE, 1))
                p.wait("tensor", sDVE)
                b2 = banks.acquire(2)
                b3 = banks.acquire(3)
                p.op("tensor", lambda e, b2=b2: e.matmul(b2[:, :], ones_f[0:1, :], rowt,
                                                         start=True, stop=True))
                p.op("tensor", lambda e, b3=b3: e.matmul(b3[:, :], ones_f[0:1, :], rowm,
                                                         start=True, stop=True), inc=(sPE, 1))
                p.wait("scalar", sPE)
                p.op("scalar", lambda e, b2=b2: e.copy(out=ab[:, 0:512], in_=b2[:, :]))
                p.op("scalar", lambda e, b3=b3: e.copy(out=ab[:, 512:1024], in_=b3[:, :]), inc=(sACT, 1))
                banks.release(2, sACT)
                banks.release(3, sACT)
                p.wait("vector", sACT)
                for c in range(NCH):
                    p.op("vector", lambda e, c=c, t0=t0: e.tensor_mul(
                        out=scr[:, 0:512], in0=x[:, cs(c, t0, 512)], in1=ab[:, 0:512]))
                    p.op("vector", lambda e, c=c, t0=t0: e.tensor_add(
                        out=xn[:, cs(c, t0, 512)], in0=scr[:, 0:512], in1=ab[:, 512:1024]),
                        inc=(sDVE, 1) if c == NCH - 1 else None)

        p.wait("vector", sX, x_loaded)
        p.wait("scalar", sX, x_loaded)
        p.wait("tensor", sPOOL, consts_done)
        emit_ln()
        xn_done = sDVE.n

        p.wait("tensor", sW, weights_loaded)
        p.wait("tensor", sDVE, xn_done)
        for j in range(NCH):
            for tt in range(NTT):
                b = 4 + ((j * NTT + tt) % 2)
                pbx = banks.acquire(b)
                for half in range(2):
                    h = 2 * j + half
                    for c in range(NCH):
                        p.op("tensor", lambda e, pbx=pbx, half=half, h=h, c=c, tt=tt: e.matmul(
                            pbx[half * 64:(half + 1) * 64, :], proj_tile(c, h),
                            xn[:, cs(c, tt * 512, 512)],
                            start=(c == 0), stop=(c == NCH - 1)),
                            inc=(sPE, 1) if (half == 1 and c == NCH - 1) else None)
                p.wait("scalar", sPE)
                p.op("scalar", lambda e, pbx=pbx, j=j, tt=tt: e.copy(
                    out=pfm[:, cs(j, tt * 512, 512)], in_=pbx[:, :]), inc=(sACT, 1))
                banks.release(b, sACT)

        g_mm_done = {}
        for tc in range(NTC):
            pt = tc % 2
            pb6 = banks.acquire(6)
            pb7 = banks.acquire(7)
            for c in range(NCH):
                p.op("tensor", lambda e, pb6=pb6, c=c, tc=tc: e.matmul(
                    pb6[:, :], xn[:, cs(c, tc * 128, 128)], wbuf[:, c * 1024: c * 1024 + 512],
                    start=(c == 0), stop=(c == NCH - 1)))
                p.op("tensor", lambda e, pb7=pb7, c=c, tc=tc: e.matmul(
                    pb7[:, :], xn[:, cs(c, tc * 128, 128)], wbuf[:, c * 1024 + 512: c * 1024 + 1024],
                    start=(c == 0), stop=(c == NCH - 1)),
                    inc=(sPE, 1) if c == NCH - 1 else None)
            p.wait("scalar", sPE)
            if tc >= 2:
                p.wait("scalar", sPE, g_mm_done[tc - 2])
            p.op("scalar", lambda e, pb6=pb6, pt=pt: e.copy(
                out=ptokb[:, pt * 1024: pt * 1024 + 512], in_=pb6[:, :]))
            p.op("scalar", lambda e, pb7=pb7, pt=pt: e.copy(
                out=ptokb[:, pt * 1024 + 512: pt * 1024 + 1024], in_=pb7[:, :]), inc=(sACT, 1))
            banks.release(6, sACT)
            banks.release(7, sACT)
            p.wait("tensor", sACT)
            b0 = banks.acquire(0)
            b1 = banks.acquire(1)
            for h in range(NH):
                pbx = b0 if h < 8 else b1
                p.op("tensor", lambda e, pbx=pbx, h=h, pt=pt: e.matmul(
                    pbx[0:64, (h % 8) * 64:(h % 8) * 64 + 64],
                    ptokb[:, pt * 1024 + h * 64: pt * 1024 + h * 64 + 64],
                    ptokb[:, pt * 1024 + h * 64: pt * 1024 + h * 64 + 64],
                    start=True, stop=True), inc=(sPE, 1) if h == NH - 1 else None)
            g_mm_done[tc] = sPE.n
            p.wait("vector", sPE)
            p.wait("vector", sPOOL, consts_done)
            p.op("vector", lambda e, b0=b0: e.tensor_add(out=gacc[0:64, 0:512],
                                                         in0=gacc[0:64, 0:512], in1=b0[0:64, :]))
            p.op("vector", lambda e, b1=b1: e.tensor_add(out=gacc[0:64, 512:1024],
                                                         in0=gacc[0:64, 512:1024], in1=b1[0:64, :]),
                 inc=(sDVE, 1))
            banks.release(0, sDVE)
            banks.release(1, sDVE)
        ptok_mm_done = sPE.n
        p.op("vector", lambda e: e.tensor_copy(out=gbf[:], in_=gacc[:]), inc=(sDVE, 1))
        g_done = sDVE.n

        p.wait("tensor", sDVE, g_done)
        b2 = banks.acquire(2)
        b3 = banks.acquire(3)
        for h in range(NH):
            pbx = b2 if h < 8 else b3
            p.op("tensor", lambda e, pbx=pbx, h=h: e.matmul(
                pbx[0:64, (h % 8) * 64:(h % 8) * 64 + 64],
                metric_sb[:, h * 64: h * 64 + 64], gbf[:, h * 64: h * 64 + 64],
                start=True, stop=True), inc=(sPE, 1) if h == NH - 1 else None)
        p.wait("scalar", sPE)
        for h in range(NH):
            pbx = b2 if h < 8 else b3
            p.op("scalar", lambda e, pbx=pbx, h=h: e.copy(
                out=mg[(h % 2) * 64:(h % 2) * 64 + 64, (h // 2) * 64:(h // 2) * 64 + 64],
                in_=pbx[0:64, (h % 8) * 64:(h % 8) * 64 + 64]),
                inc=(sACT, 1) if h == NH - 1 else None)
        banks.release(2, sACT)
        banks.release(3, sACT)
        mg_done = sACT.n

        p.wait("tensor", sACT, mg_done)
        for h in range(NH):
            base = (h % 2) * 64
            j = h // 2
            for tt in range(NTT):
                b = 2 + ((h * NTT + tt) % 2)
                pbx = banks.acquire(b)
                p.op("tensor", lambda e, pbx=pbx, j=j, tt=tt, base=base: e.matmul(
                    pbx[0:64, :], mg[base:base + 64, j * 64: j * 64 + 64],
                    pfm[base:base + 64, cs(j, tt * 512, 512)],
                    start=True, stop=True), inc=(sPE, 1))
                p.wait("scalar", sPE)
                p.op("scalar", lambda e, pbx=pbx, j=j, tt=tt, base=base: e.copy(
                    out=pfm[base:base + 64, cs(j, tt * 512, 512)], in_=pbx[0:64, :]),
                    inc=(sACT, 1))
                banks.release(b, sACT)
        nudged_done = sACT.n

        p.wait("tensor", sDVE, xn_done)
        for tc in range(NTC):
            b = 4 + (tc % 2)
            pbx = banks.acquire(b)
            for c in range(NCH):
                p.op("tensor", lambda e, pbx=pbx, c=c, tc=tc: e.matmul(
                    pbx[:, 0:8], xn[:, cs(c, tc * 128, 128)], gatew_bf[:, c * 8: c * 8 + 8],
                    start=(c == 0), stop=False))
            p.op("tensor", lambda e, pbx=pbx: e.matmul(
                pbx[:, 0:8], ones_bf[0:1, :], gb_bf[:], start=False, stop=True), inc=(sPE, 1))
            p.wait("vector", sPE)
            p.op("vector", lambda e, pbx=pbx: e.reduce_max(out=col1[:], in_=pbx[:, 0:8], axis=AX,
                                                           negate=True), inc=(sDVE, 1))
            p.wait("scalar", sDVE)
            p.op("scalar", lambda e, pbx=pbx: e.activation(out=gtok[:], in_=pbx[:, 0:8],
                                                           func=AF.Exp, bias=col1[:],
                                                           accum_out=col2[:]), inc=(sACT, 1))
            banks.release(b, sACT)
            p.wait("vector", sACT)
            p.op("vector", lambda e: e.reciprocal(out=col2[:], in_=col2[:]), inc=(sDVE, 1))
            p.wait("vector", sDVE)
            p.op("vector", lambda e: e.tensor_scalar(out=gtok[:], in0=gtok[:], scalar1=col2[:],
                                                     scalar2=None, op0=ALU.mult), inc=(sDVE, 1))
            p.wait("vector", sDVE)
            p.op("vector", lambda e: e.reduce_max(out=col1[:], in_=gtok[:], axis=AX),
                 inc=(sDVE, 1))
            p.wait("vector", sDVE)
            p.op("vector", lambda e: e.tensor_scalar(out=gtok2[:], in0=gtok[:], scalar1=col1[:],
                                                     scalar2=None, op0=ALU.is_equal), inc=(sDVE, 1))
            p.wait("vector", sDVE)
            p.op("vector", lambda e: e.scalar_tensor_tensor(out=gtok2[:], in0=gtok2[:], scalar=-2.0,
                                                            in1=gtok[:], op0=ALU.mult, op1=ALU.add),
                 inc=(sDVE, 1))
            p.wait("vector", sDVE)
            p.op("vector", lambda e: e.reduce_max(out=col3[:], in_=gtok2[:], axis=AX),
                 inc=(sDVE, 1))
            p.wait("vector", sDVE)
            p.op("vector", lambda e: e.tensor_scalar(out=gtok2[:], in0=gtok[:], scalar1=col3[:],
                                                     scalar2=None, op0=ALU.is_ge), inc=(sDVE, 1))
            p.wait("vector", sDVE)
            p.op("vector", lambda e: e.tensor_mul(out=gtok[:], in0=gtok[:], in1=gtok2[:]),
                 inc=(sDVE, 1))
            p.wait("vector", sDVE)
            p.op("vector", lambda e: e.tensor_copy(out=gtok_bf[:], in_=gtok[:]), inc=(sDVE, 1))
            p.wait("tensor", sDVE)
            bT = 6 + (tc % 2)
            pbt = banks.acquire(bT)
            p.op("tensor", lambda e, pbt=pbt: e.matmul(pbt[0:8, 0:128], gtok_bf[:], ident_bf[:],
                                                       start=True, stop=True), inc=(sPE, 1))
            p.wait("scalar", sPE)
            p.op("scalar", lambda e, pbt=pbt, tc=tc: e.copy(out=g_sb[:, tc * 128:(tc + 1) * 128],
                                                            in_=pbt[0:8, 0:128]), inc=(sACT, 1))
            banks.release(bT, sACT)
        gates_done = sACT.n

        p.wait("tensor", sACT, nudged_done)
        p.wait("vector", sX, x_loaded)
        for oc in range(NCH):
            for tt in range(NTT):
                b = (oc * NTT + tt) % 4
                pbx = banks.acquire(b)
                for c in range(NCH):
                    p.op("tensor", lambda e, pbx=pbx, c=c, oc=oc, tt=tt: e.matmul(
                        pbx[:, :], mix_tile(c, oc), pfm[:, cs(c, tt * 512, 512)],
                        start=(c == 0), stop=(c == NCH - 1)),
                        inc=(sPE, 1) if c == NCH - 1 else None)
                p.wait("vector", sPE)
                p.op("vector", lambda e, pbx=pbx, oc=oc, tt=tt: e.scalar_tensor_tensor(
                    out=x[:, cs(oc, tt * 512, 512)], in0=pbx[:, :], scalar=mixb_sb[:, oc:oc + 1],
                    in1=x[:, cs(oc, tt * 512, 512)], op0=ALU.add, op1=ALU.add), inc=(sDVE, 1))
                banks.release(b, sDVE)
        pe_after_mixer = sPE.n

        expert_bufs = [wbuf[:, 0:8192], wbuf[:, 8192:16384]]

        def ebuf_tile(ebuf, c, oc):
            return ebuf[:, c * D + oc * 128: c * D + oc * 128 + 128]

        w_loaded = {}
        p.wait("gpsimd", sPE, ptok_mm_done)
        p.op("gpsimd", lambda e: e.dma_start(
            out=expert_bufs[0].rearrange("p (c o) -> p c o", c=NCH),
            in_=expert_wT[0].rearrange("(c p) o -> p c o", p=128)), inc=(sWE[0], 16))
        w_loaded[0] = (0, sWE[0].n)
        p.wait("gpsimd", sPE, pe_after_mixer)
        p.op("gpsimd", lambda e: e.dma_start(
            out=expert_bufs[1].rearrange("p (c o) -> p c o", c=NCH),
            in_=expert_wT[1].rearrange("(c p) o -> p c o", p=128)), inc=(sWE[1], 16))
        w_loaded[1] = (1, sWE[1].n)
        p.wait("sync", sACT, gates_done)
        grow_loaded = {}
        p.op("sync", lambda e: e.dma_start(out=g_row[:], in_=g_sb[0:1, :]), inc=(sX, 16))
        grow_loaded[0] = sX.n

        p.wait("vector", sDVE)
        xne2[1] = pfm[:, 0:4096]
        last_xne_reader = {0: 0, 1: pe_after_mixer}
        last_gbc_reader = {0: 0, 1: 0}
        last_grow_reader = {0: pe_after_mixer}
        for e_i in range(8):
            ebuf = expert_bufs[e_i % 2]
            p.wait("tensor", sWE[w_loaded[e_i][0]], w_loaded[e_i][1])
            p.wait("tensor", sX, grow_loaded[e_i])
            for tt in range(NTT):
                par = tt % 2
                bg = par
                pbg = banks.acquire(bg)
                p.op("tensor", lambda e, pbg=pbg, e_i=e_i, tt=tt: e.matmul(
                    pbg[:, :], ones_bf[0:1, :], g_row[0:1, tt * 512:tt * 512 + 512],
                    start=True, stop=True), inc=(sPE, 1))
                last_grow_reader[0] = sPE.n
                p.wait("scalar", sPE)
                p.wait("scalar", sDVE, last_gbc_reader[par])
                p.op("scalar", lambda e, pbg=pbg, par=par: e.copy(
                    out=gbc[:, par * 512:par * 512 + 512], in_=pbg[:, :]), inc=(sACT, 1))
                banks.release(bg, sACT)
                xneb = xne2[par]
                p.wait("vector", sACT)
                p.wait("vector", sPE, last_xne_reader[par])
                for c in range(NCH):
                    p.op("vector", lambda e, c=c, tt=tt, par=par, xneb=xneb: e.tensor_mul(
                        out=xneb[:, c * 512: c * 512 + 512],
                        in0=xn[:, cs(c, tt * 512, 512)],
                        in1=gbc[:, par * 512:par * 512 + 512]),
                        inc=(sDVE, 1) if c == NCH - 1 else None)
                last_gbc_reader[par] = sDVE.n
                p.wait("tensor", sDVE)
                for oc in range(NCH):
                    pbx = banks.acquire(oc)
                    for c in range(NCH):
                        p.op("tensor", lambda e, pbx=pbx, ebuf=ebuf, c=c, oc=oc, e_i=e_i,
                             xneb=xneb: e.matmul(
                            pbx[:, :], ebuf_tile(ebuf, c, oc),
                            xneb[:, c * 512: c * 512 + 512],
                            start=(c == 0), stop=(c == NCH - 1 and e_i != 0)),
                            inc=(sPE, 1) if (c == NCH - 1 and e_i != 0) else None)
                    if e_i == 0:
                        p.op("tensor", lambda e, pbx=pbx, oc=oc, tt=tt: e.matmul(
                            pbx[:, :], eb_bf[:, oc * 128:oc * 128 + 128],
                            g_sb[:, tt * 512:tt * 512 + 512],
                            start=False, stop=True), inc=(sPE, 1))
                    p.wait("vector", sPE)
                    p.op("vector", lambda e, pbx=pbx, oc=oc, tt=tt: e.tensor_add(
                        out=x[:, cs(oc, tt * 512, 512)], in0=x[:, cs(oc, tt * 512, 512)],
                        in1=pbx[:, :]), inc=(sDVE, 1))
                    banks.release(oc, sDVE)
                last_xne_reader[par] = sPE.n
            if e_i + 2 < 8:
                p.wait("gpsimd", sPE, sPE.n)
                p.op("gpsimd", lambda e, e_i=e_i: e.dma_start(
                    out=expert_bufs[e_i % 2].rearrange("p (c o) -> p c o", c=NCH),
                    in_=expert_wT[e_i + 2].rearrange("(c p) o -> p c o", p=128)),
                    inc=(sWE[e_i % 2], 16))
                w_loaded[e_i + 2] = (e_i % 2, sWE[e_i % 2].n)
            if e_i + 1 < 8:
                p.wait("sync", sPE, last_grow_reader[0])
                p.op("sync", lambda e, e_i=e_i: e.dma_start(
                    out=g_row[:], in_=g_sb[e_i + 1:e_i + 2, :]), inc=(sX, 16))
                grow_loaded[e_i + 1] = sX.n
        x1_done = sDVE.n

        if "x1" in dbg:
            p.wait("gpsimd", sDVE, x1_done)
            p.op("gpsimd", lambda e: e.dma_start(
                out=dbg["x1"].rearrange("(c p) t -> p c t", p=128),
                in_=x.rearrange("p (c t) -> p c t", c=NCH)), inc=(sW, 16))

        p.wait("vector", sPE)
        p.wait("vector", sDVE, x1_done)
        p.wait("scalar", sPE)
        p.wait("tensor", sDVE, x1_done)
        emit_ln()
        ln1_done = sDVE.n

        p.wait("sync", sDVE, ln1_done)
        p.op("sync", lambda e: e.dma_start(
            out=outT.rearrange("(c p) t -> p c t", p=128),
            in_=x.rearrange("p (c t) -> p c t", c=NCH)), inc=(sOI, 16))
        out_init = sOI.n

        p.wait("gpsimd", sOI, out_init)
        p.op("gpsimd", lambda e: e.iota(xstg[0:64, 0:2048], pattern=[[1, 2048]], base=0,
                                        channel_multiplier=0,
                                        allow_small_or_imprecise_dtypes=True), inc=(sPOOL, 1))
        p.wait("scalar", sPOOL)
        p.op("scalar", lambda e: e.activation(out=invf[:], in_=invf[:], func=AF.Exp,
                                              scale=-math.log(10000.0) / 64.0), inc=(sACT, 1))
        p.wait("vector", sACT)
        p.wait("vector", sPOOL)
        tmpA = xstg[0:64, 0:2048]
        tmpB = xstg[0:64, 2048:4096]
        p.op("vector", lambda e: e.tensor_scalar(out=tmpA, in0=tmpA, scalar1=invf[0:64, 0:1],
                                                 scalar2=None, op0=ALU.mult), inc=(sDVE, 1))
        p.wait("vector", sDVE)
        p.op("vector", lambda e: e.tensor_scalar(out=tmpB, in0=tmpA, scalar1=1.0 / (2 * math.pi),
                                                 scalar2=MAGIC, op0=ALU.mult, op1=ALU.add))
        p.op("vector", lambda e: e.tensor_scalar(out=tmpB, in0=tmpB, scalar1=MAGIC,
                                                 scalar2=None, op0=ALU.subtract))
        p.op("vector", lambda e: e.scalar_tensor_tensor(out=tmpA, in0=tmpB, scalar=-2 * math.pi,
                                                        in1=tmpA, op0=ALU.mult, op1=ALU.add),
             inc=(sDVE, 1))
        p.wait("scalar", sDVE)
        p.op("scalar", lambda e: e.activation(out=tmpB, in_=tmpA, func=AF.Sin, scale=0.5))
        p.op("scalar", lambda e: e.activation(out=sinT[0:64, :], in_=tmpA, func=AF.Sin,
                                              scale=-0.5, bias=hpi[:, 0:1]))
        p.op("scalar", lambda e: e.activation(out=cosT[0:64, :], in_=tmpB, func=AF.Square),
             inc=(sACT, 1))
        p.wait("vector", sACT)
        p.op("vector", lambda e: e.scalar_tensor_tensor(out=sinT[0:64, :], in0=tmpB, scalar=2.0,
                                                        in1=sinT[0:64, :], op0=ALU.mult,
                                                        op1=ALU.mult))
        p.op("vector", lambda e: e.tensor_scalar(out=cosT[0:64, :], in0=cosT[0:64, :],
                                                 scalar1=-2.0, scalar2=1.0,
                                                 op0=ALU.mult, op1=ALU.add), inc=(sDVE, 1))
        p.wait("sync", sDVE)
        p.op("sync", lambda e: e.dma_start(out=cosT[64:128, :], in_=cosT[0:64, :]), inc=(sX, 16))
        p.op("sync", lambda e: e.dma_start(out=sinT[64:128, :], in_=sinT[0:64, :]), inc=(sX, 16))
        p.wait("vector", sX)
        p.op("vector", lambda e: e.tensor_scalar(out=sinT[0:64, :], in0=sinT[0:64, :],
                                                 scalar1=-1.0, scalar2=None, op0=ALU.mult),
             inc=(sDVE, 1))
        rope_done = sDVE.n

        p.wait("gpsimd", sOI, out_init)
        p.wait("gpsimd", sWE[0])
        for name, dst, c in [("wq", pwq, NCH), ("wkv", pwkv, NCH), ("wqc", pwqc, 2),
                             ("wkc", pwkc, 2), ("wqr", wqr_sb, NCH), ("wkr", wkr_sb, NCH),
                             ("wv", wv_sb, 2)]:
            p.op("gpsimd", lambda e, name=name, dst=dst, c=c: e.dma_start(
                out=dst.rearrange("p (c o) -> p c o", c=c),
                in_=W[name].rearrange("(c p) o -> p c o", p=128)), inc=(sW, 16))
        attw_loaded = sW.n
        for name in ("bq", "bqc", "bqr", "bkv", "bkr", "bkc", "bv", "bo", "convw", "convb"):
            p.op("sync", lambda e, name=name: e.dma_start(out=bias_sb[name][:], in_=W[name]),
                 inc=(sB, 16))
        bias_loaded = sB.n

        p.wait("tensor", sW, attw_loaded)
        p.wait("tensor", sDVE, ln1_done)
        p.wait("scalar", sB, bias_loaded)
        p.wait("scalar", sDVE, rope_done)

        def proj_block(dst_fn, w_tile_fn, nin_ch, bias_name, noc, func, src_fn):
            for oc in range(noc):
                for tt in range(NTT):
                    b = (oc * NTT + tt) % 4
                    pbx = banks.acquire(b)
                    for c in range(nin_ch):
                        p.op("tensor", lambda e, pbx=pbx, c=c, oc=oc, tt=tt: e.matmul(
                            pbx[:, :], w_tile_fn(c, oc), src_fn(c, tt),
                            start=(c == 0), stop=(c == nin_ch - 1)),
                            inc=(sPE, 1) if c == nin_ch - 1 else None)
                    p.wait("scalar", sPE)
                    p.op("scalar", lambda e, pbx=pbx, oc=oc, tt=tt: e.activation(
                        out=dst_fn(oc, tt), in_=pbx[:, :], func=func,
                        bias=bias_sb[bias_name][:, oc:oc + 1]), inc=(sACT, 1))
                    banks.release(b, sACT)

        def xn_src(c, tt):
            return xn[:, cs(c, tt * 512, 512)]

        def sl(buf, oc, tt):
            return buf[:, oc * 2048 + tt * 512: oc * 2048 + tt * 512 + 512]

        proj_block(lambda oc, tt: sl(cq, oc, tt),
                   lambda c, oc: pwq[:, c * 256 + oc * 128: c * 256 + oc * 128 + 128],
                   NCH, "bq", 2, AF.Gelu, xn_src)
        proj_block(lambda oc, tt: sl(ckv, oc, tt),
                   lambda c, oc: pwkv[:, c * 256 + oc * 128: c * 256 + oc * 128 + 128],
                   NCH, "bkv", 2, AF.Gelu, xn_src)
        cqkv_done = sACT.n

        p.wait("tensor", sACT, cqkv_done)
        proj_block(lambda oc, tt: sl(Q, oc, tt),
                   lambda c, oc: pwqc[:, c * 896 + oc * 128: c * 896 + oc * 128 + 128],
                   2, "bqc", 7, AF.Gelu, lambda c, tt: sl(cq, c, tt))
        proj_block(lambda oc, tt: sl(Kb, oc, tt),
                   lambda c, oc: pwkc[:, c * 896 + oc * 128: c * 896 + oc * 128 + 128],
                   2, "bkc", 7, AF.Gelu, lambda c, tt: sl(ckv, c, tt))

        Vv = Vb.rearrange("p (t h o) -> p t h o", t=NTC, h=NH)
        p.op("gpsimd", lambda e: e.memset(Vv[:, :, :, 64:65], 1.0), inc=(sPOOL, 1))
        for tc in range(NTC):
            for n2 in range(2):
                b = 4 + (tc * 2 + n2) % 2
                pbx = banks.acquire(b)
                for c in range(2):
                    p.op("tensor", lambda e, pbx=pbx, c=c, tc=tc, n2=n2: e.matmul(
                        pbx[:, :], ckv[:, c * 2048 + tc * 128: c * 2048 + tc * 128 + 128],
                        wv_sb[:, c * 1024 + n2 * 512: c * 1024 + n2 * 512 + 512],
                        start=(c == 0), stop=(c == 1)), inc=(sPE, 1) if c == 1 else None)
                p.wait("scalar", sPE)
                p.op("scalar", lambda e, pbx=pbx, tc=tc, n2=n2: e.activation(
                    out=Vv[:, tc, n2 * 8:(n2 + 1) * 8, 0:64],
                    in_=pbx[:, :].rearrange("p (h o) -> p h o", h=8), func=AF.Copy),
                    inc=(sACT, 1))
                banks.release(b, sACT)
        v_done = sACT.n
        qkv_mm_done = sPE.n

        p.wait("vector", sDVE, rope_done)
        for which, w_sb, bias_name, dstc in (("q", wqr_sb, "bqr", Q), ("k", wkr_sb, "bkr", Kb)):
            for tt in range(NTT):
                b = (tt % 2)
                pbx = banks.acquire(b)
                for c in range(NCH):
                    p.op("tensor", lambda e, pbx=pbx, c=c, tt=tt, w_sb=w_sb: e.matmul(
                        pbx[:, :], w_sb[:, c * 128: c * 128 + 128], xn_src(c, tt),
                        start=(c == 0), stop=(c == NCH - 1)),
                        inc=(sPE, 1) if c == NCH - 1 else None)
                if which == "q" and tt == 0:
                    p.wait("scalar", sPE, qkv_mm_done)
                p.wait("scalar", sPE)
                p.op("scalar", lambda e, pbx=pbx, tt=tt, bias_name=bias_name: e.activation(
                    out=qrS[:, tt * 512:(tt + 1) * 512], in_=pbx[:, :], func=AF.Identity,
                    bias=bias_sb[bias_name][:, 0:1]), inc=(sACT, 1))
                banks.release(b, sACT)
            p.wait("sync", sACT)
            p.op("sync", lambda e: e.dma_start(out=rotS[0:64, :], in_=qrS[64:128, :]),
                 inc=(sX, 16))
            p.op("sync", lambda e: e.dma_start(out=rotS[64:128, :], in_=qrS[0:64, :]),
                 inc=(sX, 16))
            p.wait("vector", sX)
            p.wait("vector", sACT)
            p.op("vector", lambda e: e.tensor_mul(out=qrS[:], in0=qrS[:], in1=cosT[:]))
            p.op("vector", lambda e: e.tensor_mul(out=rotS[:], in0=rotS[:], in1=sinT[:]))
            p.op("vector", lambda e: e.tensor_add(out=qrS[:], in0=qrS[:], in1=rotS[:]),
                 inc=(sDVE, 1))
            p.wait("scalar", sDVE)
            for tt in range(NTT):
                p.op("scalar", lambda e, tt=tt, dstc=dstc: e.activation(
                    out=dstc[:, 7 * 2048 + tt * 512: 7 * 2048 + tt * 512 + 512],
                    in_=qrS[:, tt * 512:(tt + 1) * 512], func=AF.Gelu),
                    inc=(sACT, 1) if tt == NTT - 1 else None)
        qk_done = sACT.n

        p.wait("vector", sACT, qk_done)
        for i, buf in enumerate((Q, Kb)):
            for c in range(NCH):
                p.op("vector", lambda e, c=c, buf=buf: e.tensor_reduce(
                    out=qmaxc[:, c:c + 1], in_=buf[:, c * 2048:(c + 1) * 2048], axis=AX,
                    op=ALU.max, apply_absolute_value=True),
                    inc=(sDVE, 1) if c == NCH - 1 else None)
            p.wait("vector", sDVE)
            p.op("vector", lambda e: e.tensor_reduce(out=qmaxc[:, 8:9], in_=qmaxc[:, 0:8],
                                                     axis=AX, op=ALU.max), inc=(sDVE, 1))
            p.wait("vector", sDVE)
            p.op("vector", lambda e, i=i: e.tensor_copy(out=qkmax[:, i:i + 1],
                                                        in_=qmaxc[:, 8:9]), inc=(sDVE, 1))
            p.wait("vector", sDVE)
        p.wait("tensor", sDVE)
        b0 = banks.acquire(0)
        p.op("tensor", lambda e, b0=b0: e.matmul(b0[0:1, 0:128], qkmax[:, 0:1], ident_bf[:],
                                                 start=True, stop=True))
        p.op("tensor", lambda e, b0=b0: e.matmul(b0[0:1, 128:256], qkmax[:, 1:2], ident_bf[:],
                                                 start=True, stop=True), inc=(sPE, 1))
        p.wait("vector", sPE)
        p.op("vector", lambda e, b0=b0: e.reduce_max(out=brow[:, 0:1], in_=b0[0:1, 0:128],
                                                     axis=AX))
        p.op("vector", lambda e, b0=b0: e.reduce_max(out=brow[:, 1:2], in_=b0[0:1, 128:256],
                                                     axis=AX), inc=(sDVE, 1))
        banks.release(0, sDVE)
        p.wait("vector", sDVE)
        p.op("vector", lambda e: e.tensor_mul(out=brow[:, 2:3], in0=brow[:, 0:1],
                                              in1=brow[:, 1:2]), inc=(sDVE, 1))
        p.wait("vector", sDVE)
        p.op("vector", lambda e: e.tensor_scalar(out=brow[:, 2:3], in0=brow[:, 2:3],
                                                 scalar1=-8.0, scalar2=None, op0=ALU.mult),
             inc=(sDVE, 1))
        p.wait("tensor", sDVE)
        b0 = banks.acquire(0)
        p.op("tensor", lambda e, b0=b0: e.matmul(b0[:, 0:1], ones_f[0:1, :], brow[:, 2:3],
                                                 start=True, stop=True), inc=(sPE, 1))
        p.wait("scalar", sPE)
        p.op("scalar", lambda e, b0=b0: e.copy(out=negb[:], in_=b0[:, 0:1]), inc=(sACT, 1))
        banks.release(0, sACT)
        bound_done = sACT.n

        _wdone = {}
        _wgen = gen_wmean(p, banks, locals(), _wdone)

        p.wait("tensor", sACT, bound_done)
        p.wait("tensor", sACT, v_done)
        p.wait("tensor", sPOOL)
        ptb_reader = [0, 0, 0, 0]
        scrA_reader = 0
        ep_stage = []
        sPV = p.sem("sPV")

        def make_stages(pbO0_, bO0_, pbO1_, bO1_, jc_, qg_, pv_done_):
            col0 = jc_ * 2048 + qg_ * 512
            st = {}

            def s1():
                p.wait("vector", sPV, pv_done_)
                p.op("vector", lambda e: e.reciprocal(out=rrowA[:], in_=pbO0_[64:65, :]),
                     inc=(sDVE, 1))
                p.op("vector", lambda e: e.reciprocal(out=rrowB[:], in_=pbO1_[64:65, :]),
                     inc=(sDVE, 1))
                st["recip"] = sDVE.n

            def s2():
                p.wait("tensor", sDVE, st["recip"])
                p.op("tensor", lambda e: e.matmul(pbO0_[64:128, :], ones_f[0:1, 0:64],
                                                  rrowA[:], start=True, stop=True),
                     inc=(sPE, 1))
                p.op("tensor", lambda e: e.matmul(pbO1_[64:128, :], ones_f[0:1, 0:64],
                                                  rrowB[:], start=True, stop=True),
                     inc=(sPE, 1))
                st["bcast"] = sPE.n

            def s3():
                p.wait("scalar", sPE, st["bcast"])
                p.op("scalar", lambda e: e.copy(out=rstgA[:], in_=pbO0_[64:128, :]),
                     inc=(sACT, 1))
                p.op("scalar", lambda e: e.copy(out=rstgB[:], in_=pbO1_[64:128, :]),
                     inc=(sACT, 1))
                st["evac"] = sACT.n

            def s4():
                p.wait("vector", sACT, st["evac"])
                p.wait("vector", sACT, scrA_reader)
                p.op("vector", lambda e: e.tensor_mul(out=scrA_A[:], in0=pbO0_[0:64, :],
                                                      in1=rstgA[:]), inc=(sDVE, 1))
                p.op("vector", lambda e: e.tensor_mul(out=scrA_B[:], in0=pbO1_[0:64, :],
                                                      in1=rstgB[:]), inc=(sDVE, 1))
                banks.release(bO0_, sDVE)
                banks.release(bO1_, sDVE)
                st["mul"] = sDVE.n

            def s5():
                nonlocal scrA_reader
                p.wait("scalar", sDVE, st["mul"])
                p.op("scalar", lambda e: e.activation(
                    out=attn_out[0:64, col0: col0 + 512], in_=scrA_A[:], func=AF.Copy),
                    inc=(sACT, 1))
                p.op("scalar", lambda e: e.activation(
                    out=attn_out[64:128, col0: col0 + 512], in_=scrA_B[:], func=AF.Copy),
                    inc=(sACT, 1))
                scrA_reader = sACT.n

            return [s1, s2, s3, s4, s5]

        for jc in range(NCH):
            for qg in range(NTT):
                nkt = 4 * qg + 4
                bO0 = 4 + 2 * ((jc * NTT + qg) % 2)
                bO1 = bO0 + 1
                pbO0 = banks.acquire(bO0)
                pbO1 = banks.acquire(bO1)

                def emit_S(kt, jc=jc, qg=qg):
                    bp = kt % 2
                    banks.acquire(2 * bp)
                    banks.acquire(2 * bp + 1)
                    for half in range(2):
                        p.op("tensor", lambda e, bp=bp, half=half, kt=kt: e.matmul(
                            pbD[bp][:, half * 512:(half + 1) * 512],
                            Kb[half * 64:half * 64 + 64,
                               jc * 2048 + kt * 128: jc * 2048 + kt * 128 + 128],
                            Q[half * 64:half * 64 + 64,
                              jc * 2048 + qg * 512: jc * 2048 + qg * 512 + 512],
                            start=True, stop=True), inc=(sPE, 1))
                    slot = kt % 4
                    p.wait("scalar", sPE)
                    p.wait("scalar", sPV, ptb_reader[slot])
                    p.op("scalar", lambda e, bp=bp, slot=slot: e.activation(
                        out=ptb[:, slot * 1024:(slot + 1) * 1024], in_=pbD[bp][:, :], func=AF.Exp,
                        scale=0.125, bias=negb[:, 0:1]), inc=(sACT, 1))
                    banks.release(2 * bp, sACT)
                    banks.release(2 * bp + 1, sACT)
                    ktrel = kt - 4 * qg
                    if ktrel >= 0:
                        p.wait("vector", sACT)
                        for half in range(2):
                            c0 = slot * 1024 + half * 512
                            if ktrel > 0:
                                p.op("vector", lambda e, c0=c0, ktrel=ktrel: e.memset(
                                    ptb[:, c0: c0 + ktrel * 128], 0.0))
                            p.op("vector", lambda e, c0=c0, ktrel=ktrel: e.tensor_mul(
                                out=ptb[:, c0 + ktrel * 128: c0 + (ktrel + 1) * 128],
                                in0=ptb[:, c0 + ktrel * 128: c0 + (ktrel + 1) * 128],
                                in1=tri_bf[:]), inc=(sDVE, 1))
                        return ("dve", sDVE.n)
                    return ("act", sACT.n)

                def emit_PV(kt, ready, jc=jc, nkt=nkt, pbO0=pbO0, pbO1=pbO1):
                    slot = kt % 4
                    if ready[0] == "dve":
                        p.wait("tensor", sDVE, ready[1])
                    else:
                        p.wait("tensor", sACT, ready[1])
                    for half, pbO_ in ((0, pbO0), (1, pbO1)):
                        p.op("tensor", lambda e, kt=kt, slot=slot, half=half, pbO_=pbO_: e.matmul(
                            pbO_[0:65, :], Vv[:, kt, 2 * jc + half, :],
                            ptb[:, slot * 1024 + half * 512: slot * 1024 + half * 512 + 512],
                            start=(kt == 0), stop=(kt == nkt - 1)), inc=(sPV, 1))
                    ptb_reader[slot] = sPV.n

                ready = emit_S(0)
                for kt in range(1, nkt):
                    ready_next = emit_S(kt)
                    emit_PV(kt - 1, ready)
                    ready = ready_next
                    if ep_stage:
                        ep_stage.pop(0)()
                emit_PV(nkt - 1, ready)
                while ep_stage:
                    ep_stage.pop(0)()
                pv_done = sPV.n
                next(_wgen, None)
                next(_wgen, None)
                ep_stage.extend(make_stages(pbO0, bO0, pbO1, bO1, jc, qg, pv_done))
        while ep_stage:
            ep_stage.pop(0)()
        for _ in _wgen:
            pass
        wmean_done = _wdone["act"]
        scores_done = sACT.n
        pe_attn_end = sPE.n

        sWO = [p.sem("sWO0"), p.sem("sWO1")]
        p.wait("vector", sB, bias_loaded)
        p.op("vector", lambda e: e.tensor_copy(out=bv_bf[:], in_=bias_sb["bv"][:]),
             inc=(sDVE, 1))
        wo_reader = {0: ptok_mm_done, 1: ptok_mm_done}
        wo_loaded = {}
        for oc in (0, 1):
            p.wait("gpsimd", sPE, wo_reader[oc])
            p.op("gpsimd", lambda e, oc=oc: e.dma_start(
                out=woS[oc % 2].rearrange("p (c o) -> p c o", c=NCH),
                in_=W["wo"].rearrange("(c p) o -> p c o", p=128)[:, :, oc * 128:(oc + 1) * 128]),
                inc=(sWO[oc % 2], 16))
            wo_loaded[oc] = sWO[oc % 2].n

        p.wait("tensor", sACT, scores_done)
        p.wait("tensor", sDVE)
        p.wait("scalar", sDVE)
        xstg_f = xstg[:]
        astg_reader = {0: 0, 1: 0}
        for oc in range(NCH):
            par = oc % 2
            p.wait("tensor", sWO[par], wo_loaded[oc])
            bpb = banks.acquire(6 + par)
            for c in range(NCH):
                p.op("tensor", lambda e, bpb=bpb, c=c, par=par: e.matmul(
                    bpb[:, 0:1], woS[par][:, c * 128: c * 128 + 128],
                    bv_bf[:, c:c + 1], start=(c == 0), stop=(c == NCH - 1)),
                    inc=(sPE, 1) if c == NCH - 1 else None)
            p.wait("scalar", sPE)
            p.op("scalar", lambda e, bpb=bpb, oc=oc: e.activation(
                out=bprime[:, oc:oc + 1], in_=bpb[:, 0:1], func=AF.Identity,
                bias=bias_sb["bo"][:, oc:oc + 1]), inc=(sACT, 1))
            banks.release(6 + par, sACT)
            p.wait("scalar", sACT)
            for tt in range(NTT):
                b = 2 + (oc * NTT + tt) % 4
                pbx = banks.acquire(b)
                for c in range(NCH):
                    p.op("tensor", lambda e, pbx=pbx, c=c, tt=tt, par=par: e.matmul(
                        pbx[:, :], woS[par][:, c * 128: c * 128 + 128],
                        attn_out[:, cs(c, tt * 512, 512)],
                        start=(c == 0), stop=(c == NCH - 1)),
                        inc=(sPE, 1) if c == NCH - 1 else None)
                p.wait("scalar", sPE)
                p.wait("scalar", sW, astg_reader[par] and sW.n)
                p.op("scalar", lambda e, pbx=pbx, oc=oc, tt=tt, par=par: e.activation(
                    out=xstg_f[:, par * 2048 + tt * 512: par * 2048 + tt * 512 + 512],
                    in_=pbx[:, :], func=AF.Identity, bias=bprime[:, oc:oc + 1]), inc=(sACT, 1))
                banks.release(b, sACT)
            wo_reader[par] = sPE.n
            if oc + 2 < NCH:
                p.wait("gpsimd", sPE, wo_reader[par])
                p.op("gpsimd", lambda e, oc=oc, par=par: e.dma_start(
                    out=woS[par].rearrange("p (c o) -> p c o", c=NCH),
                    in_=W["wo"].rearrange("(c p) o -> p c o", p=128)[:, :, (oc + 2) * 128:(oc + 3) * 128]),
                    inc=(sWO[par], 16))
                wo_loaded[oc + 2] = sWO[par].n
            p.wait("gpsimd", sACT)
            p.wait("gpsimd", sW)
            p.wait("gpsimd", sOI, out_init)
            p.op("gpsimd", lambda e, oc=oc, par=par: e.dma_start(
                out=outT.rearrange("(c p) t -> p c t", p=128)[:, oc, :],
                in_=xstg_f[:, par * 2048: par * 2048 + 2048],
                accum_op=ALU.add), inc=(sW, 16))
            astg_reader[par] = sW.n
        attn_accum_done = sW.n

        arena2 = Dd
        emit_ffn(p, banks, locals())

        p.wait("gpsimd", sDVE)
        p.wait("gpsimd", sACT)
        if "xn" in dbg:
            p.op("gpsimd", lambda e: e.dma_start(
                out=dbg["xn"].rearrange("(c p) t -> p c t", p=128),
                in_=xn[:].rearrange("p (c t) -> p c t", c=NCH)), inc=(sW, 16))
        p.wait("sync", sW)
        p.wait("sync", sX)
        p.wait("sync", sOI, out_init)

        p.finalize(block)
    return nc


def fused_in_map(g, xb):
    m = l0_in_map(g, xb)
    for k in L1_WEIGHT_KEYS:
        m[k] = g[k]
    return m


L1_WEIGHT_KEYS = ("wq", "bq", "wqc", "bqc", "wqr", "bqr", "wkv", "bkv", "wkr", "bkr",
                  "wkc", "bkc", "wv", "bv", "wo", "bo", "w1T", "convw", "convb", "w2T")


def kernel(**inputs):
    g = host_prepare(inputs)
    x = np.ascontiguousarray(np.asarray(inputs["L0_attn_in_sequence_bwc"], np.float32))
    nc = build_fused()
    in_maps = [fused_in_map(g, x[b]) for b in range(NCORES)]
    res = run_bass_kernel_spmd(nc, in_maps, core_ids=list(range(NCORES)))
    out = np.stack([np.asarray(res.results[b]["outT"], np.float32).T for b in range(NCORES)])
    return np.ascontiguousarray(out)
